# version 12
# baseline (speedup 1.0000x reference)
from contextlib import ExitStack

import numpy as np
import concourse.bass as bass
import concourse.mybir as mybir

F32 = mybir.dt.float32
BF16 = mybir.dt.bfloat16
FP8 = mybir.dt.float8e4
AF = mybir.ActivationFunctionType
ALU = mybir.AluOpType
AX = mybir.AxisListType
AP = bass.AP

D = 1024
H = 16
HD = 64
LH = 4
LHD = 64
LD = 256
NLAT = 64
EPS = 1e-5
FP8MAX = 448.0
RG = [[0, 1, 2, 3, 4, 5, 6, 7]]


class Cfg:
    def __init__(self, s_core=4096, dbg=False):
        self.s_core = s_core
        self.s_seq = 4 * s_core
        self.chunk = self.s_seq // NLAT
        self.ntiles = s_core // 128
        self.ngroups = self.ntiles // 4
        self.ch_per_grp = 512 // self.chunk
        self.dbg = dbg
        assert self.chunk * NLAT == self.s_seq
        assert self.ntiles % 4 == 0


PARANOID = False


class Prog:
    ENGINES = ("sync", "gpsimd", "tensor", "vector", "scalar")

    def __init__(self):
        self.ops = []
        self.cnt = {}
        self.marks = {}
        self.all_sems = {}

    def _bump(self, sem, k):
        self.cnt[sem.name] = self.cnt.get(sem.name, 0) + k
        return self.cnt[sem.name]

    def i(self, eng, fn, sem=None, k=1):
        if PARANOID:
            snap = dict(self.cnt)
            self.ops.append((eng, "psnap", snap))
        self.ops.append((eng, "inst", (fn, sem, k)))
        if sem is not None:
            self.all_sems[sem.name] = sem
            return self._bump(sem, k)

    def w(self, eng, sem, val):
        if val > 0:
            self.ops.append((eng, "wait", (sem, val)))

    def mark(self, name, sem):
        self.marks[name] = (sem, self.cnt.get(sem.name, 0))

    def wm(self, eng, name):
        sem, val = self.marks[name]
        self.w(eng, sem, val)

    def cur(self, sem):
        return self.cnt.get(sem.name, 0)

    def selfwait(self, eng, sem):
        self.w(eng, sem, self.cur(sem))

    def emit(self, block):
        by_eng = {e: [] for e in self.ENGINES}
        for eng, kind, payload in self.ops:
            by_eng[eng].append((kind, payload))

        def run(E, ops):
            waited = {}
            for kind, payload in ops:
                if kind == "psnap":
                    for nm, val in payload.items():
                        if val > 0 and waited.get(nm, -1) < val:
                            E.wait_ge(self.all_sems[nm], val)
                            waited[nm] = val
                    continue
                if kind == "wait":
                    sem, val = payload
                    if waited.get(sem.name, -1) >= val:
                        continue
                    E.wait_ge(sem, val)
                    waited[sem.name] = val
                else:
                    fn, sem, k = payload
                    instr = fn(E)
                    if sem is not None:
                        instr.then_inc(sem, k)

        for eng in self.ENGINES:
            if by_eng[eng]:
                getattr(block, eng)(lambda E, ops=by_eng[eng]: run(E, ops))


class MK:
    def __init__(self, cfg: Cfg):
        self.cfg = cfg
        self.nc = bass.Bass("TRN2", target_bir_lowering=False,
                            dynamic_dma_scratch_size=16384)
        self.P = Prog()
        self.ctx = ExitStack()

    def bcast_col(self, cell_ap_fn, dst_col_fn, tag):
        P = self.P
        P.wm("tensor", "consts")
        P.w("tensor", self.s_ve, P.cur(self.s_ve))
        P.i("tensor", lambda E: E.matmul(
            self.ps_s[:, 30:31], self.ones1f[0:1, :], cell_ap_fn(),
            start=True, stop=True), self.s_te)
        P.w("vector", self.s_te, P.cur(self.s_te))
        P.i("vector", lambda E: E.tensor_copy(out=dst_col_fn(),
                                              in_=self.ps_s[:, 30:31]), self.s_ve)
        P.mark(tag, self.s_ve)

    def lp(self, f):
        with self.nc.allow_low_precision("bf16 gate-weight reduction"):
            return f()

    def sb(self, name, shape, dtype):
        t = self.ctx.enter_context(self.nc.sbuf_tensor(name, shape, dtype))
        setattr(self, name, t)
        return t

    def ps(self, name, shape, dtype):
        t = self.ctx.enter_context(self.nc.psum_tensor(name, shape, dtype))
        setattr(self, name, t)
        return t

    def sem(self, name):
        s = self.ctx.enter_context(self.nc.semaphore(name))
        setattr(self, name, s)
        return s

    def dram(self, name, shape, dtype=F32, kind="ExternalInput", **kw):
        ap = self.nc.dram_tensor(name, shape, dtype, kind=kind, **kw).ap()
        setattr(self, "d_" + name, ap)
        return ap

    def declare(self):
        c = self.cfg
        S, NT = c.s_core, c.ntiles
        self.dram("x", [S, D])
        self.dram("xh", [128, D])
        self.dram("qkv_w", [D, 3 * D]); self.dram("qkv_b", [3 * D])
        self.dram("rq_w", [D, LD]); self.dram("rq_b", [LD])
        self.dram("rkv_w", [D, 2 * LD]); self.dram("rkv_b", [2 * LD])
        self.dram("rout_w", [LD, D]); self.dram("rout_b", [D])
        self.dram("out_w", [D, D]); self.dram("out_b", [D])
        self.dram("gl_w", [HD, 1]); self.dram("gr_w", [HD, 1])
        self.dram("ffn_w", [D, D]); self.dram("ffn_b", [D])
        self.dram("latmask", [NLAT, 16])
        self.dram("halo_valid", [1, 1])
        self.dram("out", [S, D], kind="ExternalOutput")
        self.dram("h_scratch", [S, D], kind="Internal")
        self.d_ccp_in = self.nc.dram_tensor("ccp_in", [16, D], BF16, kind="Internal").ap()
        self.d_ccp_out = self.nc.dram_tensor("ccp_out", [128, D], BF16, kind="Internal", addr_space="Shared").ap()
        self.dram("cca_in", [1, 128], kind="Internal")
        self.dram("cca_out", [1, 128], kind="Internal", addr_space="Shared")
        self.dbg = {}
        if c.dbg:
            for nm, shape, dt_ in [
                ("ln1", [S + 128, D], BF16), ("q", [D, S], BF16),
                ("k", [D, S + 128], BF16), ("v", [S + 128, H * 66], BF16),
                ("lu", [S, H * 66], BF16), ("remote", [S, D], BF16),
                ("mixed", [S, D], BF16), ("h", [S, D], F32),
                ("scal", [1, 8], F32), ("rq", [LD, S], BF16),
                ("rk", [LD, NLAT], BF16), ("rv", [NLAT, LH * 65], BF16),
                ("latn", [S, LD], BF16), ("pooled", [128, D], BF16),
                ("x8", [S, D], BF16), ("rgr", [S, H], F32),
                ("ffn", [S, D], BF16), ("elat", [NLAT, LH * S], BF16),
                ("probe", [128, 2048], F32),
            ]:
                self.dbg[nm] = self.nc.dram_tensor(
                    "dbg_" + nm, shape, dt_, kind="ExternalOutput").ap()

        NTt = NT
        self.sb("qkv_wb", [128, 8, 3 * D], BF16)
        self.sb("rq_wb", [128, 8, LD], BF16)
        self.sb("rkv_wb", [128, 8, 2 * LD], BF16)
        self.sb("rout_wb", [128, 2, 1040], BF16)
        self.sb("out_wb", [128, 8, D], BF16)
        self.sb("wvgl", [128, 8, H], BF16)
        self.sb("w8t", [128, 8, D], BF16)
        self.sb("qkb_col", [128, 16], F32)
        self.sb("rqb_col", [128, 2], F32)
        self.sb("rkb_col", [128, 2], F32)
        self.sb("vb_row", [1, D], BF16)
        self.sb("rvb_row", [1, LD], BF16)
        self.sb("rqb_row", [1, LD], BF16)
        self.sb("rkb_row", [1, LD], BF16)
        self.sb("routb_row", [1, 1040], BF16)
        self.sb("outb_row", [1, D], BF16)
        self.sb("bgl_row", [1, H], BF16)
        self.sb("fbias_row", [1, D], BF16)
        self.sb("glrep", [128, HD], BF16)
        self.sb("pid_sb", [1, 2], mybir.dt.uint32)
        self.sb("grrep", [128, HD], BF16)
        self.sb("glrow", [1, HD], BF16)
        self.sb("grrow", [1, HD], BF16)
        self.sb("maskAB", [128, 512], BF16)
        self.sb("maskAB0", [128, 512], BF16)
        self.sb("a_all", [128, NTt, 16], BF16)
        self.sb("latmask_sb", [128, 16], BF16)
        self.sb("ident", [128, 128], BF16)
        self.sb("ones1", [1, 128], BF16)
        self.sb("ones1f", [1, 128], F32)
        self.sb("cpool", [128, 8], F32)
        cvals = [0.0, 1.0, EPS, -0.5, 0.125, -1.0]
        self.cvals = cvals
        for j, v in enumerate(cvals):
            self.nc.const_aps.register(F32, v, self.cpool[:, j:j + 1])
        self.sb("hval_col", [128, 1], F32)
        self.sb("mr1", [128, NTt + 1, 2], F32)
        self.sb("mr2", [128, NTt, 2], F32)
        self.sb("amax_acc", [128, 1], F32)
        self.sb("wamax_col", [128, 1], F32)
        self.sb("scal", [1, 8], F32)
        self.sb("sxh_col", [128, 1], F32)
        self.sb("sout_col", [128, 1], F32)
        self.sb("sc_tile", [128, 5, 2], F32)
        self.sb("mv_tmp", [128, 2], F32)
        self.sb("bn_buf", [128, 12], F32)
        self.sb("rowmax_sb", [128, 1], F32)
        regP = self.nc.alloc_sbuf_tensor("regP", [128, 2048], mybir.dt.uint8)
        baseP = self.nc.lookup_mloc(regP).addr
        self.pooled_all = self.nc.alloc_sbuf_tensor_at(
            "pooled_all", [64, D], BF16, offset=baseP)
        self.pool_sb = self.nc.alloc_sbuf_tensor_at(
            "pool_sb", [16, D], BF16, offset=baseP)
        self.sb("pooled_fm", [128, 8, NLAT], BF16)
        self.sb("rk_fm", [128, 2, NLAT], BF16)
        self.sb("rv_aug", [128, LH, 65], BF16)
        regA = self.nc.alloc_sbuf_tensor("regA", [128, 16640], mybir.dt.uint8)
        baseA = self.nc.lookup_mloc(regA).addr

        def at(name, shape, dtype, off):
            t = self.nc.alloc_sbuf_tensor_at(name, shape, dtype, offset=off)
            setattr(self, name, t)
            return t

        at("xa_sb", [128, 2, D], F32, baseA)
        at("ln1a_sb", [128, 3, D], BF16, baseA + 8192)
        at("lu_sb", [128, 4, H, 66], BF16, baseA)
        at("remote_sb", [128, 4, D], BF16, baseA + 8448)
        at("outt_sb", [128, 2, D], F32, baseA)
        at("ffn_sb", [128, 2, D], BF16, baseA + 8192)
        at("x8_sb", [128, 2, D], FP8, baseA + 12288)
        regB = self.nc.alloc_sbuf_tensor("regB", [128, 24576], mybir.dt.uint8)
        baseB = self.nc.lookup_mloc(regB).addr
        at("qkvb_row", [1, 3 * D], BF16, baseB)
        at("tmp_row", [1, D], BF16, baseB + 6144)
        at("wstage", [128, D], F32, baseB + 8192)
        at("wtmpb", [128, D], BF16, baseB + 12288)
        at("wtmp8", [128, D], FP8, baseB + 14336)
        self.sb("amax_row", [1, 128], F32)
        at("q_sb", [128, 8, 512], BF16, baseB)
        at("k_sb", [128, 8, 512], BF16, baseB + 8192)
        at("e_sb", [128, 1, 4096], BF16, baseB + 16384)
        at("h2_sb", [128, 5, D], F32, baseB)
        regC = self.nc.alloc_sbuf_tensor("regC", [128, 4096], mybir.dt.uint8)
        baseC = self.nc.lookup_mloc(regC).addr
        at("lnfmh_sb", [128, 8, 128], BF16, baseC)
        at("kh_sb", [128, 8, 128], BF16, baseC + 2048)
        at("mixed_sb", [128, 1, D], BF16, baseC)
        at("mixt_sb", [128, D], BF16, baseC + 2048)
        at("x8b_sb", [128, 2, D], BF16, baseC)
        regD = self.nc.alloc_sbuf_tensor("regD", [128, 16384], mybir.dt.uint8)
        baseD = self.nc.lookup_mloc(regD).addr
        at("xl_sb", [128, 2, D], F32, baseD)
        at("xr_sb", [128, 1, D], F32, baseD + 8192)
        at("h_sb", [128, 1, D], F32, baseD + 12288)
        at("x8fm_sb", [128, 4, 8, 128], BF16, baseD)
        self.sb("ln1b_sb", [128, 2, D], BF16)
        self.sb("lnfm_sb", [128, 8, 512], BF16)
        self.sb("kprev_sb", [128, 8, 128], BF16)
        self.sb("rq_sb", [128, 2, 512], BF16)
        self.sb("v_ring", [128, 5, H, 66], BF16)
        self.sb("elat_sb", [128, 2, 512], BF16)
        self.sb("latn_sb", [128, 4, LH, LHD], BF16)
        at("latfm_sb", [128, 2, 512], BF16, baseP)
        at("fbiaspre_row", [1, D], BF16, baseA + 14336)
        self.sb("dinv_sb", [128, 4, LH], F32)
        self.sb("dtmp", [128, 4], F32)
        self.sb("gsc", [128, 6, H], F32)
        self.sb("rgr_sb", [128, 4, H], F32)

        self.ps("ps_tr", [128, 1024], BF16)
        self.ps("ps_a", [128, 1024], F32)
        self.ps("ps_b", [128, 1024], F32)
        self.ps("ps_c", [128, 1024], F32)
        self.ps("ps_s", [128, 512], F32)

        for s in ["s_sy", "s_gd", "s_gd2", "s_gp", "s_te", "s_sc", "s_ve",
                  "s_vd", "s_sd", "s_xa0", "s_xa1", "s_xl0", "s_xl1", "s_xr",
                  "s_h20", "s_h21", "s_hw", "s_out0", "s_out1", "s_stg"]:
            self.sem(s)
        self.s_xa = [self.s_xa0, self.s_xa1]
        self.s_xl = [self.s_xl0, self.s_xl1]
        self.s_h2 = [self.s_h20, self.s_h21]
        self.s_out = [self.s_out0, self.s_out1]

    def prologue_early(self):
        P, c = self.P, self.cfg
        nc = self.nc
        CH = c.chunk
        NT = c.ntiles

        from concourse.hw_specs import get_activation_tables
        tabs = list(get_activation_tables(nc.m.arch).keys())
        set_id = tabs.index("natural_log_exp_and_others")
        P.i("scalar", lambda E: E.add_instruction(mybir.InstLoadActFuncSet(
            name=nc.get_next_instruction_name(), act_func_set_id=set_id,
            ins=[], outs=[])))

        for cc in range(8):
            P.i("gpsimd", lambda E, cc=cc: E.dma_start(
                out=self.qkv_wb[:, cc, :], in_=self.d_qkv_w[cc * 128:(cc + 1) * 128, :]), self.s_gd, 16)
        for cc in range(8):
            P.i("gpsimd", lambda E, cc=cc: E.dma_start(
                out=self.rq_wb[:, cc, :], in_=self.d_rq_w[cc * 128:(cc + 1) * 128, :]), self.s_gd, 16)
            P.i("gpsimd", lambda E, cc=cc: E.dma_start(
                out=self.rkv_wb[:, cc, :], in_=self.d_rkv_w[cc * 128:(cc + 1) * 128, :]), self.s_gd, 16)
        rows = [
            (self.qkvb_row[0:1, :], AP(self.d_qkv_b.tensor, 0, [[3 * D, 1], [1, 3 * D]])),
            (self.rqb_row[0:1, :], AP(self.d_rq_b.tensor, 0, [[LD, 1], [1, LD]])),
            (self.rkb_row[0:1, :], AP(self.d_rkv_b.tensor, 0, [[LD, 1], [1, LD]])),
            (self.rvb_row[0:1, :], AP(self.d_rkv_b.tensor, LD, [[LD, 1], [1, LD]])),
            (self.routb_row[0:1, 0:D], AP(self.d_rout_b.tensor, 0, [[D, 1], [1, D]])),
            (self.outb_row[0:1, :], AP(self.d_out_b.tensor, 0, [[D, 1], [1, D]])),
            (self.fbias_row[0:1, :], AP(self.d_ffn_b.tensor, 0, [[D, 1], [1, D]])),
            (self.glrow[0:1, :], AP(self.d_gl_w.tensor, 0, [[HD, 1], [1, HD]])),
            (self.grrow[0:1, :], AP(self.d_gr_w.tensor, 0, [[HD, 1], [1, HD]])),
            (self.glrep[:, :], AP(self.d_gl_w.tensor, 0, [[0, 128], [1, HD]])),
            (self.grrep[:, :], AP(self.d_gr_w.tensor, 0, [[0, 128], [1, HD]])),
            (self.latmask_sb[0:64, :], self.d_latmask),
            (self.latmask_sb[64:128, :], self.d_latmask),
            (self.hval_col[:, :], AP(self.d_halo_valid.tensor, 0, [[0, 128], [1, 1]])),
        ]
        for out, in_ in rows:
            P.i("gpsimd", lambda E, o=out, i=in_: E.dma_start(
                out=o, in_=i, allow_slow_non_contiguous=True), self.s_gd, 16)
        P.mark("wdma", self.s_gd)

        P.i("gpsimd", lambda E: E.memset(self.ident[:, :], 1.0), self.s_gp)
        P.i("gpsimd", lambda E: E.memset(self.ones1[0:1, :], 1.0), self.s_gp)
        P.i("gpsimd", lambda E: E.memset(self.ones1f[0:1, :], 1.0), self.s_gp)
        for j, v in enumerate(self.cvals):
            P.i("gpsimd", lambda E, j=j, v=v: E.memset(self.cpool[:, j:j + 1], v),
                self.s_gp)
        P.i("gpsimd", lambda E: E.memset(self.maskAB[:, :], 1.0), self.s_gp)
        P.i("gpsimd", lambda E: E.memset(self.a_all[:, :, :], 1.0 / CH), self.s_gp)
        P.selfwait("gpsimd", self.s_gp)
        P.i("gpsimd", lambda E: E.affine_select(
            out=self.ident[:, :], in_=self.ident[:, :], compare_op=ALU.is_ge,
            fill=0.0, base=0, pattern=[[-1, 128]], channel_multiplier=1), self.s_gp)
        P.selfwait("gpsimd", self.s_gp)
        P.i("gpsimd", lambda E: E.affine_select(
            out=self.ident[:, :], in_=self.ident[:, :], compare_op=ALU.is_ge,
            fill=0.0, base=0, pattern=[[1, 128]], channel_multiplier=-1), self.s_gp)
        P.i("gpsimd", lambda E: E.affine_select(
            out=AP(self.maskAB, 0, [[512, 128], [256, 2], [1, 128]]),
            in_=AP(self.maskAB, 0, [[512, 128], [256, 2], [1, 128]]),
            compare_op=ALU.is_ge, fill=0.0, base=0,
            pattern=[[0, 2], [-1, 128]], channel_multiplier=1), self.s_gp)
        P.selfwait("gpsimd", self.s_gp)
        P.i("gpsimd", lambda E: E.affine_select(
            out=AP(self.maskAB, 128, [[512, 128], [256, 2], [1, 128]]),
            in_=AP(self.maskAB, 128, [[512, 128], [256, 2], [1, 128]]),
            compare_op=ALU.is_gt, fill=0.0, base=0,
            pattern=[[0, 2], [1, 128]], channel_multiplier=-1), self.s_gp)
        P.i("gpsimd", lambda E: E.affine_select(
            out=self.a_all[:, :, :], in_=self.a_all[:, :, :], compare_op=ALU.is_ge,
            fill=0.0, base=0, pattern=[[128, NT], [-CH, 16]], channel_multiplier=1),
            self.s_gp)
        P.selfwait("gpsimd", self.s_gp)
        P.i("gpsimd", lambda E: E.affine_select(
            out=self.a_all[:, :, :], in_=self.a_all[:, :, :], compare_op=ALU.is_ge,
            fill=0.0, base=(CH - 1), pattern=[[-128, NT], [CH, 16]],
            channel_multiplier=-1), self.s_gp)
        P.mark("consts", self.s_gp)
        P.wm("scalar", "consts")

    def wamax_start(self):
        P = self.P
        P.i("vector", lambda E: E.memset(self.wamax_col[:, :], 0.0), self.s_ve)
        for cc in range(2):
            P.i("gpsimd", lambda E, cc=cc: E.dma_start(
                out=self.wam_buf(cc), in_=self.d_ffn_w[cc * 128:(cc + 1) * 128, :]),
                self.s_gd2, 16)
            P.mark(f"wam_d_{cc}", self.s_gd2)

    def wam_buf(self, cc):
        return self.wstage[:, :] if cc % 2 == 0 else self.xl_sb[:, 0, :]

    def wamax_reduce(self, cc):
        P = self.P
        P.w("vector", self.s_gd2, P.cur(self.s_gd2))
        P.selfwait("vector", self.s_ve)
        P.i("vector", lambda E, cc=cc: E.reduce_max(
            out=self.rowmax_sb[:, 0:1], in_=self.wam_buf(cc), axis=AX.X,
            apply_absolute_value=True), self.s_ve)
        P.selfwait("vector", self.s_ve)
        P.i("vector", lambda E: E.tensor_max(
            out=self.wamax_col[:, :], in0=self.wamax_col[:, :],
            in1=self.rowmax_sb[:, 0:1]), self.s_ve)
        P.mark(f"wam_rd_{cc}", self.s_ve)
        if cc + 2 < 8:
            P.wm("gpsimd", f"wam_rd_{cc}")
            P.i("gpsimd", lambda E, cc=cc: E.dma_start(
                out=self.wam_buf(cc + 2),
                in_=self.d_ffn_w[(cc + 2) * 128:(cc + 3) * 128, :]), self.s_gd2, 16)
            P.mark(f"wam_d_{cc + 2}", self.s_gd2)

    def wamax_finish(self):
        P = self.P
        P.selfwait("vector", self.s_ve)
        P.w("scalar", self.s_ve, P.cur(self.s_ve))
        P.i("scalar", lambda E: E.dma_start(out=self.amax_row[0:1, :],
                                            in_=self.wamax_col[:, :]), self.s_stg, 16)
        P.w("vector", self.s_stg, P.cur(self.s_stg))
        P.i("vector", lambda E: E.reduce_max(
            out=self.scal[0:1, 2:3], in_=self.amax_row[0:1, :], axis=AX.X), self.s_ve)
        P.selfwait("vector", self.s_ve)
        P.i("vector", lambda E: E.tensor_scalar_max(
            out=self.scal[0:1, 2:3], in0=self.scal[0:1, 2:3], scalar1=1e-12), self.s_ve)
        P.selfwait("vector", self.s_ve)
        P.i("vector", lambda E: E.reciprocal(out=self.scal[0:1, 3:4],
                                             in_=self.scal[0:1, 2:3]), self.s_ve)
        P.selfwait("vector", self.s_ve)
        P.i("vector", lambda E: E.tensor_scalar(
            out=self.scal[0:1, 3:4], in0=self.scal[0:1, 3:4], scalar1=FP8MAX,
            scalar2=FP8MAX, op0=ALU.mult, op1=ALU.min), self.s_ve)
        P.selfwait("vector", self.s_ve)
        P.i("vector", lambda E: E.tensor_scalar_mul(
            out=self.scal[0:1, 4:5], in0=self.scal[0:1, 3:4], scalar1=0.5), self.s_ve)
        P.selfwait("vector", self.s_ve)
        self.bcast_col(lambda: self.scal[0:1, 4:5], lambda: self.sxh_col[:, :], "swh")
        for cc in range(8):
            P.i("gpsimd", lambda E, cc=cc: E.dma_start(
                out=self.out_wb[:, cc, :], in_=self.d_out_w[cc * 128:(cc + 1) * 128, :]), self.s_gd, 16)
        for cc in range(2):
            P.i("gpsimd", lambda E, cc=cc: E.dma_start(
                out=self.rout_wb[:, cc, 0:D], in_=self.d_rout_w[cc * 128:(cc + 1) * 128, :]), self.s_gd, 16)
        P.mark("wdma2", self.s_gd)

    def prologue_late(self):
        P, c = self.P, self.cfg
        NT = c.ntiles

        P.wm("vector", "consts")
        P.wm("vector", "wdma")
        P.i("vector", lambda E: E.tensor_copy(out=self.maskAB0[:, :], in_=self.maskAB[:, :]), self.s_ve)
        P.selfwait("vector", self.s_ve)
        P.i("vector", lambda E: E.tensor_scalar_mul(
            out=AP(self.maskAB0, 0, [[512, 128], [256, 2], [1, 128]]),
            in0=AP(self.maskAB0, 0, [[512, 128], [256, 2], [1, 128]]),
            scalar1=self.hval_col[:, 0:1]), self.s_ve)

        for cc in range(8):
            P.selfwait("vector", self.s_ve)
            P.i("vector", lambda E, cc=cc: E.tensor_tensor(
                out=AP(self.wtmpb, 0, [[D, 128], [HD, H], [1, HD]]),
                in0=AP(self.qkv_wb, cc * 3 * D + 2 * D,
                       [[8 * 3 * D, 128], [HD, H], [1, HD]]),
                in1=AP(self.glrep, 0, [[HD, 128], [0, H], [1, HD]]),
                op=ALU.mult), self.s_ve)
            P.selfwait("vector", self.s_ve)
            P.i("vector", lambda E, cc=cc: self.lp(lambda: E.reduce_sum(
                out=self.wvgl[:, cc, :],
                in_=AP(self.wtmpb, 0, [[D, 128], [HD, H], [1, HD]]),
                axis=AX.X)), self.s_ve)
        P.wm("vector", "wdma2")
        for cc in range(2):
            P.selfwait("vector", self.s_ve)
            P.i("vector", lambda E, cc=cc: E.tensor_tensor(
                out=AP(self.wtmpb, 0, [[D, 128], [HD, H], [1, HD]]),
                in0=AP(self.rout_wb, cc * 1040, [[2 * 1040, 128], [HD, H], [1, HD]]),
                in1=AP(self.grrep, 0, [[HD, 128], [0, H], [1, HD]]),
                op=ALU.mult), self.s_ve)
            P.selfwait("vector", self.s_ve)
            P.i("vector", lambda E, cc=cc: self.lp(lambda: E.reduce_sum(
                out=self.rout_wb[:, cc, D:D + H],
                in_=AP(self.wtmpb, 0, [[D, 128], [HD, H], [1, HD]]),
                axis=AX.X)), self.s_ve)
        P.i("vector", lambda E: E.tensor_tensor(
            out=AP(self.tmp_row, 0, [[D, 1], [HD, H], [1, HD]]),
            in0=AP(self.qkvb_row, 2 * D, [[3 * D, 1], [HD, H], [1, HD]]),
            in1=AP(self.glrow, 0, [[HD, 1], [0, H], [1, HD]]), op=ALU.mult), self.s_ve)
        P.selfwait("vector", self.s_ve)
        P.i("vector", lambda E: self.lp(lambda: E.reduce_sum(
            out=self.bgl_row[0:1, :], in_=AP(self.tmp_row, 0, [[D, 1], [HD, H], [1, HD]]),
            axis=AX.X)), self.s_ve)
        P.selfwait("vector", self.s_ve)
        P.i("vector", lambda E: E.tensor_tensor(
            out=AP(self.tmp_row, 0, [[D, 1], [HD, H], [1, HD]]),
            in0=AP(self.routb_row, 0, [[1040, 1], [HD, H], [1, HD]]),
            in1=AP(self.grrow, 0, [[HD, 1], [0, H], [1, HD]]), op=ALU.mult), self.s_ve)
        P.selfwait("vector", self.s_ve)
        P.i("vector", lambda E: self.lp(lambda: E.reduce_sum(
            out=self.routb_row[0:1, D:D + H],
            in_=AP(self.tmp_row, 0, [[D, 1], [HD, H], [1, HD]]), axis=AX.X)), self.s_ve)
        P.i("vector", lambda E: E.tensor_copy(
            out=self.vb_row[0:1, :], in_=self.qkvb_row[0:1, 2 * D:3 * D]), self.s_ve)
        for sl in range(5):
            P.i("vector", lambda E, sl=sl: E.memset(
                AP(self.v_ring, sl * H * 66 + 64, [[5 * H * 66, 128], [66, H], [1, 1]]),
                1.0), self.s_ve)
        P.i("vector", lambda E: E.memset(
            AP(self.rv_aug, 64, [[LH * 65, 128], [65, LH], [1, 1]]), 1.0), self.s_ve)
        P.mark("prologue_v", self.s_ve)

        P.wm("tensor", "wdma")
        P.wm("tensor", "consts")
        for j in range(16):
            P.i("tensor", lambda E, j=j: E.transpose(
                self.ps_tr[:, 2 * j:2 * j + 1], self.qkvb_row[0:1, j * 128:(j + 1) * 128],
                self.ones1[0:1, 0:1]), self.s_te)
        for j in range(2):
            P.i("tensor", lambda E, j=j: E.transpose(
                self.ps_tr[:, 32 + 2 * j:33 + 2 * j],
                self.rqb_row[0:1, j * 128:(j + 1) * 128],
                self.ones1[0:1, 0:1]), self.s_te)
            P.i("tensor", lambda E, j=j: E.transpose(
                self.ps_tr[:, 36 + 2 * j:37 + 2 * j],
                self.rkb_row[0:1, j * 128:(j + 1) * 128],
                self.ones1[0:1, 0:1]), self.s_te)
        P.w("vector", self.s_te, P.cur(self.s_te))
        P.i("vector", lambda E: E.tensor_copy(
            out=self.qkb_col[:, :], in_=AP(self.ps_tr, 0, [[1024, 128], [2, 16]])),
            self.s_ve)
        P.i("vector", lambda E: E.tensor_copy(
            out=self.rqb_col[:, :], in_=AP(self.ps_tr, 32, [[1024, 128], [2, 2]])),
            self.s_ve)
        P.i("vector", lambda E: E.tensor_copy(
            out=self.rkb_col[:, :], in_=AP(self.ps_tr, 36, [[1024, 128], [2, 2]])),
            self.s_ve)
        P.mark("bias_cols", self.s_ve)

    def w8t_chunk(self, cc, pstr_wait="bias_cols"):
        P = self.P
        if cc == 0:
            P.wm("sync", "wam_rd_6")
        else:
            P.wm("sync", f"w8q_{cc - 1}")
        P.i("sync", lambda E, cc=cc: E.dma_start(
            out=self.wstage[:, :], in_=self.d_ffn_w[cc * 128:(cc + 1) * 128, :]), self.s_sy, 16)
        P.mark(f"w8d_{cc}", self.s_sy)
        P.wm("scalar", f"w8d_{cc}")
        P.wm("scalar", "swh")
        if cc > 0:
            P.wm("scalar", f"w8c_{cc - 1}")
        P.i("scalar", lambda E: E.activation(
            out=self.wtmp8[:, :], in_=self.wstage[:, :], func=AF.Copy,
            scale=self.sxh_col[:, 0:1]), self.s_sc)
        P.mark(f"w8q_{cc}", self.s_sc)
        P.wm("vector", f"w8q_{cc}")
        if cc > 0:
            P.wm("vector", f"w8tr_{cc - 1}")
        P.i("vector", lambda E: E.tensor_copy(out=self.wtmpb[:, :],
                                              in_=self.wtmp8[:, :]), self.s_ve)
        P.mark(f"w8c_{cc}", self.s_ve)
        P.wm("tensor", f"w8c_{cc}")
        if cc > 0:
            P.wm("tensor", f"w8t_ev_{cc - 1}")
        else:
            P.wm("tensor", pstr_wait)
        for j in range(8):
            P.i("tensor", lambda E, j=j: E.transpose(
                self.ps_tr[:, j * 128:(j + 1) * 128],
                self.wtmpb[:, j * 128:(j + 1) * 128], self.ident[:, :]), self.s_te)
        P.mark(f"w8tr_{cc}", self.s_te)
        P.wm("vector", f"w8tr_{cc}")
        P.i("vector", lambda E, cc=cc: E.tensor_copy(
            out=AP(self.w8t, cc * 128, [[8 * D, 128], [D, 8], [1, 128]]),
            in_=self.ps_tr[:, :]), self.s_ve)
        P.mark(f"w8t_ev_{cc}", self.s_ve)
        if cc == 7:
            P.mark("w8t", self.s_ve)

    def p1a(self):
        P, c = self.P, self.cfg
        NT = c.ntiles

        def tile(i, t):
            self.dbgq("scalar", "vector", "sync")
            sl = i % 2
            lsl = i % 3
            eng = "sync" if sl == 0 else "scalar"
            src = self.d_xh if t == NT else self.d_x[t * 128:(t + 1) * 128, :]
            if i >= 2:
                P.wm(eng, f"a_bn_{i - 2}")
                P.wm(eng, f"a_ln_{i - 2}")
            P.i(eng, lambda E: E.dma_start(out=self.xa_sb[:, sl, :], in_=src),
                self.s_xa[sl], 16)
            P.w("vector", self.s_xa[sl], P.cur(self.s_xa[sl]))
            P.selfwait("vector", self.s_ve)
            P.i("vector", lambda E: E.bn_stats(out=self.bn_buf[:, 0:6],
                                               in_=self.xa_sb[:, sl, 0:512]), self.s_ve)
            P.i("vector", lambda E: E.bn_stats(out=self.bn_buf[:, 6:12],
                                               in_=self.xa_sb[:, sl, 512:1024]), self.s_ve)
            P.selfwait("vector", self.s_ve)
            P.i("vector", lambda E: E.bn_aggr(out=self.mv_tmp[:, :],
                                              in_=self.bn_buf[:, :]), self.s_ve)
            P.mark(f"a_bn_{i}", self.s_ve)
            P.w("scalar", self.s_ve, P.cur(self.s_ve))
            P.i("scalar", lambda E, t=t: E.activation(
                out=self.mr1[:, t, 0:1], in_=self.mv_tmp[:, 1:2], func=AF.Ln,
                bias=EPS), self.s_sc)
            P.selfwait("scalar", self.s_sc)
            P.i("scalar", lambda E, t=t: E.activation(
                out=self.mr1[:, t, 0:1], in_=self.mr1[:, t, 0:1], func=AF.Exp,
                scale=-0.5), self.s_sc)
            P.w("vector", self.s_sc, P.cur(self.s_sc))
            P.i("vector", lambda E, t=t: E.tensor_scalar(
                out=self.mr1[:, t, 1:2], in0=self.mv_tmp[:, 0:1],
                scalar1=self.mr1[:, t, 0:1], scalar2=-1.0,
                op0=ALU.mult, op1=ALU.mult), self.s_ve)
            P.mark(f"a_mr_{i}", self.s_ve)
            P.w("scalar", self.s_ve, P.cur(self.s_ve))
            if i >= 3:
                P.wm("scalar", f"a_pool_{i - 3}")
            P.i("scalar", lambda E, t=t, sl=sl, lsl=lsl: E.activation(
                out=self.ln1a_sb[:, lsl, :], in_=self.xa_sb[:, sl, :], func=AF.Identity,
                scale=self.mr1[:, t, 0:1], bias=self.mr1[:, t, 1:2]), self.s_sc)
            P.mark(f"a_ln_{i}", self.s_sc)
            if t != NT:
                P.w("tensor", self.s_sc, P.cur(self.s_sc))
                P.wm("tensor", "consts")
                first = (t == 0)
                P.i("tensor", lambda E, t=t, lsl=lsl, first=first: E.matmul(
                    self.ps_a[0:16, 0:512], self.a_all[:, t, :],
                    self.ln1a_sb[:, lsl, 0:512], start=first,
                    stop=(t == NT - 1)), self.s_te)
                P.i("tensor", lambda E, t=t, lsl=lsl, first=first: E.matmul(
                    self.ps_a[0:16, 512:1024], self.a_all[:, t, :],
                    self.ln1a_sb[:, lsl, 512:1024], start=first,
                    stop=(t == NT - 1)), self.s_te)
            P.mark(f"a_pool_{i}", self.s_te)
            if self.cfg.dbg:
                P.selfwait("scalar", self.s_sc)
                row = NT * 128 if t == NT else t * 128
                P.i("scalar", lambda E, lsl=lsl, row=row: E.dma_start(
                    out=self.dbg["ln1"][row:row + 128, :],
                    in_=self.ln1a_sb[:, lsl, :]), self.s_sd, 16)

        tile(0, NT)
        self.wamax_start()
        for t in range(NT):
            tile(t + 1, t)
            if t >= 2 and t % 2 == 0 and (t - 2) // 2 < 8:
                self.wamax_reduce((t - 2) // 2)
        P.mark("pool_mm", self.s_te)

    def ag_start(self):
        P = self.P
        P.wm("vector", "pool_mm")
        P.i("vector", lambda E: E.tensor_copy(out=self.pool_sb[:, :],
                                              in_=self.ps_a[0:16, :]), self.s_ve)
        P.mark("pool_ev", self.s_ve)
        P.selfwait("vector", self.s_ve)
        P.w("scalar", self.s_ve, P.cur(self.s_ve))
        P.i("scalar", lambda E: E.dma_start(out=self.d_ccp_in, in_=self.pool_sb[:, :]),
            self.s_stg, 16)
        P.w("gpsimd", self.s_stg, P.cur(self.s_stg))
        P.i("gpsimd", lambda E: E.collective_compute(
            "AllGather", ALU.bypass, ins=[self.d_ccp_in], outs=[self.d_ccp_out],
            replica_groups=RG), self.s_gp)
        P.selfwait("gpsimd", self.s_gp)
        pid = self.nc.partition_id_tensor
        P.i("gpsimd", lambda E: E.dma_start(out=self.pid_sb[0:1, 0:1],
                                            in_=AP(pid, 0, [[1, 1], [1, 1]])), self.s_gd2, 16)
        P.w("gpsimd", self.s_gd2, P.cur(self.s_gd2))

        def dyn_read(E):
            with E.register("rpid") as r:
                E.reg_load(r, self.pid_sb[0:1, 0:1])
                E.reg_alu(r, r, 4, ALU.divide)
                E.reg_alu(r, r, 64 * D, ALU.mult)
                return E.dma_start(out=self.pooled_all[0:64, :],
                                   in_=AP(self.d_ccp_out.tensor, r, [[D, 64], [1, D]]))
        P.i("gpsimd", dyn_read, self.s_gd2, 16)
        P.mark("pooled_my", self.s_gd2)
        if self.cfg.dbg:
            P.w("gpsimd", self.s_gd2, P.cur(self.s_gd2))
            P.i("gpsimd", lambda E: E.dma_start(out=self.dbg["pooled"],
                                                in_=self.d_ccp_out), self.s_gd2, 16)

    def ag_finish(self):
        P = self.P
        P.wm("tensor", "pooled_my")
        P.wm("tensor", "b_fm_2")
        for j in range(8):
            P.i("tensor", lambda E, j=j: E.transpose(
                self.ps_tr[:, j * NLAT:(j + 1) * NLAT],
                self.pooled_all[0:64, j * 128:(j + 1) * 128],
                self.ident[0:64, 0:64]), self.s_te)
        P.w("vector", self.s_te, P.cur(self.s_te))
        P.i("vector", lambda E: E.tensor_copy(
            out=self.pooled_fm[:, :, :],
            in_=AP(self.ps_tr, 0, [[1024, 128], [NLAT, 8], [1, NLAT]])), self.s_ve)
        P.mark("pooled_fm", self.s_ve)

        P.wm("tensor", "pooled_fm")
        P.wm("tensor", "v_2")
        for rc in range(2):
            for dc in range(8):
                P.i("tensor", lambda E, rc=rc, dc=dc: E.matmul(
                    self.ps_s[:, rc * NLAT:(rc + 1) * NLAT],
                    self.rkv_wb[:, dc, rc * 128:(rc + 1) * 128],
                    self.pooled_fm[:, dc, :], start=(dc == 0), stop=(dc == 7)),
                    self.s_te)
        P.w("scalar", self.s_te, P.cur(self.s_te))
        for rc in range(2):
            P.i("scalar", lambda E, rc=rc: E.activation(
                out=self.rk_fm[:, rc, :], in_=self.ps_s[:, rc * NLAT:(rc + 1) * NLAT],
                func=AF.Identity, bias=self.rkb_col[:, rc:rc + 1]), self.s_sc)
        P.mark("rk", self.s_sc)

        P.wm("tensor", "rk")
        P.i("tensor", lambda E: E.matmul(
            self.ps_s[0:64, 0:LD], self.ones1[0:1, 0:64],
            self.rvb_row[0:1, :], start=True, stop=False), self.s_te)
        for dc in range(8):
            P.i("tensor", lambda E, dc=dc: E.matmul(
                self.ps_s[0:64, 0:LD], self.pooled_fm[:, dc, :],
                self.rkv_wb[:, dc, LD:2 * LD], start=False, stop=(dc == 7)), self.s_te)
        P.w("vector", self.s_te, P.cur(self.s_te))
        P.i("vector", lambda E: E.tensor_copy(
            out=AP(self.rv_aug, 0, [[LH * 65, 64], [65, LH], [1, LHD]]),
            in_=self.ps_s[0:64, 0:LD]), self.s_ve)
        P.mark("rv_ps_done", self.s_ve)
        P.selfwait("vector", self.s_ve)
        P.w("scalar", self.s_ve, P.cur(self.s_ve))
        P.i("scalar", lambda E: E.dma_start(
            out=self.rv_aug[64:128, :, :], in_=self.rv_aug[0:64, :, :]), self.s_stg, 16)
        P.mark("rv", self.s_stg)
        if self.cfg.dbg:
            P.wm("vector", "rv")
            P.w("scalar", self.s_ve, P.cur(self.s_ve))
            P.i("scalar", lambda E: E.dma_start(
                out=AP(self.dbg["rk"].tensor, 0,
                       [[NLAT, 128], [128 * NLAT, 2], [1, NLAT]]),
                in_=AP(self.rk_fm, 0, [[2 * NLAT, 128], [NLAT, 2], [1, NLAT]])),
                self.s_vd, 16)
            P.w("scalar", self.s_ve, P.cur(self.s_ve))
            P.i("scalar", lambda E: E.dma_start(
                out=self.dbg["rv"], in_=self.rv_aug[0:64, :, :]), self.s_vd, 16)

    def dbgq(self, *engs):
        if not self.cfg.dbg:
            return
        for e in engs:
            self.P.w(e, self.s_vd, self.P.cur(self.s_vd))
            self.P.w(e, self.s_sd, self.P.cur(self.s_sd))

    def chain_tile(self, i, t, pstr_wait=None, pss_wait=None):
        P, c = self.P, self.cfg
        self.dbgq(*("vector", "scalar", "tensor"))
        NT = c.ntiles
        halo = (t == NT)
        xsl = i % 2
        lsl = i % 2
        src = self.d_xh if halo else self.d_x[t * 128:(t + 1) * 128, :]
        if i >= 2:
            P.wm("sync", f"b_ln_{i - 2}")
        if i == 0:
            P.wm("sync", "wam_rd_7")
        P.i("sync", lambda E, xsl=xsl, src=src: E.dma_start(
            out=self.xl_sb[:, xsl, :], in_=src), self.s_xl[xsl], 16)
        P.w("scalar", self.s_xl[xsl], P.cur(self.s_xl[xsl]))
        if i >= 2:
            P.wm("scalar", f"b_tr_{i - 2}")
        P.i("scalar", lambda E, t=t, xsl=xsl, lsl=lsl: E.activation(
            out=self.ln1b_sb[:, lsl, :], in_=self.xl_sb[:, xsl, :], func=AF.Identity,
            scale=self.mr1[:, t, 0:1], bias=self.mr1[:, t, 1:2]), self.s_sc)
        P.mark(f"b_ln_{i}", self.s_sc)
        P.w("tensor", self.s_sc, P.cur(self.s_sc))
        if i >= 1:
            P.wm("tensor", f"b_fm_{i - 1}")
        if pstr_wait:
            P.wm("tensor", pstr_wait)
        if not halo and t % 4 == 0 and t // 4 >= 1:
            P.wm("tensor", f"latfm_{t // 4 - 1}_1")
        for j in range(8):
            P.i("tensor", lambda E, j=j, lsl=lsl: E.transpose(
                self.ps_tr[:, j * 128:(j + 1) * 128],
                self.ln1b_sb[:, lsl, j * 128:(j + 1) * 128], self.ident[:, :]),
                self.s_te)
        P.mark(f"b_tr_{i}", self.s_te)
        P.w("vector", self.s_te, P.cur(self.s_te))
        if halo:
            P.i("vector", lambda E: E.tensor_copy(
                out=self.lnfmh_sb[:, :, :],
                in_=AP(self.ps_tr, 0, [[1024, 128], [128, 8], [1, 128]])), self.s_ve)
        else:
            tin = t % 4
            P.i("vector", lambda E, tin=tin: E.tensor_copy(
                out=AP(self.lnfm_sb, tin * 128, [[8 * 512, 128], [512, 8], [1, 128]]),
                in_=AP(self.ps_tr, 0, [[1024, 128], [128, 8], [1, 128]])), self.s_ve)
        P.mark(f"b_fm_{i}", self.s_ve)

        P.wm("tensor", f"b_fm_{i}")
        if i >= 1:
            P.wm("tensor", f"v_{i - 1}")
        vslot = 0 if halo else (t + 1) % 5
        if not halo and t >= 4:
            P.wm("tensor", f"av_done_{t - 4}")
        if pss_wait:
            P.wm("tensor", pss_wait)
        if halo:
            P.wm("tensor", "pool_ev")
        elif t // 4 >= 1 and t % 4 == 0:
            P.wm("tensor", f"rem_ev_v_{t // 4 - 1}_3")
            P.wm("tensor", f"exp_{t - 1}_3")
        lnsrc = (lambda dc: self.lnfmh_sb[:, dc, :]) if halo else \
                (lambda dc, tin=t % 4: self.lnfm_sb[:, dc, tin * 128:(tin + 1) * 128])
        for half in range(2):
            P.i("tensor", lambda E, half=half: E.matmul(
                self.ps_a[:, half * 512:(half + 1) * 512], self.ones1[0:1, :],
                self.vb_row[0:1, half * 512:(half + 1) * 512],
                start=True, stop=False), self.s_te)
            for dc in range(8):
                P.i("tensor", lambda E, dc=dc, half=half: E.matmul(
                    self.ps_a[:, half * 512:(half + 1) * 512], lnsrc(dc),
                    self.qkv_wb[:, dc, 2 * D + half * 512:2 * D + (half + 1) * 512],
                    start=False, stop=(dc == 7)), self.s_te)
        P.i("tensor", lambda E: E.matmul(
            self.ps_s[:, 0:H], self.ones1[0:1, :], self.bgl_row[0:1, :],
            start=True, stop=False), self.s_te)
        for dc in range(8):
            P.i("tensor", lambda E, dc=dc: E.matmul(
                self.ps_s[:, 0:H], lnsrc(dc), self.wvgl[:, dc, :],
                start=False, stop=(dc == 7)), self.s_te)
        P.mark(f"v_mm_{i}", self.s_te)
        P.w("vector", self.s_te, P.cur(self.s_te))
        P.i("vector", lambda E, vslot=vslot: E.tensor_copy(
            out=AP(self.v_ring, vslot * H * 66, [[5 * H * 66, 128], [66, H], [1, HD]]),
            in_=AP(self.ps_a, 0, [[1024, 128], [HD, H], [1, HD]])), self.s_ve)
        P.i("vector", lambda E, vslot=vslot: E.tensor_copy(
            out=AP(self.v_ring, vslot * H * 66 + 65, [[5 * H * 66, 128], [66, H], [1, 1]]),
            in_=self.ps_s[:, 0:H]), self.s_ve)
        P.mark(f"v_{i}", self.s_ve)
        if self.cfg.dbg:
            P.selfwait("vector", self.s_ve)
            row = NT * 128 if halo else t * 128
            P.w("scalar", self.s_ve, P.cur(self.s_ve))
            P.i("scalar", lambda E, vslot=vslot, row=row: E.dma_start(
                out=self.dbg["v"][row:row + 128, :],
                in_=self.v_ring[:, vslot, :, :]), self.s_vd, 16)

    def qk_group(self, g, gates=()):
        P, c = self.P, self.cfg
        self.dbgq(*("scalar",))
        for j in range(16):
            half = j % 2
            if j >= 2:
                P.wm("tensor", f"qk_ev_{g}_{j - 2}")
            else:
                P.wm("tensor", f"v_{4 * g + 4}")
            if g == 0 and j == 0:
                P.wm("tensor", "kh")
                P.wm("tensor", "prologue_v")
            for dc in range(8):
                P.i("tensor", lambda E, j=j, dc=dc, half=half: E.matmul(
                    self.ps_a[:, half * 512:(half + 1) * 512],
                    self.qkv_wb[:, dc, j * 128:(j + 1) * 128],
                    self.lnfm_sb[:, dc, :], start=(dc == 0), stop=(dc == 7)), self.s_te)
            P.mark(f"qk_mm_{g}_{j}", self.s_te)
            if self.cfg.dbg and g == 0 and j == 0:
                P.w("gpsimd", self.s_te, P.cur(self.s_te))
                P.i("gpsimd", lambda E: E.dma_start(
                    out=self.dbg["probe"][0:128, 0:2048],
                    in_=self.qkv_wb[:, 0, 0:2048]), self.s_gd2, 16)
                P.w("scalar", self.s_gd2, P.cur(self.s_gd2))
            P.w("scalar", self.s_te, P.cur(self.s_te))
            if j < 8:
                dst = lambda E, j=j: self.q_sb[:, j, :]
            else:
                dst = lambda E, j=j: self.k_sb[:, j - 8, :]
            if j == 8 and g > 0:
                P.wm("scalar", f"kprev_{g - 1}")
            P.i("scalar", lambda E, j=j, half=half, dst=dst: E.activation(
                out=dst(E), in_=self.ps_a[:, half * 512:(half + 1) * 512],
                func=AF.Identity, bias=self.qkb_col[:, j:j + 1]), self.s_sc)
            P.mark(f"qk_ev_{g}_{j}", self.s_sc)
            if j % 4 == 3 and gates:
                self.gate_outw_res(gates[j // 4])
        P.mark(f"qk_{g}", self.s_sc)
        if self.cfg.dbg:
            P.selfwait("scalar", self.s_sc)
            for j in range(8):
                P.i("scalar", lambda E, j=j, g=g: E.dma_start(
                    out=self.dbg["q"][j * 128:(j + 1) * 128, g * 512:(g + 1) * 512],
                    in_=self.q_sb[:, j, :]), self.s_sd, 16)
                P.i("scalar", lambda E, j=j, g=g: E.dma_start(
                    out=self.dbg["k"][j * 128:(j + 1) * 128, g * 512:(g + 1) * 512],
                    in_=self.k_sb[:, j, :]), self.s_sd, 16)

    def rq_group(self, g):
        P = self.P
        self.dbgq(*("scalar",))
        for rc in range(2):
            P.wm("tensor", f"qk_ev_{g}_{14 + rc}")
            for dc in range(8):
                P.i("tensor", lambda E, rc=rc, dc=dc: E.matmul(
                    self.ps_a[:, rc * 512:(rc + 1) * 512],
                    self.rq_wb[:, dc, rc * 128:(rc + 1) * 128],
                    self.lnfm_sb[:, dc, :], start=(dc == 0), stop=(dc == 7)), self.s_te)
            P.mark(f"rq_mm_{g}_{rc}", self.s_te)
            P.w("scalar", self.s_te, P.cur(self.s_te))
            P.i("scalar", lambda E, rc=rc: E.activation(
                out=self.rq_sb[:, rc, :], in_=self.ps_a[:, rc * 512:(rc + 1) * 512],
                func=AF.Identity, bias=self.rqb_col[:, rc:rc + 1]), self.s_sc)
            P.mark(f"rq_ev_{g}_{rc}", self.s_sc)
        P.mark(f"rq_{g}", self.s_sc)
        if self.cfg.dbg:
            P.selfwait("scalar", self.s_sc)
            for rc in range(2):
                P.i("scalar", lambda E, rc=rc, g=g: E.dma_start(
                    out=self.dbg["rq"][rc * 128:(rc + 1) * 128, g * 512:(g + 1) * 512],
                    in_=self.rq_sb[:, rc, :]), self.s_sd, 16)

    def local_qtile(self, tq):
        P, c = self.P, self.cfg
        self.dbgq(*("vector", "scalar", "tensor"))
        g = tq // 4
        t = tq % 4
        esl = 0
        for w in range(4):
            pb = self.ps_b if w % 2 == 0 else self.ps_a
            if w >= 2:
                P.wm("tensor", f"exp_{tq}_{w - 2}")
            elif w == 0:
                if tq >= 4 and t == 0:
                    P.wm("tensor", f"res1_{tq - 1}")
                elif tq >= 1:
                    P.wm("tensor", f"exp_{tq - 1}_2")
            else:
                if t == 0:
                    P.wm("tensor", f"rq_ev_{g}_1")
                else:
                    P.wm("tensor", f"exp_{tq - 1}_3")
            if w == 0 and t == 0 and g > 0:
                P.wm("tensor", f"kprev_{g - 1}")
            P.wm("tensor", f"qk_{g}")
            for hh in range(2):
                for ab in range(2):
                    for hp in range(2):
                        h = 4 * w + 2 * hh + hp
                        kchunk = h // 2
                        rb = (h % 2) * 64
                        if ab == 0:
                            if tq == 0:
                                klhs = lambda E, kchunk=kchunk, rb=rb: \
                                    self.kh_sb[rb:rb + 64, kchunk, :]
                            elif t == 0:
                                klhs = lambda E, kchunk=kchunk, rb=rb: \
                                    self.kprev_sb[rb:rb + 64, kchunk, :]
                            else:
                                klhs = lambda E, kchunk=kchunk, rb=rb, t=t: \
                                    self.k_sb[rb:rb + 64, kchunk, (t - 1) * 128:t * 128]
                        else:
                            klhs = lambda E, kchunk=kchunk, rb=rb, t=t: \
                                self.k_sb[rb:rb + 64, kchunk, t * 128:(t + 1) * 128]
                        col = (h % 2) * 512 + hh * 256 + ab * 128
                        P.i("tensor", lambda E, klhs=klhs, h=h, t=t, col=col, pb=pb: E.matmul(
                            pb[:, col:col + 128], klhs(E),
                            self.q_sb[(h % 2) * 64:(h % 2) * 64 + 64, h // 2,
                                      t * 128:(t + 1) * 128],
                            start=True, stop=True), self.s_te)
            P.mark(f"sc_{tq}_{w}", self.s_te)
            P.w("scalar", self.s_te, P.cur(self.s_te))
            if tq >= 1 and w == 0:
                P.wm("scalar", f"av_done_{tq - 1}")
            P.i("scalar", lambda E, esl=esl, w=w, pb=pb: E.activation(
                out=self.e_sb[:, 0, w * 1024:(w + 1) * 1024],
                in_=pb[:, :], func=AF.Exp, scale=0.125), self.s_sc)
            P.mark(f"exp_{tq}_{w}", self.s_sc)
            if w % 2 == 1:
                P.w("vector", self.s_sc, P.cur(self.s_sc))
                mask = self.maskAB0 if tq == 0 else self.maskAB
                P.i("vector", lambda E, w=w, mask=mask: E.tensor_tensor(
                    out=AP(self.e_sb, (w - 1) * 1024, [[4096, 128], [1, 2048]]),
                    in0=AP(self.e_sb, (w - 1) * 1024, [[4096, 128], [1, 2048]]),
                    in1=AP(mask, 0, [[512, 128], [0, 4], [1, 512]]),
                    op=ALU.mult), self.s_ve)
                P.mark(f"mask_{tq}_{w // 2}", self.s_ve)

        vA = tq % 5
        vB = (tq + 1) % 5
        P.wm("tensor", f"v_{tq + 1}")
        if t == 0 and tq >= 4:
            P.wm("tensor", f"amax_{tq - 1}")
        if t == 0 and g >= 1:
            P.wm("tensor", f"latn_ev_{g - 1}_3")

        def av(h, col, extra_wait=None):
            P.wm("tensor", f"mask_{tq}_{(h // 4) // 2}")
            if extra_wait:
                P.wm("tensor", extra_wait)
            if tq >= 1:
                P.wm("tensor", f"lu_ev_{tq - 1}_all")
            w = h // 4
            ecol = lambda ab: w * 1024 + (h % 2) * 512 + \
                ((h % 4) // 2) * 256 + ab * 128
            P.i("tensor", lambda E, h=h, col=col: E.matmul(
                self.ps_c[:, col:col + 66],
                AP(self.e_sb, ecol(0), [[4096, 128], [1, 128]]),
                AP(self.v_ring, vA * H * 66 + h * 66, [[5 * H * 66, 128], [1, 66]]),
                start=True, stop=False), self.s_te)
            P.i("tensor", lambda E, h=h, col=col: E.matmul(
                self.ps_c[:, col:col + 66],
                AP(self.e_sb, ecol(1), [[4096, 128], [1, 128]]),
                AP(self.v_ring, vB * H * 66 + h * 66, [[5 * H * 66, 128], [1, 66]]),
                start=False, stop=True), self.s_te)

        for h in range(7):
            av(h, h * 66)
        P.mark(f"av_r0_{tq}", self.s_te)
        for h in range(7, 14):
            av(h, 512 + (h - 7) * 66)
        P.mark(f"av_r1_{tq}", self.s_te)
        P.wm("vector", f"av_r0_{tq}")
        P.i("vector", lambda E, t=t: E.tensor_copy(
            out=self.lu_sb[:, t, 0:7, :], in_=self.ps_c[:, 0:462]), self.s_ve)
        P.mark(f"lu_ev_{tq}_0", self.s_ve)
        for h in range(14, 16):
            av(h, (h - 14) * 66, extra_wait=f"lu_ev_{tq}_0")
        P.mark(f"av_r2_{tq}", self.s_te)
        P.wm("vector", f"av_r1_{tq}")
        P.i("vector", lambda E, t=t: E.tensor_copy(
            out=self.lu_sb[:, t, 7:14, :], in_=self.ps_c[:, 512:974]), self.s_ve)
        P.wm("vector", f"av_r2_{tq}")
        P.i("vector", lambda E, t=t: E.tensor_copy(
            out=self.lu_sb[:, t, 14:16, :], in_=self.ps_c[:, 0:132]), self.s_ve)
        P.mark(f"lu_ev_{tq}_all", self.s_ve)
        P.mark(f"av_done_{tq}", self.s_te)
        if self.cfg.dbg:
            P.selfwait("vector", self.s_ve)
            P.w("scalar", self.s_ve, P.cur(self.s_ve))
            P.i("scalar", lambda E, tq=tq, t=t: E.dma_start(
                out=self.dbg["lu"][tq * 128:(tq + 1) * 128, :],
                in_=self.lu_sb[:, t, :, :]), self.s_vd, 16)

    def latent_group(self, g):
        P, c = self.P, self.cfg
        self.dbgq(*("vector", "scalar", "tensor"))
        CH = c.chunk
        lsc = 1.0 / np.sqrt(LHD)
        P.wm("tensor", "rv")
        P.wm("tensor", "rk")
        P.wm("tensor", f"rq_{g}")
        P.wm("tensor", f"exp_{4 * g + 3}_3")
        for pr in range(2):
            for hp in range(2):
                hq = 2 * pr + hp
                rb = hp * 64
                if pr == 0:
                    P.i("tensor", lambda E, hq=hq, rb=rb: E.matmul(
                        self.ps_s[rb:rb + 64, 0:512],
                        self.rk_fm[rb:rb + 64, hq // 2, :],
                        self.rq_sb[rb:rb + 64, hq // 2, :],
                        start=True, stop=True), self.s_te)
                else:
                    P.i("tensor", lambda E, hq=hq, rb=rb: E.matmul(
                        self.ps_b[rb:rb + 64, 0:512],
                        self.rk_fm[rb:rb + 64, hq // 2, :],
                        self.rq_sb[rb:rb + 64, hq // 2, :],
                        start=True, stop=True), self.s_te)
            P.mark(f"lsc_{g}_{pr}", self.s_te)
            P.w("scalar", self.s_te, P.cur(self.s_te))
            if pr == 0:
                P.i("scalar", lambda E: E.activation(
                    out=self.elat_sb[:, 0, :], in_=self.ps_s[:, 0:512],
                    func=AF.Exp, scale=lsc), self.s_sc)
            else:
                P.i("scalar", lambda E: E.activation(
                    out=self.elat_sb[:, 1, :], in_=self.ps_b[:, 0:512],
                    func=AF.Exp, scale=lsc), self.s_sc)
            P.mark(f"lexp_{g}_{pr}", self.s_sc)
        P.w("vector", self.s_sc, P.cur(self.s_sc))
        ncg = c.ch_per_grp
        P.i("vector", lambda E, g=g, ncg=ncg: E.tensor_tensor(
            out=AP(self.elat_sb, 0, [[2 * 512, 128], [512, 2], [1, 512]]),
            in0=AP(self.elat_sb, 0, [[2 * 512, 128], [512, 2], [1, 512]]),
            in1=AP(self.latmask_sb, g * ncg, [[16, 128], [0, 2], [1, ncg], [0, CH]]),
            op=ALU.mult), self.s_ve)
        P.mark(f"lmask_{g}", self.s_ve)
        if self.cfg.dbg:
            P.selfwait("vector", self.s_ve)
            P.w("scalar", self.s_ve, P.cur(self.s_ve))
            P.i("scalar", lambda E, g=g: E.dma_start(
                out=self.dbg["elat"][0:NLAT, 0 * 512 + g * 512:(g + 1) * 512],
                in_=self.elat_sb[0:64, 0, :]), self.s_vd, 16)
            P.w("scalar", self.s_ve, P.cur(self.s_ve))
            P.i("scalar", lambda E, g=g: E.dma_start(
                out=self.dbg["elat"][0:NLAT, LH // 2 * c.s_core + g * 512:
                                     LH // 2 * c.s_core + (g + 1) * 512],
                in_=self.elat_sb[0:64, 1, :]), self.s_vd, 16)
            P.w("scalar", self.s_ve, P.cur(self.s_ve))
            P.i("scalar", lambda E, g=g: E.dma_start(
                out=self.dbg["elat"][0:NLAT, 1 * c.s_core + g * 512:
                                     1 * c.s_core + (g + 1) * 512],
                in_=self.elat_sb[64:128, 0, :]), self.s_vd, 16)
            P.w("scalar", self.s_ve, P.cur(self.s_ve))
            P.i("scalar", lambda E, g=g: E.dma_start(
                out=self.dbg["elat"][0:NLAT, 3 * c.s_core + g * 512:
                                     3 * c.s_core + (g + 1) * 512],
                in_=self.elat_sb[64:128, 1, :]), self.s_vd, 16)

        P.wm("tensor", f"lmask_{g}")
        P.wm("tensor", f"av_done_{4 * g + 3}")
        P.wm("tensor", f"lu_ev_{4 * g + 3}_all")
        for st in range(4):
            if st >= 1:
                P.wm("tensor", f"latn_ev_{g}_{st - 1}")
            for hq in range(LH):
                pr, hp = hq // 2, hq % 2
                rb = hp * 64
                col = (hq % 2) * 512 + (hq // 2) * 130
                P.i("tensor", lambda E, st=st, hq=hq, pr=pr, rb=rb, col=col: E.matmul(
                    self.ps_c[:, col:col + 65],
                    self.elat_sb[rb:rb + 64, pr, st * 128:(st + 1) * 128],
                    self.rv_aug[rb:rb + 64, hq, :],
                    start=True, stop=True), self.s_te)
            P.mark(f"latav_{g}_{st}", self.s_te)
            P.wm("vector", f"latav_{g}_{st}")
            P.i("vector", lambda E: E.tensor_copy(
                out=self.dtmp[:, :],
                in_=AP(self.ps_c, 64, [[1024, 128], [512, 2], [130, 2], [1, 1]])),
                self.s_ve)
            P.selfwait("vector", self.s_ve)
            P.i("vector", lambda E: E.tensor_scalar_max(
                out=self.dtmp[:, :], in0=self.dtmp[:, :], scalar1=1e-30), self.s_ve)
            P.selfwait("vector", self.s_ve)
            P.i("vector", lambda E, st=st: E.reciprocal(
                out=self.dinv_sb[:, st, :], in_=self.dtmp[:, :]), self.s_ve)
            P.selfwait("vector", self.s_ve)
            P.i("vector", lambda E, st=st: E.tensor_tensor(
                out=self.latn_sb[:, st, :, :],
                in0=AP(self.ps_c, 0, [[1024, 128], [512, 2], [130, 2], [1, LHD]]),
                in1=AP(self.dinv_sb, st * LH, [[4 * LH, 128], [1, LH], [0, LHD]]),
                op=ALU.mult), self.s_ve)
            P.mark(f"latn_ev_{g}_{st}", self.s_ve)
        jof = {0: 0, 1: 2, 2: 1, 3: 3}
        for cc in range(2):
            P.wm("tensor", f"latn_ev_{g}_3")
            if g >= 1:
                P.wm("tensor", f"mixT_ev_{4 * g - 1}")
            else:
                P.wm("tensor", "w8t_ev_7")
            if cc == 1:
                P.wm("tensor", f"latfm_{g}_0")
            for hp in range(2):
                hq = 2 * cc + hp
                j = jof[hq]
                for st in range(4):
                    P.i("tensor", lambda E, st=st, j=j, hp=hp: E.transpose(
                        self.ps_tr[hp * 64:hp * 64 + 64, st * 128:(st + 1) * 128],
                        self.latn_sb[:, st, j, :], self.ident[:, :]), self.s_te)
            P.mark(f"latT_{g}_{cc}", self.s_te)
            P.w("vector", self.s_te, P.cur(self.s_te))
            P.i("vector", lambda E, cc=cc: E.tensor_copy(
                out=self.latfm_sb[:, cc, :], in_=self.ps_tr[:, 0:512]), self.s_ve)
            P.mark(f"latfm_{g}_{cc}", self.s_ve)
        if self.cfg.dbg:
            P.selfwait("vector", self.s_ve)
            P.w("scalar", self.s_ve, P.cur(self.s_ve))
            P.i("scalar", lambda E, g=g: E.dma_start(
                out=AP(self.dbg["latn"].tensor, g * 512 * LD,
                       [[LD, 128], [128 * LD, 4], [1, LD]]),
                in_=AP(self.latn_sb, 0, [[4 * LH * LHD, 128], [LH * LHD, 4],
                                         [1, LH * LHD]])), self.s_vd, 16)

        P.wm("tensor", "wdma2")
        for st in range(4):
            P.wm("tensor", f"latfm_{g}_1")
            if st >= 1:
                P.wm("tensor", f"rem_ev_{g}_{st - 1}")
                P.wm("tensor", f"rem_ev_v_{g}_{st - 1}")
            P.i("tensor", lambda E: E.matmul(
                self.ps_b[:, 0:512], self.ones1[0:1, :], self.routb_row[0:1, 0:512],
                start=True, stop=False), self.s_te)
            P.i("tensor", lambda E: E.matmul(
                self.ps_b[:, 512:1024], self.ones1[0:1, :],
                self.routb_row[0:1, 512:1024], start=True, stop=False), self.s_te)
            P.i("tensor", lambda E: E.matmul(
                self.ps_s[:, 0:H], self.ones1[0:1, :], self.routb_row[0:1, 1024:1040],
                start=True, stop=False), self.s_te)
            for cc in range(2):
                last = (cc == 1)
                P.i("tensor", lambda E, st=st, cc=cc, last=last: E.matmul(
                    self.ps_b[:, 0:512],
                    self.latfm_sb[:, cc, st * 128:(st + 1) * 128],
                    self.rout_wb[:, cc, 0:512], start=False, stop=last), self.s_te)
                P.i("tensor", lambda E, st=st, cc=cc, last=last: E.matmul(
                    self.ps_b[:, 512:1024],
                    self.latfm_sb[:, cc, st * 128:(st + 1) * 128],
                    self.rout_wb[:, cc, 512:1024], start=False, stop=last), self.s_te)
                P.i("tensor", lambda E, st=st, cc=cc, last=last: E.matmul(
                    self.ps_s[:, 0:H],
                    self.latfm_sb[:, cc, st * 128:(st + 1) * 128],
                    self.rout_wb[:, cc, 1024:1040], start=False, stop=last), self.s_te)
            P.mark(f"rout_mm_{g}_{st}", self.s_te)
            P.w("scalar", self.s_te, P.cur(self.s_te))
            P.i("scalar", lambda E, st=st: E.activation(
                out=self.remote_sb[:, st, 0:512], in_=self.ps_b[:, 0:512],
                func=AF.Copy), self.s_sc)
            P.i("scalar", lambda E, st=st: E.activation(
                out=self.remote_sb[:, st, 512:1024], in_=self.ps_b[:, 512:1024],
                func=AF.Copy), self.s_sc)
            P.w("vector", self.s_te, P.cur(self.s_te))
            P.i("vector", lambda E, st=st: E.tensor_copy(
                out=self.rgr_sb[:, st, :], in_=self.ps_s[:, 0:H]), self.s_ve)
            P.mark(f"rem_ev_s_{g}_{st}", self.s_sc)
            P.mark(f"rem_ev_v_{g}_{st}", self.s_ve)
            P.mark(f"rem_ev_{g}_{st}", self.s_sc)
        P.mark(f"latent_{g}", self.s_sc)
        if self.cfg.dbg:
            P.selfwait("scalar", self.s_sc)
            P.i("scalar", lambda E, g=g: E.dma_start(
                out=AP(self.dbg["remote"].tensor, g * 512 * D,
                       [[D, 128], [128 * D, 4], [1, D]]),
                in_=AP(self.remote_sb, 0, [[4 * D, 128], [D, 4], [1, D]])),
                self.s_sd, 16)
            P.selfwait("vector", self.s_ve)
            P.w("scalar", self.s_ve, P.cur(self.s_ve))
            P.i("scalar", lambda E, g=g: E.dma_start(
                out=AP(self.dbg["rgr"].tensor, g * 512 * H,
                       [[H, 128], [128 * H, 4], [1, H]]),
                in_=AP(self.rgr_sb, 0, [[4 * H, 128], [H, 4], [1, H]])), self.s_vd, 16)

    def gate_outw_res(self, tq):
        P, c = self.P, self.cfg
        self.dbgq(*("vector", "scalar", "tensor"))
        g, t = tq // 4, tq % 4
        NT = c.ntiles
        P.wm("vector", f"rem_ev_s_{g}_{t}")
        P.wm("vector", f"rem_ev_v_{g}_{t}")
        P.wm("vector", f"lu_ev_{tq}_all")
        lu_base = t * H * 66
        luap = lambda off, inner: AP(self.lu_sb, lu_base + off,
                                     [[4 * H * 66, 128], [66, H], [1, inner]])
        P.i("vector", lambda E: E.tensor_scalar_max(
            out=self.gsc[:, 0, :], in0=luap(64, 1), scalar1=1e-30), self.s_ve)
        P.selfwait("vector", self.s_ve)
        P.i("vector", lambda E: E.reciprocal(out=self.gsc[:, 0, :],
                                             in_=self.gsc[:, 0, :]), self.s_ve)
        P.selfwait("vector", self.s_ve)
        P.i("vector", lambda E: E.tensor_tensor(
            out=self.gsc[:, 1, :], in0=luap(65, 1), in1=self.gsc[:, 0, :],
            op=ALU.mult), self.s_ve)
        P.selfwait("vector", self.s_ve)
        P.i("vector", lambda E, t=t: E.tensor_tensor(
            out=self.gsc[:, 1, :], in0=self.gsc[:, 1, :], in1=self.rgr_sb[:, t, :],
            op=ALU.add), self.s_ve)
        P.selfwait("vector", self.s_ve)
        P.w("scalar", self.s_ve, P.cur(self.s_ve))
        P.i("scalar", lambda E: E.activation(
            out=self.gsc[:, 2, :], in_=self.gsc[:, 1, :], func=AF.Exp,
            scale=-1.0), self.s_sc)
        P.w("vector", self.s_sc, P.cur(self.s_sc))
        P.i("vector", lambda E: E.tensor_scalar_add(
            out=self.gsc[:, 3, :], in0=self.gsc[:, 2, :], scalar1=1.0), self.s_ve)
        P.selfwait("vector", self.s_ve)
        P.i("vector", lambda E: E.reciprocal(out=self.gsc[:, 3, :],
                                             in_=self.gsc[:, 3, :]), self.s_ve)
        P.selfwait("vector", self.s_ve)
        P.i("vector", lambda E: E.tensor_tensor(
            out=self.gsc[:, 4, :], in0=self.gsc[:, 3, :], in1=self.gsc[:, 0, :],
            op=ALU.mult), self.s_ve)
        P.i("vector", lambda E: E.tensor_scalar(
            out=self.gsc[:, 5, :], in0=self.gsc[:, 3, :], scalar1=-1.0, scalar2=1.0,
            op0=ALU.mult, op1=ALU.add), self.s_ve)
        P.selfwait("vector", self.s_ve)
        msl = 0
        if tq >= 1:
            P.wm("vector", f"mixT_{tq - 1}")
        P.i("vector", lambda E, msl=msl: E.tensor_tensor(
            out=AP(self.mixed_sb, 0, [[D, 128], [HD, H], [1, HD]]),
            in0=luap(0, HD),
            in1=AP(self.gsc, 4 * H, [[6 * H, 128], [1, H], [0, HD]]),
            op=ALU.mult), self.s_ve)
        if tq >= 1:
            P.wm("vector", f"outw_{tq - 1}")
        P.i("vector", lambda E, t=t: E.tensor_tensor(
            out=AP(self.mixt_sb, 0, [[D, 128], [HD, H], [1, HD]]),
            in0=AP(self.remote_sb, t * D, [[4 * D, 128], [HD, H], [1, HD]]),
            in1=AP(self.gsc, 5 * H, [[6 * H, 128], [1, H], [0, HD]]),
            op=ALU.mult), self.s_ve)
        P.selfwait("vector", self.s_ve)
        P.i("vector", lambda E: E.tensor_tensor(
            out=self.mixed_sb[:, 0, :], in0=self.mixed_sb[:, 0, :],
            in1=self.mixt_sb[:, :], op=ALU.add), self.s_ve)
        P.mark(f"mixed_{tq}", self.s_ve)
        if self.cfg.dbg:
            P.selfwait("vector", self.s_ve)
            P.w("scalar", self.s_ve, P.cur(self.s_ve))
            P.i("scalar", lambda E, tq=tq: E.dma_start(
                out=self.dbg["mixed"][tq * 128:(tq + 1) * 128, :],
                in_=self.mixed_sb[:, 0, :]), self.s_vd, 16)

        P.wm("tensor", f"mixed_{tq}")
        if tq >= 1:
            P.wm("tensor", f"mixT_ev_{tq - 1}")
        for j in range(8):
            P.i("tensor", lambda E, j=j: E.transpose(
                self.ps_tr[:, j * 128:(j + 1) * 128],
                self.mixed_sb[:, 0, j * 128:(j + 1) * 128], self.ident[:, :]),
                self.s_te)
        P.mark(f"mixT_{tq}", self.s_te)
        P.w("vector", self.s_te, P.cur(self.s_te))
        P.i("vector", lambda E: E.tensor_copy(
            out=self.mixt_sb[:, :], in_=self.ps_tr[:, :]), self.s_ve)
        P.mark(f"mixT_ev_{tq}", self.s_ve)

        P.wm("tensor", f"mixT_ev_{tq}")
        P.wm("tensor", "wdma2")
        if t == 0:
            P.wm("tensor", f"rem_ev_{g}_3")
        else:
            P.wm("tensor", f"res1_{tq - 1}")
        for half in range(2):
            P.i("tensor", lambda E, half=half: E.matmul(
                self.ps_b[:, half * 512:(half + 1) * 512], self.ones1[0:1, :],
                self.outb_row[0:1, half * 512:(half + 1) * 512],
                start=True, stop=False), self.s_te)
            for dc in range(8):
                P.i("tensor", lambda E, dc=dc, half=half: E.matmul(
                    self.ps_b[:, half * 512:(half + 1) * 512],
                    self.mixt_sb[:, dc * 128:(dc + 1) * 128],
                    self.out_wb[:, dc, half * 512:(half + 1) * 512],
                    start=False, stop=(dc == 7)), self.s_te)
        P.mark(f"outw_{tq}", self.s_te)

        if tq >= 1:
            P.wm("sync", f"res1_{tq - 1}")
        P.i("sync", lambda E, tq=tq: E.dma_start(
            out=self.xr_sb[:, 0, :], in_=self.d_x[tq * 128:(tq + 1) * 128, :]),
            self.s_xr, 16)
        P.w("vector", self.s_xr, P.cur(self.s_xr))
        P.wm("vector", f"outw_{tq}")
        if tq >= 1:
            P.wm("vector", f"hdma_{tq - 1}")
        P.selfwait("vector", self.s_ve)
        P.i("vector", lambda E, tq=tq: E.tensor_tensor(
            out=self.h_sb[:, 0, :], in0=self.ps_b[:, :],
            in1=self.xr_sb[:, 0, :], op=ALU.add), self.s_ve)
        P.mark(f"res1_{tq}", self.s_ve)
        P.selfwait("vector", self.s_ve)
        P.w("scalar", self.s_ve, P.cur(self.s_ve))
        P.i("scalar", lambda E, tq=tq: E.dma_start(
            out=self.d_h_scratch[tq * 128:(tq + 1) * 128, :],
            in_=self.h_sb[:, 0, :]), self.s_hw, 16)
        P.mark(f"hdma_{tq}", self.s_hw)
        P.i("vector", lambda E: E.bn_stats(
            out=self.bn_buf[:, 0:6], in_=self.h_sb[:, 0, 0:512]), self.s_ve)
        P.i("vector", lambda E: E.bn_stats(
            out=self.bn_buf[:, 6:12], in_=self.h_sb[:, 0, 512:1024]), self.s_ve)
        P.selfwait("vector", self.s_ve)
        P.i("vector", lambda E: E.bn_aggr(out=self.mv_tmp[:, :],
                                          in_=self.bn_buf[:, :]), self.s_ve)
        P.selfwait("vector", self.s_ve)
        P.w("scalar", self.s_ve, P.cur(self.s_ve))
        P.i("scalar", lambda E, tq=tq: E.activation(
            out=self.mr2[:, tq, 0:1], in_=self.mv_tmp[:, 1:2], func=AF.Ln,
            bias=EPS), self.s_sc)
        P.selfwait("scalar", self.s_sc)
        P.i("scalar", lambda E, tq=tq: E.activation(
            out=self.mr2[:, tq, 0:1], in_=self.mr2[:, tq, 0:1], func=AF.Exp,
            scale=-0.5), self.s_sc)
        P.w("vector", self.s_sc, P.cur(self.s_sc))
        P.i("vector", lambda E, tq=tq: E.tensor_scalar(
            out=self.mr2[:, tq, 1:2], in0=self.mv_tmp[:, 0:1],
            scalar1=self.mr2[:, tq, 0:1], scalar2=-1.0,
            op0=ALU.mult, op1=ALU.mult), self.s_ve)
        P.selfwait("vector", self.s_ve)
        P.i("vector", lambda E, tq=tq: E.tensor_scalar(
            out=self.ps_c[:, :], in0=self.h_sb[:, 0, :],
            scalar1=self.mr2[:, tq, 0:1], scalar2=self.mr2[:, tq, 1:2],
            op0=ALU.mult, op1=ALU.add), self.s_ve)
        P.selfwait("vector", self.s_ve)
        P.i("vector", lambda E: E.reduce_max(
            out=self.rowmax_sb[:, 0:1], in_=self.ps_c[:, :], axis=AX.X,
            apply_absolute_value=True), self.s_ve)
        P.selfwait("vector", self.s_ve)
        P.i("vector", lambda E: E.tensor_tensor(
            out=self.amax_acc[:, :], in0=self.amax_acc[:, :],
            in1=self.rowmax_sb[:, 0:1], op=ALU.max), self.s_ve)
        P.mark(f"amax_{tq}", self.s_ve)
        if self.cfg.dbg:
            P.selfwait("vector", self.s_ve)
            P.w("scalar", self.s_ve, P.cur(self.s_ve))
            P.i("scalar", lambda E, tq=tq: E.dma_start(
                out=self.dbg["h"][tq * 128:(tq + 1) * 128, :],
                in_=self.h_sb[:, 0, :]), self.s_vd, 16)

    def barrier(self):
        P = self.P
        NT = self.cfg.ntiles
        P.wm("vector", f"amax_{NT - 1}")
        P.selfwait("vector", self.s_ve)
        P.w("scalar", self.s_ve, P.cur(self.s_ve))
        P.i("scalar", lambda E: E.dma_start(out=self.amax_row[0:1, :],
                                            in_=self.amax_acc[:, :]), self.s_stg, 16)
        P.w("vector", self.s_stg, P.cur(self.s_stg))
        P.i("vector", lambda E: E.reduce_max(
            out=self.scal[0:1, 0:1], in_=self.amax_row[0:1, :], axis=AX.X), self.s_ve)
        P.selfwait("vector", self.s_ve)
        P.i("vector", lambda E: E.memset(self.amax_row[0:1, :], 0.0), self.s_ve)
        P.selfwait("vector", self.s_ve)
        P.i("vector", lambda E: E.tensor_copy(out=self.amax_row[0:1, 0:1],
                                              in_=self.scal[0:1, 0:1]), self.s_ve)
        P.selfwait("vector", self.s_ve)
        P.w("scalar", self.s_ve, P.cur(self.s_ve))
        P.i("scalar", lambda E: E.dma_start(out=self.d_cca_in,
                                            in_=self.amax_row[0:1, :]), self.s_stg, 16)
        P.w("gpsimd", self.s_stg, P.cur(self.s_stg))
        P.i("gpsimd", lambda E: E.collective_compute(
            "AllReduce", ALU.max, ins=[self.d_cca_in], outs=[self.d_cca_out],
            replica_groups=RG), self.s_gp)
        P.selfwait("gpsimd", self.s_gp)
        P.i("gpsimd", lambda E: E.dma_start(out=self.scal[0:1, 0:1],
                                            in_=self.d_cca_out[0:1, 0:1]), self.s_gd2, 16)
        P.mark("amax_g", self.s_gd2)
        P.wm("vector", "amax_g")
        P.i("vector", lambda E: E.tensor_scalar_max(
            out=self.scal[0:1, 0:1], in0=self.scal[0:1, 0:1], scalar1=1e-12), self.s_ve)
        P.selfwait("vector", self.s_ve)
        P.i("vector", lambda E: E.reciprocal(out=self.scal[0:1, 5:6],
                                             in_=self.scal[0:1, 0:1]), self.s_ve)
        P.selfwait("vector", self.s_ve)
        P.i("vector", lambda E: E.tensor_scalar(
            out=self.scal[0:1, 5:6], in0=self.scal[0:1, 5:6], scalar1=FP8MAX,
            scalar2=FP8MAX, op0=ALU.mult, op1=ALU.min), self.s_ve)
        P.selfwait("vector", self.s_ve)
        P.i("vector", lambda E: E.tensor_scalar_mul(
            out=self.scal[0:1, 6:7], in0=self.scal[0:1, 5:6], scalar1=0.5), self.s_ve)
        P.i("vector", lambda E: E.tensor_tensor(
            out=self.scal[0:1, 7:8], in0=self.scal[0:1, 5:6], in1=self.scal[0:1, 3:4],
            op=ALU.mult), self.s_ve)
        P.selfwait("vector", self.s_ve)
        P.i("vector", lambda E: E.reciprocal(out=self.scal[0:1, 7:8],
                                             in_=self.scal[0:1, 7:8]), self.s_ve)
        P.selfwait("vector", self.s_ve)
        P.i("vector", lambda E: E.tensor_scalar_mul(
            out=self.scal[0:1, 7:8], in0=self.scal[0:1, 7:8], scalar1=4.0), self.s_ve)
        P.selfwait("vector", self.s_ve)
        self.bcast_col(lambda: self.scal[0:1, 6:7], lambda: self.sxh_col[:, :], "sxh2")
        self.bcast_col(lambda: self.scal[0:1, 7:8], lambda: self.sout_col[:, :], "sout2")
        P.i("vector", lambda E: E.tensor_tensor(
            out=self.scal[0:1, 1:2], in0=self.scal[0:1, 5:6], in1=self.scal[0:1, 3:4],
            op=ALU.mult), self.s_ve)
        P.selfwait("vector", self.s_ve)
        P.i("vector", lambda E: E.tensor_scalar_mul(
            out=self.scal[0:1, 1:2], in0=self.scal[0:1, 1:2], scalar1=0.25), self.s_ve)
        P.selfwait("vector", self.s_ve)
        P.i("vector", lambda E: E.tensor_scalar_mul(
            out=self.fbiaspre_row[0:1, :], in0=self.fbias_row[0:1, :],
            scalar1=self.scal[0:1, 1:2]), self.s_ve)
        P.mark("scales", self.s_ve)
        if self.cfg.dbg:
            P.selfwait("vector", self.s_ve)
            P.w("scalar", self.s_ve, P.cur(self.s_ve))
            P.i("scalar", lambda E: E.dma_start(out=self.dbg["scal"],
                                                in_=self.scal[0:1, :]), self.s_vd, 16)

    def p2_early_pre(self, u):
        P, c = self.P, self.cfg
        NT = c.ntiles
        s5, s2 = u % 5, u % 2
        if u >= 5:
            P.wm("sync", f"p2_res_{u - 5}")
        else:
            P.wm("sync", f"av_done_{NT - 1}")
            P.wm("sync", f"lu_ev_{NT - 1}_all")
        P.wm("sync", f"hdma_{u}")
        P.i("sync", lambda E, u=u, s5=s5: E.dma_start(
            out=self.h2_sb[:, s5, 0:512],
            in_=self.d_h_scratch[u * 128:(u + 1) * 128, 0:512]),
            self.s_h2[u % 2], 16)
        if u >= 5:
            P.wm("gpsimd", f"p2_res_{u - 5}")
        else:
            P.wm("gpsimd", f"av_done_{NT - 1}")
            P.wm("gpsimd", f"lu_ev_{NT - 1}_all")
        P.wm("gpsimd", f"hdma_{u}")
        P.i("gpsimd", lambda E, u=u, s5=s5: E.dma_start(
            out=self.h2_sb[:, s5, 512:1024],
            in_=self.d_h_scratch[u * 128:(u + 1) * 128, 512:1024]),
            self.s_gd2, 16)
        P.wm("vector", "scales")
        P.wm("vector", "sxh2")
        if u >= 5:
            P.wm("vector", f"p2_x8_{u - 5}")
        P.selfwait("vector", self.s_ve)
        P.i("vector", lambda E, u=u, s5=s5: E.tensor_scalar_mul(
            out=self.sc_tile[:, s5, 0:1], in0=self.mr2[:, u, 0:1],
            scalar1=self.sxh_col[:, 0:1]), self.s_ve)
        P.i("vector", lambda E, u=u, s5=s5: E.tensor_scalar_mul(
            out=self.sc_tile[:, s5, 1:2], in0=self.mr2[:, u, 1:2],
            scalar1=self.sxh_col[:, 0:1]), self.s_ve)
        P.mark(f"p2_sc_{u}", self.s_ve)
        P.w("scalar", self.s_ve, P.cur(self.s_ve))
        P.w("scalar", self.s_h2[u % 2], P.cur(self.s_h2[u % 2]))
        if u >= 2:
            P.wm("scalar", f"p2_up_{u - 2}")
        else:
            P.wm("scalar", f"mixed_{NT - 1}")
        P.w("scalar", self.s_gd2, P.cur(self.s_gd2))
        P.i("scalar", lambda E, s5=s5, s2=s2: E.activation(
            out=self.x8_sb[:, s2, :], in_=self.h2_sb[:, s5, :], func=AF.Identity,
            scale=self.sc_tile[:, s5, 0:1], bias=self.sc_tile[:, s5, 1:2]), self.s_sc)
        P.mark(f"p2_x8_{u}", self.s_sc)
        P.w("vector", self.s_sc, P.cur(self.s_sc))
        if u >= 2:
            P.wm("vector", f"p2_tr_{u - 2}")
        P.i("vector", lambda E, s2=s2: E.tensor_copy(
            out=self.x8b_sb[:, s2, :], in_=self.x8_sb[:, s2, :]), self.s_ve)
        P.mark(f"p2_up_{u}", self.s_ve)

    def p2_early_pe(self, u):
        P, c = self.P, self.cfg
        NT = c.ntiles
        s2, s4 = u % 2, u % 4
        P.wm("tensor", f"p2_up_{u}")
        if u >= 1:
            P.wm("tensor", f"p2_fm_{u - 1}")
        else:
            P.wm("tensor", f"mixT_ev_{NT - 1}")
        for j in range(8):
            P.i("tensor", lambda E, j=j, s2=s2: E.transpose(
                self.ps_tr[:, j * 128:(j + 1) * 128],
                self.x8b_sb[:, s2, j * 128:(j + 1) * 128], self.ident[:, :]),
                self.s_te)
        P.mark(f"p2_tr_{u}", self.s_te)
        P.w("vector", self.s_te, P.cur(self.s_te))
        if u == 0:
            P.wm("vector", f"av_done_{NT - 1}")
            P.wm("vector", f"b_ln_{NT}")
        if u >= 4:
            P.wm("vector", f"p2_mm_{u - 4}")
        P.i("vector", lambda E, s4=s4: E.tensor_copy(
            out=self.x8fm_sb[:, s4, :, :], in_=self.ps_tr[:, :]), self.s_ve)
        P.mark(f"p2_fm_{u}", self.s_ve)

    def p2_late(self, t):
        P, c = self.P, self.cfg
        NT = c.ntiles
        s4, s5, s2 = t % 4, t % 5, t % 2
        psf = self.ps_b if t % 2 == 0 else self.ps_c
        P.wm("tensor", f"p2_fm_{t}")
        if t == 0:
            P.wm("tensor", "w8t")
            P.wm("tensor", "sout2")
            P.wm("tensor", "scales")
            P.wm("tensor", f"res1_{NT - 1}")
        elif t == 1:
            P.wm("tensor", f"amax_{NT - 1}")
        if t >= 2:
            P.wm("tensor", f"p2_ffn_{t - 2}")
        for half in range(2):
            P.i("tensor", lambda E, half=half, psf=psf: E.matmul(
                psf[:, half * 512:(half + 1) * 512], self.ones1[0:1, :],
                self.fbiaspre_row[0:1, half * 512:(half + 1) * 512],
                start=True, stop=False), self.s_te)
            for dc in range(8):
                P.i("tensor", lambda E, dc=dc, half=half, psf=psf, s4=s4: E.matmul(
                    psf[:, half * 512:(half + 1) * 512],
                    self.x8fm_sb[:, s4, dc, :],
                    self.w8t[:, dc, half * 512:(half + 1) * 512],
                    start=False, stop=(dc == 7)), self.s_te)
        P.mark(f"p2_mm_{t}", self.s_te)
        P.w("scalar", self.s_te, P.cur(self.s_te))
        if t < 2:
            P.wm("scalar", "sout2")
            P.wm("scalar", f"mixed_{NT - 1}")
        else:
            P.wm("scalar", f"p2_res_{t - 2}")
        P.i("scalar", lambda E, s2=s2, psf=psf: E.activation(
            out=self.ffn_sb[:, s2, :], in_=psf[:, :], func=AF.Copy,
            scale=self.sout_col[:, 0:1]), self.s_sc)
        P.mark(f"p2_ffn_{t}", self.s_sc)
        P.w("vector", self.s_sc, P.cur(self.s_sc))
        if t >= 2:
            P.wm("vector", f"odma_{t - 2}")
            P.wm("vector", f"odmav_{t - 2}")
        P.selfwait("vector", self.s_ve)
        P.i("vector", lambda E, s5=s5, s2=s2: E.tensor_tensor(
            out=self.outt_sb[:, s2, :], in0=self.h2_sb[:, s5, :],
            in1=self.ffn_sb[:, s2, :], op=ALU.add), self.s_ve)
        P.mark(f"p2_res_{t}", self.s_ve)
        P.selfwait("vector", self.s_ve)
        P.w("scalar", self.s_ve, P.cur(self.s_ve))
        P.i("scalar", lambda E, t=t, s2=s2: E.dma_start(
            out=self.d_out[t * 128:(t + 1) * 128, 0:512],
            in_=self.outt_sb[:, s2, 0:512]), self.s_out[s2], 16)
        P.mark(f"odma_{t}", self.s_out[s2])
        P.w("gpsimd", self.s_ve, P.cur(self.s_ve))
        P.i("gpsimd", lambda E, t=t, s2=s2: E.dma_start(
            out=self.d_out[t * 128:(t + 1) * 128, 512:1024],
            in_=self.outt_sb[:, s2, 512:1024]), self.s_vd, 16)
        P.mark(f"odmav_{t}", self.s_vd)

    def build(self):
        c = self.cfg
        P = self.P
        NT = c.ntiles
        self.declare()
        self.prologue_early()
        self.p1a()
        self.wamax_finish()
        self.ag_start()
        self.prologue_late()
        P.i("vector", lambda E: E.memset(self.amax_acc[:, :], 0.0), self.s_ve)
        self.chain_tile(0, NT, pstr_wait="bias_cols")
        return self.main_loop()

    def kh_block(self):
        P, c = self.P, self.cfg
        P.wm("tensor", f"b_fm_0")
        P.wm("tensor", "v_4")
        for j in range(8):
            if j >= 2:
                P.wm("tensor", f"kh_{j - 2}")
            for dc in range(8):
                P.i("tensor", lambda E, j=j, dc=dc: E.matmul(
                    self.ps_a[:, (j % 2) * 512:(j % 2) * 512 + 128],
                    self.qkv_wb[:, dc, (8 + j) * 128:(9 + j) * 128],
                    self.lnfmh_sb[:, dc, :], start=(dc == 0), stop=(dc == 7)), self.s_te)
            P.w("scalar", self.s_te, P.cur(self.s_te))
            P.i("scalar", lambda E, j=j: E.activation(
                out=self.kh_sb[:, j, :], in_=self.ps_a[:, (j % 2) * 512:(j % 2) * 512 + 128],
                func=AF.Identity, bias=self.qkb_col[:, 8 + j:9 + j]), self.s_sc)
            P.mark(f"kh_{j}", self.s_sc)
        P.mark("kh", self.s_sc)
        if self.cfg.dbg:
            P.selfwait("scalar", self.s_sc)
            for j in range(8):
                P.i("scalar", lambda E, j=j: E.dma_start(
                    out=self.dbg["k"][j * 128:(j + 1) * 128, c.s_core:c.s_core + 128],
                    in_=self.kh_sb[:, j, :]), self.s_sd, 16)

    def main_loop(self):
        c = self.cfg
        P = self.P
        NT = c.ntiles
        for g in range(c.ngroups):
            if g == 0:
                self.chain_tile(1, 0)
                self.chain_tile(2, 1)
                self.ag_finish()
                self.chain_tile(3, 2, pstr_wait="pooled_fm",
                                pss_wait="rv_ps_done")
                self.chain_tile(4, 3)
                self.kh_block()
                for cc in range(8):
                    self.w8t_chunk(cc, pstr_wait="b_fm_4")
            else:
                for t in range(4):
                    self.chain_tile(4 * g + t + 1, 4 * g + t)
            gates = [] if g == 0 else [4 * (g - 1) + t for t in range(4)]
            self.qk_group(g, gates)
            self.rq_group(g)
            for t in range(4):
                self.local_qtile(4 * g + t)
            self.latent_group(g)
            P.selfwait("vector", self.s_ve)
            P.i("vector", lambda E: E.tensor_copy(
                out=self.kprev_sb[:, :, :],
                in_=AP(self.k_sb, 384, [[8 * 512, 128], [512, 8], [1, 128]])),
                self.s_ve)
            P.mark(f"kprev_{g}", self.s_ve)
        for t in range(4):
            self.gate_outw_res(4 * (c.ngroups - 1) + t)
        P.mark("hw_all", self.s_hw)
        self.barrier()
        self.p2_early_pre(0)
        self.p2_early_pre(1)
        for u in range(4):
            self.p2_early_pe(u)
            if u + 2 < 5:
                self.p2_early_pre(u + 2)
        for t in range(NT):
            self.p2_late(t)
            if t + 5 < NT:
                self.p2_early_pre(t + 5)
            if t + 4 < NT:
                self.p2_early_pe(t + 4)
        for s in [self.s_vd, self.s_sy, self.s_gd, self.s_gd2, self.s_sd,
                  self.s_ve, self.s_sc, self.s_te, self.s_gp, self.s_xa0,
                  self.s_xa1, self.s_xl0, self.s_xl1, self.s_xr, self.s_h20,
                  self.s_h21, self.s_hw, self.s_out0, self.s_out1, self.s_stg]:
            P.selfwait("gpsimd", s)
        with self.nc.Block() as block:
            P.emit(block)
        self.ctx.pop_all()
        return self.nc


def build(cfg: Cfg):
    return MK(cfg).build()


S_CORE = 4096
S_SEQ = 16384
B = 2


def _core_inputs(xb, wd, core):
    b, c = core // 4, core % 4
    xs = np.ascontiguousarray(xb[b, c * S_CORE:(c + 1) * S_CORE])
    if c == 0:
        xh = np.zeros((128, D), np.float32)
        hv = 0.0
    else:
        xh = np.ascontiguousarray(xb[b, c * S_CORE - 128:c * S_CORE])
        hv = 1.0
    lm = (np.arange(NLAT)[:, None] < (16 * c + np.arange(16))[None, :])
    m = {"x": xs, "xh": xh, "latmask": lm.astype(np.float32),
         "halo_valid": np.array([[hv]], np.float32)}
    m.update(wd)
    return m


def kernel(**inputs):
    f32 = np.float32
    xb = np.ascontiguousarray(np.asarray(inputs["L0_attn_x"], f32))
    wd = {
        "qkv_w": np.ascontiguousarray(np.asarray(inputs["L0_attn_qkv_w"], f32)),
        "qkv_b": np.ascontiguousarray(np.asarray(inputs["L0_attn_qkv_b"], f32)),
        "rq_w": np.ascontiguousarray(np.asarray(inputs["L0_attn_rq_w"], f32)),
        "rq_b": np.ascontiguousarray(np.asarray(inputs["L0_attn_rq_b"], f32)),
        "rkv_w": np.ascontiguousarray(np.asarray(inputs["L0_attn_rkv_w"], f32)),
        "rkv_b": np.ascontiguousarray(np.asarray(inputs["L0_attn_rkv_b"], f32)),
        "rout_w": np.ascontiguousarray(np.asarray(inputs["L0_attn_rout_w"], f32)),
        "rout_b": np.ascontiguousarray(np.asarray(inputs["L0_attn_rout_b"], f32)),
        "out_w": np.ascontiguousarray(np.asarray(inputs["L0_attn_out_w"], f32)),
        "out_b": np.ascontiguousarray(np.asarray(inputs["L0_attn_out_b"], f32)),
        "gl_w": np.ascontiguousarray(np.asarray(inputs["L0_attn_gl_w"], f32)),
        "gr_w": np.ascontiguousarray(np.asarray(inputs["L0_attn_gr_w"], f32)),
        "ffn_w": np.ascontiguousarray(np.asarray(inputs["L0_ffn_weight"], f32)),
        "ffn_b": np.ascontiguousarray(np.asarray(inputs["L0_ffn_bias"], f32)),
    }
    nc = build(Cfg(s_core=S_CORE, dbg=False))
    in_maps = [_core_inputs(xb, wd, core) for core in range(8)]
    from concourse.bass_utils import run_bass_kernel_spmd
    res = run_bass_kernel_spmd(nc, in_maps, core_ids=list(range(8)))
    out = np.empty((B, S_SEQ, D), f32)
    for core in range(8):
        b, c = core // 4, core % 4
        out[b, c * S_CORE:(c + 1) * S_CORE] = res.results[core]["out"]
    return out


# version 13
# speedup vs baseline: 1.0892x; 1.0892x over previous
from contextlib import ExitStack

import numpy as np
import concourse.bass as bass
import concourse.mybir as mybir

F32 = mybir.dt.float32
BF16 = mybir.dt.bfloat16
FP8 = mybir.dt.float8e4
AF = mybir.ActivationFunctionType
ALU = mybir.AluOpType
AX = mybir.AxisListType
AP = bass.AP

D = 1024
H = 16
HD = 64
LH = 4
LHD = 64
LD = 256
NLAT = 64
EPS = 1e-5
FP8MAX = 448.0
RG = [[0, 1, 2, 3, 4, 5, 6, 7]]


class Cfg:
    def __init__(self, s_core=4096, dbg=False):
        self.s_core = s_core
        self.s_seq = 4 * s_core
        self.chunk = self.s_seq // NLAT
        self.ntiles = s_core // 128
        self.ngroups = self.ntiles // 4
        self.ch_per_grp = 512 // self.chunk
        self.dbg = dbg
        assert self.chunk * NLAT == self.s_seq
        assert self.ntiles % 4 == 0


PARANOID = False


class Prog:
    ENGINES = ("sync", "gpsimd", "tensor", "vector", "scalar")

    def __init__(self):
        self.ops = []
        self.cnt = {}
        self.marks = {}
        self.all_sems = {}

    def _bump(self, sem, k):
        self.cnt[sem.name] = self.cnt.get(sem.name, 0) + k
        return self.cnt[sem.name]

    def i(self, eng, fn, sem=None, k=1):
        if PARANOID:
            snap = dict(self.cnt)
            self.ops.append((eng, "psnap", snap))
        self.ops.append((eng, "inst", (fn, sem, k)))
        if sem is not None:
            self.all_sems[sem.name] = sem
            return self._bump(sem, k)

    def w(self, eng, sem, val):
        if val > 0:
            self.ops.append((eng, "wait", (sem, val)))

    def mark(self, name, sem):
        self.marks[name] = (sem, self.cnt.get(sem.name, 0))

    def wm(self, eng, name):
        sem, val = self.marks[name]
        self.w(eng, sem, val)

    def cur(self, sem):
        return self.cnt.get(sem.name, 0)

    def selfwait(self, eng, sem):
        self.w(eng, sem, self.cur(sem))

    def emit(self, block):
        by_eng = {e: [] for e in self.ENGINES}
        for eng, kind, payload in self.ops:
            by_eng[eng].append((kind, payload))

        def run(E, ops):
            waited = {}
            for kind, payload in ops:
                if kind == "psnap":
                    for nm, val in payload.items():
                        if val > 0 and waited.get(nm, -1) < val:
                            E.wait_ge(self.all_sems[nm], val)
                            waited[nm] = val
                    continue
                if kind == "wait":
                    sem, val = payload
                    if waited.get(sem.name, -1) >= val:
                        continue
                    E.wait_ge(sem, val)
                    waited[sem.name] = val
                else:
                    fn, sem, k = payload
                    instr = fn(E)
                    if sem is not None:
                        instr.then_inc(sem, k)

        for eng in self.ENGINES:
            if by_eng[eng]:
                getattr(block, eng)(lambda E, ops=by_eng[eng]: run(E, ops))


class MK:
    def __init__(self, cfg: Cfg):
        self.cfg = cfg
        self.nc = bass.Bass("TRN2", target_bir_lowering=False,
                            dynamic_dma_scratch_size=16384)
        self.P = Prog()
        self.ctx = ExitStack()

    def bcast_col(self, cell_ap_fn, dst_col_fn, tag):
        P = self.P
        P.wm("tensor", "consts")
        P.w("tensor", self.s_ve, P.cur(self.s_ve))
        P.i("tensor", lambda E: E.matmul(
            self.ps_s[:, 30:31], self.ones1f[0:1, :], cell_ap_fn(),
            start=True, stop=True), self.s_te)
        P.w("vector", self.s_te, P.cur(self.s_te))
        P.i("vector", lambda E: E.tensor_copy(out=dst_col_fn(),
                                              in_=self.ps_s[:, 30:31]), self.s_ve)
        P.mark(tag, self.s_ve)

    def lp(self, f):
        with self.nc.allow_low_precision("bf16 gate-weight reduction"):
            return f()

    def sb(self, name, shape, dtype):
        t = self.ctx.enter_context(self.nc.sbuf_tensor(name, shape, dtype))
        setattr(self, name, t)
        return t

    def ps(self, name, shape, dtype):
        t = self.ctx.enter_context(self.nc.psum_tensor(name, shape, dtype))
        setattr(self, name, t)
        return t

    def sem(self, name):
        s = self.ctx.enter_context(self.nc.semaphore(name))
        setattr(self, name, s)
        return s

    def dram(self, name, shape, dtype=F32, kind="ExternalInput", **kw):
        ap = self.nc.dram_tensor(name, shape, dtype, kind=kind, **kw).ap()
        setattr(self, "d_" + name, ap)
        return ap

    def declare(self):
        c = self.cfg
        S, NT = c.s_core, c.ntiles
        self.dram("x", [S, D])
        self.dram("xh", [128, D])
        self.dram("qkv_w", [D, 3 * D]); self.dram("qkv_b", [3 * D])
        self.dram("rq_w", [D, LD]); self.dram("rq_b", [LD])
        self.dram("rkv_w", [D, 2 * LD]); self.dram("rkv_b", [2 * LD])
        self.dram("rout_w", [LD, D]); self.dram("rout_b", [D])
        self.dram("out_w", [D, D]); self.dram("out_b", [D])
        self.dram("gl_w", [HD, 1]); self.dram("gr_w", [HD, 1])
        self.dram("ffn_w", [D, D]); self.dram("ffn_b", [D])
        self.dram("latmask", [NLAT, 16])
        self.dram("halo_valid", [1, 1])
        self.dram("out", [S, D], kind="ExternalOutput")
        self.dram("h_scratch", [S, D], kind="Internal")
        self.d_ccp_in = self.nc.dram_tensor("ccp_in", [16, D], BF16, kind="Internal").ap()
        self.d_ccp_out = self.nc.dram_tensor("ccp_out", [128, D], BF16, kind="Internal", addr_space="Shared").ap()
        self.dram("cca_in", [1, 128], kind="Internal")
        self.dram("cca_out", [1, 128], kind="Internal", addr_space="Shared")
        self.dbg = {}
        if c.dbg:
            for nm, shape, dt_ in [
                ("ln1", [S + 128, D], BF16), ("q", [D, S], BF16),
                ("k", [D, S + 128], BF16), ("v", [S + 128, H * 66], BF16),
                ("lu", [S, H * 66], BF16), ("remote", [S, D], BF16),
                ("mixed", [S, D], BF16), ("h", [S, D], F32),
                ("scal", [1, 8], F32), ("rq", [LD, S], BF16),
                ("rk", [LD, NLAT], BF16), ("rv", [NLAT, LH * 65], BF16),
                ("latn", [S, LD], BF16), ("pooled", [128, D], BF16),
                ("x8", [S, D], BF16), ("rgr", [S, H], F32),
                ("ffn", [S, D], BF16), ("elat", [NLAT, LH * S], BF16),
                ("probe", [128, 2048], F32),
            ]:
                self.dbg[nm] = self.nc.dram_tensor(
                    "dbg_" + nm, shape, dt_, kind="ExternalOutput").ap()

        NTt = NT
        self.sb("qkv_wb", [128, 8, 3 * D], BF16)
        self.sb("rq_wb", [128, 8, LD], BF16)
        self.sb("rkv_wb", [128, 8, 2 * LD], BF16)
        self.sb("rout_wb", [128, 2, 1040], BF16)
        self.sb("out_wb", [128, 8, D], BF16)
        self.sb("wvgl", [128, 8, H], BF16)
        self.sb("w8t", [128, 8, D], BF16)
        self.sb("qkb_col", [128, 16], F32)
        self.sb("rqb_col", [128, 2], F32)
        self.sb("rkb_col", [128, 2], F32)
        self.sb("vb_row", [1, D], BF16)
        self.sb("rvb_row", [1, LD], BF16)
        self.sb("rqb_row", [1, LD], BF16)
        self.sb("rkb_row", [1, LD], BF16)
        self.sb("routb_row", [1, 1040], BF16)
        self.sb("outb_row", [1, D], BF16)
        self.sb("bgl_row", [1, H], BF16)
        self.sb("fbias_row", [1, D], BF16)
        self.sb("glrep", [128, HD], BF16)
        self.sb("pid_sb", [1, 2], mybir.dt.uint32)
        self.sb("grrep", [128, HD], BF16)
        self.sb("glrow", [1, HD], BF16)
        self.sb("grrow", [1, HD], BF16)
        self.sb("maskAB", [128, 512], BF16)
        self.sb("maskAB0", [128, 512], BF16)
        self.sb("a_all", [128, NTt, 16], BF16)
        self.sb("latmask_sb", [128, 16], BF16)
        self.sb("ident", [128, 128], BF16)
        self.sb("ones1", [1, 128], BF16)
        self.sb("ones1f", [1, 128], F32)
        self.sb("cpool", [128, 8], F32)
        cvals = [0.0, 1.0, EPS, -0.5, 0.125, -1.0]
        self.cvals = cvals
        for j, v in enumerate(cvals):
            self.nc.const_aps.register(F32, v, self.cpool[:, j:j + 1])
        self.sb("hval_col", [128, 1], F32)
        self.sb("mr1", [128, NTt + 1, 2], F32)
        self.sb("mr2", [128, NTt, 2], F32)
        self.sb("amax_acc", [128, 1], F32)
        self.sb("wamax_col", [128, 1], F32)
        self.sb("scal", [1, 8], F32)
        self.sb("sxh_col", [128, 1], F32)
        self.sb("sout_col", [128, 1], F32)
        self.sb("sc_tile", [128, 5, 2], F32)
        self.sb("mv_tmp", [128, 2], F32)
        self.sb("bn_buf", [128, 12], F32)
        self.sb("rowmax_sb", [128, 1], F32)
        regP = self.nc.alloc_sbuf_tensor("regP", [128, 2048], mybir.dt.uint8)
        baseP = self.nc.lookup_mloc(regP).addr
        self.pooled_all = self.nc.alloc_sbuf_tensor_at(
            "pooled_all", [64, D], BF16, offset=baseP)
        self.pool_sb = self.nc.alloc_sbuf_tensor_at(
            "pool_sb", [16, D], BF16, offset=baseP)
        self.sb("pooled_fm", [128, 8, NLAT], BF16)
        self.sb("rk_fm", [128, 2, NLAT], BF16)
        self.sb("rv_aug", [128, LH, 65], BF16)
        regA = self.nc.alloc_sbuf_tensor("regA", [128, 16640], mybir.dt.uint8)
        baseA = self.nc.lookup_mloc(regA).addr

        def at(name, shape, dtype, off):
            t = self.nc.alloc_sbuf_tensor_at(name, shape, dtype, offset=off)
            setattr(self, name, t)
            return t

        at("xa_sb", [128, 2, D], F32, baseA)
        at("ln1a_sb", [128, 3, D], BF16, baseA + 8192)
        at("lu_sb", [128, 4, H, 66], BF16, baseA)
        at("remote_sb", [128, 4, D], BF16, baseA + 8448)
        at("outt_sb", [128, 2, D], F32, baseA)
        at("ffn_sb", [128, 2, D], BF16, baseA + 8192)
        at("x8_sb", [128, 2, D], FP8, baseA + 12288)
        regB = self.nc.alloc_sbuf_tensor("regB", [128, 24576], mybir.dt.uint8)
        baseB = self.nc.lookup_mloc(regB).addr
        at("qkvb_row", [1, 3 * D], BF16, baseB)
        at("tmp_row", [1, D], BF16, baseB + 6144)
        at("wstage", [128, D], F32, baseB + 8192)
        at("wtmpb", [128, D], BF16, baseB + 12288)
        at("wtmp8", [128, D], FP8, baseB + 14336)
        self.sb("amax_row", [1, 128], F32)
        at("q_sb", [128, 8, 512], BF16, baseB)
        at("k_sb", [128, 8, 512], BF16, baseB + 8192)
        at("e_sb", [128, 1, 4096], BF16, baseB + 16384)
        at("h2_sb", [128, 5, D], F32, baseB)
        regC = self.nc.alloc_sbuf_tensor("regC", [128, 4096], mybir.dt.uint8)
        baseC = self.nc.lookup_mloc(regC).addr
        at("lnfmh_sb", [128, 8, 128], BF16, baseC)
        at("kh_sb", [128, 8, 128], BF16, baseC + 2048)
        at("mixed_sb", [128, 1, D], BF16, baseC)
        at("mixt_sb", [128, D], BF16, baseC + 2048)
        at("x8b_sb", [128, 2, D], BF16, baseC)
        regD = self.nc.alloc_sbuf_tensor("regD", [128, 16384], mybir.dt.uint8)
        baseD = self.nc.lookup_mloc(regD).addr
        at("xl_sb", [128, 2, D], F32, baseD)
        at("xr_sb", [128, 1, D], F32, baseD + 8192)
        at("h_sb", [128, 1, D], F32, baseD + 12288)
        at("x8fm_sb", [128, 4, 8, 128], BF16, baseD)
        self.sb("ln1b_sb", [128, 2, D], BF16)
        self.sb("lnfm_sb", [128, 8, 512], BF16)
        self.sb("kprev_sb", [128, 8, 128], BF16)
        self.sb("rq_sb", [128, 2, 512], BF16)
        self.sb("v_ring", [128, 5, H, 66], BF16)
        self.sb("elat_sb", [128, 2, 512], BF16)
        self.sb("latn_sb", [128, 4, LH, LHD], BF16)
        at("latfm_sb", [128, 2, 512], BF16, baseP)
        at("fbiaspre_row", [1, D], BF16, baseA + 14336)
        self.sb("dinv_sb", [128, 4, LH], F32)
        self.sb("dtmp", [128, 4], F32)
        self.sb("gsc", [128, 6, H], F32)
        self.sb("rgr_sb", [128, 4, H], F32)

        self.ps("ps_tr", [128, 1024], BF16)
        self.ps("ps_a", [128, 1024], F32)
        self.ps("ps_b", [128, 1024], F32)
        self.ps("ps_c", [128, 1024], F32)
        self.ps("ps_s", [128, 512], F32)

        for s in ["s_sy", "s_gd", "s_gd2", "s_gd3", "s_gp", "s_te", "s_sc", "s_ve",
                  "s_vd", "s_sd", "s_xa0", "s_xa1", "s_xl0", "s_xl1", "s_xr",
                  "s_h20", "s_h21", "s_hw", "s_out0", "s_out1", "s_stg"]:
            self.sem(s)
        self.s_xa = [self.s_xa0, self.s_xa1]
        self.s_xl = [self.s_xl0, self.s_xl1]
        self.s_h2 = [self.s_h20, self.s_h21]
        self.s_out = [self.s_out0, self.s_out1]

    def prologue_early(self):
        P, c = self.P, self.cfg
        nc = self.nc
        CH = c.chunk
        NT = c.ntiles

        from concourse.hw_specs import get_activation_tables
        tabs = list(get_activation_tables(nc.m.arch).keys())
        set_id = tabs.index("natural_log_exp_and_others")
        P.i("scalar", lambda E: E.add_instruction(mybir.InstLoadActFuncSet(
            name=nc.get_next_instruction_name(), act_func_set_id=set_id,
            ins=[], outs=[])))

        for cc in range(8):
            P.i("gpsimd", lambda E, cc=cc: E.dma_start(
                out=self.qkv_wb[:, cc, :], in_=self.d_qkv_w[cc * 128:(cc + 1) * 128, :]), self.s_gd, 16)
        for cc in range(8):
            P.i("gpsimd", lambda E, cc=cc: E.dma_start(
                out=self.rq_wb[:, cc, :], in_=self.d_rq_w[cc * 128:(cc + 1) * 128, :]), self.s_gd, 16)
            P.i("gpsimd", lambda E, cc=cc: E.dma_start(
                out=self.rkv_wb[:, cc, :], in_=self.d_rkv_w[cc * 128:(cc + 1) * 128, :]), self.s_gd, 16)
        rows = [
            (self.qkvb_row[0:1, :], AP(self.d_qkv_b.tensor, 0, [[3 * D, 1], [1, 3 * D]])),
            (self.rqb_row[0:1, :], AP(self.d_rq_b.tensor, 0, [[LD, 1], [1, LD]])),
            (self.rkb_row[0:1, :], AP(self.d_rkv_b.tensor, 0, [[LD, 1], [1, LD]])),
            (self.rvb_row[0:1, :], AP(self.d_rkv_b.tensor, LD, [[LD, 1], [1, LD]])),
            (self.routb_row[0:1, 0:D], AP(self.d_rout_b.tensor, 0, [[D, 1], [1, D]])),
            (self.outb_row[0:1, :], AP(self.d_out_b.tensor, 0, [[D, 1], [1, D]])),
            (self.fbias_row[0:1, :], AP(self.d_ffn_b.tensor, 0, [[D, 1], [1, D]])),
            (self.glrow[0:1, :], AP(self.d_gl_w.tensor, 0, [[HD, 1], [1, HD]])),
            (self.grrow[0:1, :], AP(self.d_gr_w.tensor, 0, [[HD, 1], [1, HD]])),
            (self.glrep[:, :], AP(self.d_gl_w.tensor, 0, [[0, 128], [1, HD]])),
            (self.grrep[:, :], AP(self.d_gr_w.tensor, 0, [[0, 128], [1, HD]])),
            (self.latmask_sb[0:64, :], self.d_latmask),
            (self.latmask_sb[64:128, :], self.d_latmask),
            (self.hval_col[:, :], AP(self.d_halo_valid.tensor, 0, [[0, 128], [1, 1]])),
        ]
        for out, in_ in rows:
            P.i("gpsimd", lambda E, o=out, i=in_: E.dma_start(
                out=o, in_=i, allow_slow_non_contiguous=True), self.s_gd, 16)
        P.mark("wdma", self.s_gd)

        P.i("gpsimd", lambda E: E.memset(self.ident[:, :], 1.0), self.s_gp)
        P.i("gpsimd", lambda E: E.memset(self.ones1[0:1, :], 1.0), self.s_gp)
        P.i("gpsimd", lambda E: E.memset(self.ones1f[0:1, :], 1.0), self.s_gp)
        for j, v in enumerate(self.cvals):
            P.i("gpsimd", lambda E, j=j, v=v: E.memset(self.cpool[:, j:j + 1], v),
                self.s_gp)
        P.i("gpsimd", lambda E: E.memset(self.maskAB[:, :], 1.0), self.s_gp)
        P.i("gpsimd", lambda E: E.memset(self.a_all[:, :, :], 1.0 / CH), self.s_gp)
        P.selfwait("gpsimd", self.s_gp)
        P.i("gpsimd", lambda E: E.affine_select(
            out=self.ident[:, :], in_=self.ident[:, :], compare_op=ALU.is_ge,
            fill=0.0, base=0, pattern=[[-1, 128]], channel_multiplier=1), self.s_gp)
        P.selfwait("gpsimd", self.s_gp)
        P.i("gpsimd", lambda E: E.affine_select(
            out=self.ident[:, :], in_=self.ident[:, :], compare_op=ALU.is_ge,
            fill=0.0, base=0, pattern=[[1, 128]], channel_multiplier=-1), self.s_gp)
        P.i("gpsimd", lambda E: E.affine_select(
            out=AP(self.maskAB, 0, [[512, 128], [256, 2], [1, 128]]),
            in_=AP(self.maskAB, 0, [[512, 128], [256, 2], [1, 128]]),
            compare_op=ALU.is_ge, fill=0.0, base=0,
            pattern=[[0, 2], [-1, 128]], channel_multiplier=1), self.s_gp)
        P.selfwait("gpsimd", self.s_gp)
        P.i("gpsimd", lambda E: E.affine_select(
            out=AP(self.maskAB, 128, [[512, 128], [256, 2], [1, 128]]),
            in_=AP(self.maskAB, 128, [[512, 128], [256, 2], [1, 128]]),
            compare_op=ALU.is_gt, fill=0.0, base=0,
            pattern=[[0, 2], [1, 128]], channel_multiplier=-1), self.s_gp)
        P.i("gpsimd", lambda E: E.affine_select(
            out=self.a_all[:, :, :], in_=self.a_all[:, :, :], compare_op=ALU.is_ge,
            fill=0.0, base=0, pattern=[[128, NT], [-CH, 16]], channel_multiplier=1),
            self.s_gp)
        P.selfwait("gpsimd", self.s_gp)
        P.i("gpsimd", lambda E: E.affine_select(
            out=self.a_all[:, :, :], in_=self.a_all[:, :, :], compare_op=ALU.is_ge,
            fill=0.0, base=(CH - 1), pattern=[[-128, NT], [CH, 16]],
            channel_multiplier=-1), self.s_gp)
        P.mark("consts", self.s_gp)
        P.wm("scalar", "consts")

    def wamax_start(self):
        P = self.P
        P.i("vector", lambda E: E.memset(self.wamax_col[:, :], 0.0), self.s_ve)
        for cc in range(2):
            P.i("gpsimd", lambda E, cc=cc: E.dma_start(
                out=self.wam_buf(cc), in_=self.d_ffn_w[cc * 128:(cc + 1) * 128, :]),
                self.s_gd2, 16)
            P.mark(f"wam_d_{cc}", self.s_gd2)

    def wam_buf(self, cc):
        return self.wstage[:, :] if cc % 2 == 0 else self.xl_sb[:, 0, :]

    def wamax_reduce(self, cc):
        P = self.P
        P.w("vector", self.s_gd2, P.cur(self.s_gd2))
        P.selfwait("vector", self.s_ve)
        P.i("vector", lambda E, cc=cc: E.reduce_max(
            out=self.rowmax_sb[:, 0:1], in_=self.wam_buf(cc), axis=AX.X,
            apply_absolute_value=True), self.s_ve)
        P.selfwait("vector", self.s_ve)
        P.i("vector", lambda E: E.tensor_max(
            out=self.wamax_col[:, :], in0=self.wamax_col[:, :],
            in1=self.rowmax_sb[:, 0:1]), self.s_ve)
        P.mark(f"wam_rd_{cc}", self.s_ve)
        if cc + 2 < 8:
            P.wm("gpsimd", f"wam_rd_{cc}")
            P.i("gpsimd", lambda E, cc=cc: E.dma_start(
                out=self.wam_buf(cc + 2),
                in_=self.d_ffn_w[(cc + 2) * 128:(cc + 3) * 128, :]), self.s_gd2, 16)
            P.mark(f"wam_d_{cc + 2}", self.s_gd2)

    def wamax_finish(self):
        P = self.P
        P.selfwait("vector", self.s_ve)
        P.w("scalar", self.s_ve, P.cur(self.s_ve))
        P.i("scalar", lambda E: E.dma_start(out=self.amax_row[0:1, :],
                                            in_=self.wamax_col[:, :]), self.s_stg, 16)
        P.w("vector", self.s_stg, P.cur(self.s_stg))
        P.i("vector", lambda E: E.reduce_max(
            out=self.scal[0:1, 2:3], in_=self.amax_row[0:1, :], axis=AX.X), self.s_ve)
        P.selfwait("vector", self.s_ve)
        P.i("vector", lambda E: E.tensor_scalar_max(
            out=self.scal[0:1, 2:3], in0=self.scal[0:1, 2:3], scalar1=1e-12), self.s_ve)
        P.selfwait("vector", self.s_ve)
        P.i("vector", lambda E: E.reciprocal(out=self.scal[0:1, 3:4],
                                             in_=self.scal[0:1, 2:3]), self.s_ve)
        P.selfwait("vector", self.s_ve)
        P.i("vector", lambda E: E.tensor_scalar(
            out=self.scal[0:1, 3:4], in0=self.scal[0:1, 3:4], scalar1=FP8MAX,
            scalar2=FP8MAX, op0=ALU.mult, op1=ALU.min), self.s_ve)
        P.selfwait("vector", self.s_ve)
        P.i("vector", lambda E: E.tensor_scalar_mul(
            out=self.scal[0:1, 4:5], in0=self.scal[0:1, 3:4], scalar1=0.5), self.s_ve)
        P.selfwait("vector", self.s_ve)
        self.bcast_col(lambda: self.scal[0:1, 4:5], lambda: self.sxh_col[:, :], "swh")
        for cc in range(8):
            P.i("gpsimd", lambda E, cc=cc: E.dma_start(
                out=self.out_wb[:, cc, :], in_=self.d_out_w[cc * 128:(cc + 1) * 128, :]), self.s_gd3, 16)
        for cc in range(2):
            P.i("gpsimd", lambda E, cc=cc: E.dma_start(
                out=self.rout_wb[:, cc, 0:D], in_=self.d_rout_w[cc * 128:(cc + 1) * 128, :]), self.s_gd3, 16)
        P.mark("wdma2", self.s_gd3)

    def prologue_late(self):
        P, c = self.P, self.cfg
        NT = c.ntiles

        P.wm("vector", "consts")
        P.wm("vector", "wdma")
        P.i("vector", lambda E: E.tensor_copy(out=self.maskAB0[:, :], in_=self.maskAB[:, :]), self.s_ve)
        P.selfwait("vector", self.s_ve)
        P.i("vector", lambda E: E.tensor_scalar_mul(
            out=AP(self.maskAB0, 0, [[512, 128], [256, 2], [1, 128]]),
            in0=AP(self.maskAB0, 0, [[512, 128], [256, 2], [1, 128]]),
            scalar1=self.hval_col[:, 0:1]), self.s_ve)

        for cc in range(8):
            P.selfwait("vector", self.s_ve)
            P.i("vector", lambda E, cc=cc: E.tensor_tensor(
                out=AP(self.wtmpb, 0, [[D, 128], [HD, H], [1, HD]]),
                in0=AP(self.qkv_wb, cc * 3 * D + 2 * D,
                       [[8 * 3 * D, 128], [HD, H], [1, HD]]),
                in1=AP(self.glrep, 0, [[HD, 128], [0, H], [1, HD]]),
                op=ALU.mult), self.s_ve)
            P.selfwait("vector", self.s_ve)
            P.i("vector", lambda E, cc=cc: self.lp(lambda: E.reduce_sum(
                out=self.wvgl[:, cc, :],
                in_=AP(self.wtmpb, 0, [[D, 128], [HD, H], [1, HD]]),
                axis=AX.X)), self.s_ve)
        P.wm("vector", "wdma2")
        for cc in range(2):
            P.selfwait("vector", self.s_ve)
            P.i("vector", lambda E, cc=cc: E.tensor_tensor(
                out=AP(self.wtmpb, 0, [[D, 128], [HD, H], [1, HD]]),
                in0=AP(self.rout_wb, cc * 1040, [[2 * 1040, 128], [HD, H], [1, HD]]),
                in1=AP(self.grrep, 0, [[HD, 128], [0, H], [1, HD]]),
                op=ALU.mult), self.s_ve)
            P.selfwait("vector", self.s_ve)
            P.i("vector", lambda E, cc=cc: self.lp(lambda: E.reduce_sum(
                out=self.rout_wb[:, cc, D:D + H],
                in_=AP(self.wtmpb, 0, [[D, 128], [HD, H], [1, HD]]),
                axis=AX.X)), self.s_ve)
        P.i("vector", lambda E: E.tensor_tensor(
            out=AP(self.tmp_row, 0, [[D, 1], [HD, H], [1, HD]]),
            in0=AP(self.qkvb_row, 2 * D, [[3 * D, 1], [HD, H], [1, HD]]),
            in1=AP(self.glrow, 0, [[HD, 1], [0, H], [1, HD]]), op=ALU.mult), self.s_ve)
        P.selfwait("vector", self.s_ve)
        P.i("vector", lambda E: self.lp(lambda: E.reduce_sum(
            out=self.bgl_row[0:1, :], in_=AP(self.tmp_row, 0, [[D, 1], [HD, H], [1, HD]]),
            axis=AX.X)), self.s_ve)
        P.selfwait("vector", self.s_ve)
        P.i("vector", lambda E: E.tensor_tensor(
            out=AP(self.tmp_row, 0, [[D, 1], [HD, H], [1, HD]]),
            in0=AP(self.routb_row, 0, [[1040, 1], [HD, H], [1, HD]]),
            in1=AP(self.grrow, 0, [[HD, 1], [0, H], [1, HD]]), op=ALU.mult), self.s_ve)
        P.selfwait("vector", self.s_ve)
        P.i("vector", lambda E: self.lp(lambda: E.reduce_sum(
            out=self.routb_row[0:1, D:D + H],
            in_=AP(self.tmp_row, 0, [[D, 1], [HD, H], [1, HD]]), axis=AX.X)), self.s_ve)
        P.i("vector", lambda E: E.tensor_copy(
            out=self.vb_row[0:1, :], in_=self.qkvb_row[0:1, 2 * D:3 * D]), self.s_ve)
        for sl in range(5):
            P.i("vector", lambda E, sl=sl: E.memset(
                AP(self.v_ring, sl * H * 66 + 64, [[5 * H * 66, 128], [66, H], [1, 1]]),
                1.0), self.s_ve)
        P.i("vector", lambda E: E.memset(
            AP(self.rv_aug, 64, [[LH * 65, 128], [65, LH], [1, 1]]), 1.0), self.s_ve)
        P.mark("prologue_v", self.s_ve)

        P.wm("tensor", "wdma")
        P.wm("tensor", "consts")
        for j in range(16):
            P.i("tensor", lambda E, j=j: E.transpose(
                self.ps_tr[:, 2 * j:2 * j + 1], self.qkvb_row[0:1, j * 128:(j + 1) * 128],
                self.ones1[0:1, 0:1]), self.s_te)
        for j in range(2):
            P.i("tensor", lambda E, j=j: E.transpose(
                self.ps_tr[:, 32 + 2 * j:33 + 2 * j],
                self.rqb_row[0:1, j * 128:(j + 1) * 128],
                self.ones1[0:1, 0:1]), self.s_te)
            P.i("tensor", lambda E, j=j: E.transpose(
                self.ps_tr[:, 36 + 2 * j:37 + 2 * j],
                self.rkb_row[0:1, j * 128:(j + 1) * 128],
                self.ones1[0:1, 0:1]), self.s_te)
        P.w("vector", self.s_te, P.cur(self.s_te))
        P.i("vector", lambda E: E.tensor_copy(
            out=self.qkb_col[:, :], in_=AP(self.ps_tr, 0, [[1024, 128], [2, 16]])),
            self.s_ve)
        P.i("vector", lambda E: E.tensor_copy(
            out=self.rqb_col[:, :], in_=AP(self.ps_tr, 32, [[1024, 128], [2, 2]])),
            self.s_ve)
        P.i("vector", lambda E: E.tensor_copy(
            out=self.rkb_col[:, :], in_=AP(self.ps_tr, 36, [[1024, 128], [2, 2]])),
            self.s_ve)
        P.mark("bias_cols", self.s_ve)

    def w8t_chunk(self, cc, pstr_wait="bias_cols"):
        P = self.P
        if cc == 0:
            P.wm("sync", "wam_rd_6")
        else:
            P.wm("sync", f"w8q_{cc - 1}")
        P.i("sync", lambda E, cc=cc: E.dma_start(
            out=self.wstage[:, :], in_=self.d_ffn_w[cc * 128:(cc + 1) * 128, :]), self.s_sy, 16)
        P.mark(f"w8d_{cc}", self.s_sy)
        P.wm("scalar", f"w8d_{cc}")
        P.wm("scalar", "swh")
        if cc > 0:
            P.wm("scalar", f"w8c_{cc - 1}")
        P.i("scalar", lambda E: E.activation(
            out=self.wtmp8[:, :], in_=self.wstage[:, :], func=AF.Copy,
            scale=self.sxh_col[:, 0:1]), self.s_sc)
        P.mark(f"w8q_{cc}", self.s_sc)
        P.wm("vector", f"w8q_{cc}")
        if cc > 0:
            P.wm("vector", f"w8tr_{cc - 1}")
        P.i("vector", lambda E: E.tensor_copy(out=self.wtmpb[:, :],
                                              in_=self.wtmp8[:, :]), self.s_ve)
        P.mark(f"w8c_{cc}", self.s_ve)
        P.wm("tensor", f"w8c_{cc}")
        if cc > 0:
            P.wm("tensor", f"w8t_ev_{cc - 1}")
        else:
            P.wm("tensor", pstr_wait)
        for j in range(8):
            P.i("tensor", lambda E, j=j: E.transpose(
                self.ps_tr[:, j * 128:(j + 1) * 128],
                self.wtmpb[:, j * 128:(j + 1) * 128], self.ident[:, :]), self.s_te)
        P.mark(f"w8tr_{cc}", self.s_te)
        P.wm("vector", f"w8tr_{cc}")
        P.i("vector", lambda E, cc=cc: E.tensor_copy(
            out=AP(self.w8t, cc * 128, [[8 * D, 128], [D, 8], [1, 128]]),
            in_=self.ps_tr[:, :]), self.s_ve)
        P.mark(f"w8t_ev_{cc}", self.s_ve)
        if cc == 7:
            P.mark("w8t", self.s_ve)

    def p1a(self):
        P, c = self.P, self.cfg
        NT = c.ntiles

        def tile(i, t):
            self.dbgq("scalar", "vector", "sync")
            sl = i % 2
            lsl = i % 3
            eng = "sync" if sl == 0 else "scalar"
            src = self.d_xh if t == NT else self.d_x[t * 128:(t + 1) * 128, :]
            if i >= 2:
                P.wm(eng, f"a_bn_{i - 2}")
                P.wm(eng, f"a_ln_{i - 2}")
            P.i(eng, lambda E: E.dma_start(out=self.xa_sb[:, sl, :], in_=src),
                self.s_xa[sl], 16)
            P.w("vector", self.s_xa[sl], P.cur(self.s_xa[sl]))
            P.selfwait("vector", self.s_ve)
            P.i("vector", lambda E: E.bn_stats(out=self.bn_buf[:, 0:6],
                                               in_=self.xa_sb[:, sl, 0:512]), self.s_ve)
            P.i("vector", lambda E: E.bn_stats(out=self.bn_buf[:, 6:12],
                                               in_=self.xa_sb[:, sl, 512:1024]), self.s_ve)
            P.selfwait("vector", self.s_ve)
            P.i("vector", lambda E: E.bn_aggr(out=self.mv_tmp[:, :],
                                              in_=self.bn_buf[:, :]), self.s_ve)
            P.mark(f"a_bn_{i}", self.s_ve)
            P.w("scalar", self.s_ve, P.cur(self.s_ve))
            P.i("scalar", lambda E, t=t: E.activation(
                out=self.mr1[:, t, 0:1], in_=self.mv_tmp[:, 1:2], func=AF.Ln,
                bias=EPS), self.s_sc)
            P.selfwait("scalar", self.s_sc)
            P.i("scalar", lambda E, t=t: E.activation(
                out=self.mr1[:, t, 0:1], in_=self.mr1[:, t, 0:1], func=AF.Exp,
                scale=-0.5), self.s_sc)
            P.w("vector", self.s_sc, P.cur(self.s_sc))
            P.i("vector", lambda E, t=t: E.tensor_scalar(
                out=self.mr1[:, t, 1:2], in0=self.mv_tmp[:, 0:1],
                scalar1=self.mr1[:, t, 0:1], scalar2=-1.0,
                op0=ALU.mult, op1=ALU.mult), self.s_ve)
            P.mark(f"a_mr_{i}", self.s_ve)
            P.w("scalar", self.s_ve, P.cur(self.s_ve))
            if i >= 3:
                P.wm("scalar", f"a_pool_{i - 3}")
            P.i("scalar", lambda E, t=t, sl=sl, lsl=lsl: E.activation(
                out=self.ln1a_sb[:, lsl, :], in_=self.xa_sb[:, sl, :], func=AF.Identity,
                scale=self.mr1[:, t, 0:1], bias=self.mr1[:, t, 1:2]), self.s_sc)
            P.mark(f"a_ln_{i}", self.s_sc)
            if t != NT:
                P.w("tensor", self.s_sc, P.cur(self.s_sc))
                P.wm("tensor", "consts")
                first = (t == 0)
                P.i("tensor", lambda E, t=t, lsl=lsl, first=first: E.matmul(
                    self.ps_a[0:16, 0:512], self.a_all[:, t, :],
                    self.ln1a_sb[:, lsl, 0:512], start=first,
                    stop=(t == NT - 1)), self.s_te)
                P.i("tensor", lambda E, t=t, lsl=lsl, first=first: E.matmul(
                    self.ps_a[0:16, 512:1024], self.a_all[:, t, :],
                    self.ln1a_sb[:, lsl, 512:1024], start=first,
                    stop=(t == NT - 1)), self.s_te)
            P.mark(f"a_pool_{i}", self.s_te)
            if self.cfg.dbg:
                P.selfwait("scalar", self.s_sc)
                row = NT * 128 if t == NT else t * 128
                P.i("scalar", lambda E, lsl=lsl, row=row: E.dma_start(
                    out=self.dbg["ln1"][row:row + 128, :],
                    in_=self.ln1a_sb[:, lsl, :]), self.s_sd, 16)

        tile(0, NT)
        self.wamax_start()
        for t in range(NT):
            tile(t + 1, t)
            if t >= 2 and t % 2 == 0 and (t - 2) // 2 < 8:
                self.wamax_reduce((t - 2) // 2)
        P.mark("pool_mm", self.s_te)

    def ag_start(self):
        P = self.P
        P.wm("vector", "pool_mm")
        P.i("vector", lambda E: E.tensor_copy(out=self.pool_sb[:, :],
                                              in_=self.ps_a[0:16, :]), self.s_ve)
        P.mark("pool_ev", self.s_ve)
        P.selfwait("vector", self.s_ve)
        P.w("scalar", self.s_ve, P.cur(self.s_ve))
        P.i("scalar", lambda E: E.dma_start(out=self.d_ccp_in, in_=self.pool_sb[:, :]),
            self.s_stg, 16)
        P.w("gpsimd", self.s_stg, P.cur(self.s_stg))
        P.i("gpsimd", lambda E: E.collective_compute(
            "AllGather", ALU.bypass, ins=[self.d_ccp_in], outs=[self.d_ccp_out],
            replica_groups=RG), self.s_gp)
        P.selfwait("gpsimd", self.s_gp)
        pid = self.nc.partition_id_tensor
        P.i("gpsimd", lambda E: E.dma_start(out=self.pid_sb[0:1, 0:1],
                                            in_=AP(pid, 0, [[1, 1], [1, 1]])), self.s_gd2, 16)
        P.w("gpsimd", self.s_gd2, P.cur(self.s_gd2))

        def dyn_read(E):
            with E.register("rpid") as r:
                E.reg_load(r, self.pid_sb[0:1, 0:1])
                E.reg_alu(r, r, 4, ALU.divide)
                E.reg_alu(r, r, 64 * D, ALU.mult)
                return E.dma_start(out=self.pooled_all[0:64, :],
                                   in_=AP(self.d_ccp_out.tensor, r, [[D, 64], [1, D]]))
        P.i("gpsimd", dyn_read, self.s_gd2, 16)
        P.mark("pooled_my", self.s_gd2)
        if self.cfg.dbg:
            P.w("gpsimd", self.s_gd2, P.cur(self.s_gd2))
            P.i("gpsimd", lambda E: E.dma_start(out=self.dbg["pooled"],
                                                in_=self.d_ccp_out), self.s_gd2, 16)

    def ag_finish(self):
        P = self.P
        P.wm("tensor", "pooled_my")
        P.wm("tensor", "b_fm_2")
        for j in range(8):
            P.i("tensor", lambda E, j=j: E.transpose(
                self.ps_tr[:, j * NLAT:(j + 1) * NLAT],
                self.pooled_all[0:64, j * 128:(j + 1) * 128],
                self.ident[0:64, 0:64]), self.s_te)
        P.w("vector", self.s_te, P.cur(self.s_te))
        P.i("vector", lambda E: E.tensor_copy(
            out=self.pooled_fm[:, :, :],
            in_=AP(self.ps_tr, 0, [[1024, 128], [NLAT, 8], [1, NLAT]])), self.s_ve)
        P.mark("pooled_fm", self.s_ve)

        P.wm("tensor", "pooled_fm")
        P.wm("tensor", "v_2")
        for rc in range(2):
            for dc in range(8):
                P.i("tensor", lambda E, rc=rc, dc=dc: E.matmul(
                    self.ps_s[:, rc * NLAT:(rc + 1) * NLAT],
                    self.rkv_wb[:, dc, rc * 128:(rc + 1) * 128],
                    self.pooled_fm[:, dc, :], start=(dc == 0), stop=(dc == 7)),
                    self.s_te)
        P.w("scalar", self.s_te, P.cur(self.s_te))
        for rc in range(2):
            P.i("scalar", lambda E, rc=rc: E.activation(
                out=self.rk_fm[:, rc, :], in_=self.ps_s[:, rc * NLAT:(rc + 1) * NLAT],
                func=AF.Identity, bias=self.rkb_col[:, rc:rc + 1]), self.s_sc)
        P.mark("rk", self.s_sc)

        P.wm("tensor", "rk")
        P.i("tensor", lambda E: E.matmul(
            self.ps_s[0:64, 0:LD], self.ones1[0:1, 0:64],
            self.rvb_row[0:1, :], start=True, stop=False), self.s_te)
        for dc in range(8):
            P.i("tensor", lambda E, dc=dc: E.matmul(
                self.ps_s[0:64, 0:LD], self.pooled_fm[:, dc, :],
                self.rkv_wb[:, dc, LD:2 * LD], start=False, stop=(dc == 7)), self.s_te)
        P.w("vector", self.s_te, P.cur(self.s_te))
        P.i("vector", lambda E: E.tensor_copy(
            out=AP(self.rv_aug, 0, [[LH * 65, 64], [65, LH], [1, LHD]]),
            in_=self.ps_s[0:64, 0:LD]), self.s_ve)
        P.mark("rv_ps_done", self.s_ve)
        P.selfwait("vector", self.s_ve)
        P.w("scalar", self.s_ve, P.cur(self.s_ve))
        P.i("scalar", lambda E: E.dma_start(
            out=self.rv_aug[64:128, :, :], in_=self.rv_aug[0:64, :, :]), self.s_stg, 16)
        P.mark("rv", self.s_stg)
        if self.cfg.dbg:
            P.wm("vector", "rv")
            P.w("scalar", self.s_ve, P.cur(self.s_ve))
            P.i("scalar", lambda E: E.dma_start(
                out=AP(self.dbg["rk"].tensor, 0,
                       [[NLAT, 128], [128 * NLAT, 2], [1, NLAT]]),
                in_=AP(self.rk_fm, 0, [[2 * NLAT, 128], [NLAT, 2], [1, NLAT]])),
                self.s_vd, 16)
            P.w("scalar", self.s_ve, P.cur(self.s_ve))
            P.i("scalar", lambda E: E.dma_start(
                out=self.dbg["rv"], in_=self.rv_aug[0:64, :, :]), self.s_vd, 16)

    def dbgq(self, *engs):
        if not self.cfg.dbg:
            return
        for e in engs:
            self.P.w(e, self.s_vd, self.P.cur(self.s_vd))
            self.P.w(e, self.s_sd, self.P.cur(self.s_sd))

    def chain_tile(self, i, t, pstr_wait=None, pss_wait=None):
        P, c = self.P, self.cfg
        self.dbgq(*("vector", "scalar", "tensor"))
        NT = c.ntiles
        halo = (t == NT)
        xsl = i % 2
        lsl = i % 2
        src = self.d_xh if halo else self.d_x[t * 128:(t + 1) * 128, :]
        if i >= 2:
            P.wm("sync", f"b_ln_{i - 2}")
        if i == 0:
            P.wm("sync", "wam_rd_7")
        P.i("sync", lambda E, xsl=xsl, src=src: E.dma_start(
            out=self.xl_sb[:, xsl, :], in_=src), self.s_xl[xsl], 16)
        P.w("scalar", self.s_xl[xsl], P.cur(self.s_xl[xsl]))
        if i >= 2:
            P.wm("scalar", f"b_tr_{i - 2}")
        P.i("scalar", lambda E, t=t, xsl=xsl, lsl=lsl: E.activation(
            out=self.ln1b_sb[:, lsl, :], in_=self.xl_sb[:, xsl, :], func=AF.Identity,
            scale=self.mr1[:, t, 0:1], bias=self.mr1[:, t, 1:2]), self.s_sc)
        P.mark(f"b_ln_{i}", self.s_sc)
        P.w("tensor", self.s_sc, P.cur(self.s_sc))
        if i >= 1:
            P.wm("tensor", f"b_fm_{i - 1}")
        if pstr_wait:
            P.wm("tensor", pstr_wait)
        if not halo and t % 4 == 0 and t // 4 >= 1:
            P.wm("tensor", f"latfm_{t // 4 - 1}_1")
        for j in range(8):
            P.i("tensor", lambda E, j=j, lsl=lsl: E.transpose(
                self.ps_tr[:, j * 128:(j + 1) * 128],
                self.ln1b_sb[:, lsl, j * 128:(j + 1) * 128], self.ident[:, :]),
                self.s_te)
        P.mark(f"b_tr_{i}", self.s_te)
        P.w("vector", self.s_te, P.cur(self.s_te))
        if halo:
            P.i("vector", lambda E: E.tensor_copy(
                out=self.lnfmh_sb[:, :, :],
                in_=AP(self.ps_tr, 0, [[1024, 128], [128, 8], [1, 128]])), self.s_ve)
        else:
            tin = t % 4
            P.i("vector", lambda E, tin=tin: E.tensor_copy(
                out=AP(self.lnfm_sb, tin * 128, [[8 * 512, 128], [512, 8], [1, 128]]),
                in_=AP(self.ps_tr, 0, [[1024, 128], [128, 8], [1, 128]])), self.s_ve)
        P.mark(f"b_fm_{i}", self.s_ve)

        P.wm("tensor", f"b_fm_{i}")
        if i >= 1:
            P.wm("tensor", f"v_{i - 1}")
        vslot = 0 if halo else (t + 1) % 5
        if not halo and t >= 4:
            P.wm("tensor", f"av_done_{t - 4}")
        if pss_wait:
            P.wm("tensor", pss_wait)
        if halo:
            P.wm("tensor", "pool_ev")
        elif t // 4 >= 1 and t % 4 == 0:
            P.wm("tensor", f"rem_ev_v_{t // 4 - 1}_3")
            P.wm("tensor", f"exp_{t - 1}_3")
        lnsrc = (lambda dc: self.lnfmh_sb[:, dc, :]) if halo else \
                (lambda dc, tin=t % 4: self.lnfm_sb[:, dc, tin * 128:(tin + 1) * 128])
        for half in range(2):
            P.i("tensor", lambda E, half=half: E.matmul(
                self.ps_a[:, half * 512:(half + 1) * 512], self.ones1[0:1, :],
                self.vb_row[0:1, half * 512:(half + 1) * 512],
                start=True, stop=False), self.s_te)
            for dc in range(8):
                P.i("tensor", lambda E, dc=dc, half=half: E.matmul(
                    self.ps_a[:, half * 512:(half + 1) * 512], lnsrc(dc),
                    self.qkv_wb[:, dc, 2 * D + half * 512:2 * D + (half + 1) * 512],
                    start=False, stop=(dc == 7)), self.s_te)
        P.i("tensor", lambda E: E.matmul(
            self.ps_s[:, 0:H], self.ones1[0:1, :], self.bgl_row[0:1, :],
            start=True, stop=False), self.s_te)
        for dc in range(8):
            P.i("tensor", lambda E, dc=dc: E.matmul(
                self.ps_s[:, 0:H], lnsrc(dc), self.wvgl[:, dc, :],
                start=False, stop=(dc == 7)), self.s_te)
        P.mark(f"v_mm_{i}", self.s_te)
        P.w("vector", self.s_te, P.cur(self.s_te))
        P.i("vector", lambda E, vslot=vslot: E.tensor_copy(
            out=AP(self.v_ring, vslot * H * 66, [[5 * H * 66, 128], [66, H], [1, HD]]),
            in_=AP(self.ps_a, 0, [[1024, 128], [HD, H], [1, HD]])), self.s_ve)
        P.i("vector", lambda E, vslot=vslot: E.tensor_copy(
            out=AP(self.v_ring, vslot * H * 66 + 65, [[5 * H * 66, 128], [66, H], [1, 1]]),
            in_=self.ps_s[:, 0:H]), self.s_ve)
        P.mark(f"v_{i}", self.s_ve)
        if self.cfg.dbg:
            P.selfwait("vector", self.s_ve)
            row = NT * 128 if halo else t * 128
            P.w("scalar", self.s_ve, P.cur(self.s_ve))
            P.i("scalar", lambda E, vslot=vslot, row=row: E.dma_start(
                out=self.dbg["v"][row:row + 128, :],
                in_=self.v_ring[:, vslot, :, :]), self.s_vd, 16)

    def qk_group(self, g, gates=()):
        P, c = self.P, self.cfg
        self.dbgq(*("scalar",))
        for j in range(16):
            half = j % 2
            if j >= 2:
                P.wm("tensor", f"qk_ev_{g}_{j - 2}")
            else:
                P.wm("tensor", f"v_{4 * g + 4}")
            if g == 0 and j == 0:
                P.wm("tensor", "kh")
                P.wm("tensor", "prologue_v")
            for dc in range(8):
                P.i("tensor", lambda E, j=j, dc=dc, half=half: E.matmul(
                    self.ps_a[:, half * 512:(half + 1) * 512],
                    self.qkv_wb[:, dc, j * 128:(j + 1) * 128],
                    self.lnfm_sb[:, dc, :], start=(dc == 0), stop=(dc == 7)), self.s_te)
            P.mark(f"qk_mm_{g}_{j}", self.s_te)
            if self.cfg.dbg and g == 0 and j == 0:
                P.w("gpsimd", self.s_te, P.cur(self.s_te))
                P.i("gpsimd", lambda E: E.dma_start(
                    out=self.dbg["probe"][0:128, 0:2048],
                    in_=self.qkv_wb[:, 0, 0:2048]), self.s_gd2, 16)
                P.w("scalar", self.s_gd2, P.cur(self.s_gd2))
            P.w("scalar", self.s_te, P.cur(self.s_te))
            if j < 8:
                dst = lambda E, j=j: self.q_sb[:, j, :]
            else:
                dst = lambda E, j=j: self.k_sb[:, j - 8, :]
            if j == 8 and g > 0:
                P.wm("scalar", f"kprev_{g - 1}")
            P.i("scalar", lambda E, j=j, half=half, dst=dst: E.activation(
                out=dst(E), in_=self.ps_a[:, half * 512:(half + 1) * 512],
                func=AF.Identity, bias=self.qkb_col[:, j:j + 1]), self.s_sc)
            P.mark(f"qk_ev_{g}_{j}", self.s_sc)
            if j % 4 == 3 and gates:
                self.gate_outw_res(gates[j // 4])
        P.mark(f"qk_{g}", self.s_sc)
        if self.cfg.dbg:
            P.selfwait("scalar", self.s_sc)
            for j in range(8):
                P.i("scalar", lambda E, j=j, g=g: E.dma_start(
                    out=self.dbg["q"][j * 128:(j + 1) * 128, g * 512:(g + 1) * 512],
                    in_=self.q_sb[:, j, :]), self.s_sd, 16)
                P.i("scalar", lambda E, j=j, g=g: E.dma_start(
                    out=self.dbg["k"][j * 128:(j + 1) * 128, g * 512:(g + 1) * 512],
                    in_=self.k_sb[:, j, :]), self.s_sd, 16)

    def rq_group(self, g):
        P = self.P
        self.dbgq(*("scalar",))
        for rc in range(2):
            P.wm("tensor", f"qk_ev_{g}_{14 + rc}")
            for dc in range(8):
                P.i("tensor", lambda E, rc=rc, dc=dc: E.matmul(
                    self.ps_a[:, rc * 512:(rc + 1) * 512],
                    self.rq_wb[:, dc, rc * 128:(rc + 1) * 128],
                    self.lnfm_sb[:, dc, :], start=(dc == 0), stop=(dc == 7)), self.s_te)
            P.mark(f"rq_mm_{g}_{rc}", self.s_te)
            P.w("scalar", self.s_te, P.cur(self.s_te))
            P.i("scalar", lambda E, rc=rc: E.activation(
                out=self.rq_sb[:, rc, :], in_=self.ps_a[:, rc * 512:(rc + 1) * 512],
                func=AF.Identity, bias=self.rqb_col[:, rc:rc + 1]), self.s_sc)
            P.mark(f"rq_ev_{g}_{rc}", self.s_sc)
        P.mark(f"rq_{g}", self.s_sc)
        if self.cfg.dbg:
            P.selfwait("scalar", self.s_sc)
            for rc in range(2):
                P.i("scalar", lambda E, rc=rc, g=g: E.dma_start(
                    out=self.dbg["rq"][rc * 128:(rc + 1) * 128, g * 512:(g + 1) * 512],
                    in_=self.rq_sb[:, rc, :]), self.s_sd, 16)

    def local_qtile(self, tq):
        P, c = self.P, self.cfg
        self.dbgq(*("vector", "scalar", "tensor"))
        g = tq // 4
        t = tq % 4
        esl = 0
        for w in range(4):
            pb = self.ps_b if w % 2 == 0 else self.ps_a
            if w >= 2:
                P.wm("tensor", f"exp_{tq}_{w - 2}")
            elif w == 0:
                if tq >= 4 and t == 0:
                    P.wm("tensor", f"res1_{tq - 1}")
                elif tq >= 1:
                    P.wm("tensor", f"exp_{tq - 1}_2")
            else:
                if t == 0:
                    P.wm("tensor", f"rq_ev_{g}_1")
                else:
                    P.wm("tensor", f"exp_{tq - 1}_3")
            if w == 0 and t == 0 and g > 0:
                P.wm("tensor", f"kprev_{g - 1}")
            P.wm("tensor", f"qk_{g}")
            for hh in range(2):
                for ab in range(2):
                    for hp in range(2):
                        h = 4 * w + 2 * hh + hp
                        kchunk = h // 2
                        rb = (h % 2) * 64
                        if ab == 0:
                            if tq == 0:
                                klhs = lambda E, kchunk=kchunk, rb=rb: \
                                    self.kh_sb[rb:rb + 64, kchunk, :]
                            elif t == 0:
                                klhs = lambda E, kchunk=kchunk, rb=rb: \
                                    self.kprev_sb[rb:rb + 64, kchunk, :]
                            else:
                                klhs = lambda E, kchunk=kchunk, rb=rb, t=t: \
                                    self.k_sb[rb:rb + 64, kchunk, (t - 1) * 128:t * 128]
                        else:
                            klhs = lambda E, kchunk=kchunk, rb=rb, t=t: \
                                self.k_sb[rb:rb + 64, kchunk, t * 128:(t + 1) * 128]
                        col = (h % 2) * 512 + hh * 256 + ab * 128
                        P.i("tensor", lambda E, klhs=klhs, h=h, t=t, col=col, pb=pb: E.matmul(
                            pb[:, col:col + 128], klhs(E),
                            self.q_sb[(h % 2) * 64:(h % 2) * 64 + 64, h // 2,
                                      t * 128:(t + 1) * 128],
                            start=True, stop=True), self.s_te)
            P.mark(f"sc_{tq}_{w}", self.s_te)
            P.w("scalar", self.s_te, P.cur(self.s_te))
            if tq >= 1 and w == 0:
                P.wm("scalar", f"av_done_{tq - 1}")
            P.i("scalar", lambda E, esl=esl, w=w, pb=pb: E.activation(
                out=self.e_sb[:, 0, w * 1024:(w + 1) * 1024],
                in_=pb[:, :], func=AF.Exp, scale=0.125), self.s_sc)
            P.mark(f"exp_{tq}_{w}", self.s_sc)
            if w % 2 == 1:
                P.w("vector", self.s_sc, P.cur(self.s_sc))
                mask = self.maskAB0 if tq == 0 else self.maskAB
                P.i("vector", lambda E, w=w, mask=mask: E.tensor_tensor(
                    out=AP(self.e_sb, (w - 1) * 1024, [[4096, 128], [1, 2048]]),
                    in0=AP(self.e_sb, (w - 1) * 1024, [[4096, 128], [1, 2048]]),
                    in1=AP(mask, 0, [[512, 128], [0, 4], [1, 512]]),
                    op=ALU.mult), self.s_ve)
                P.mark(f"mask_{tq}_{w // 2}", self.s_ve)

        vA = tq % 5
        vB = (tq + 1) % 5
        P.wm("tensor", f"v_{tq + 1}")
        if t == 0 and tq >= 4:
            P.wm("tensor", f"amax_{tq - 1}")
        if t == 0 and g >= 1:
            P.wm("tensor", f"latn_ev_{g - 1}_3")

        def av(h, col, extra_wait=None):
            P.wm("tensor", f"mask_{tq}_{(h // 4) // 2}")
            if extra_wait:
                P.wm("tensor", extra_wait)
            if tq >= 1:
                P.wm("tensor", f"lu_ev_{tq - 1}_all")
            w = h // 4
            ecol = lambda ab: w * 1024 + (h % 2) * 512 + \
                ((h % 4) // 2) * 256 + ab * 128
            P.i("tensor", lambda E, h=h, col=col: E.matmul(
                self.ps_c[:, col:col + 66],
                AP(self.e_sb, ecol(0), [[4096, 128], [1, 128]]),
                AP(self.v_ring, vA * H * 66 + h * 66, [[5 * H * 66, 128], [1, 66]]),
                start=True, stop=False), self.s_te)
            P.i("tensor", lambda E, h=h, col=col: E.matmul(
                self.ps_c[:, col:col + 66],
                AP(self.e_sb, ecol(1), [[4096, 128], [1, 128]]),
                AP(self.v_ring, vB * H * 66 + h * 66, [[5 * H * 66, 128], [1, 66]]),
                start=False, stop=True), self.s_te)

        for h in range(7):
            av(h, h * 66)
        P.mark(f"av_r0_{tq}", self.s_te)
        for h in range(7, 14):
            av(h, 512 + (h - 7) * 66)
        P.mark(f"av_r1_{tq}", self.s_te)
        P.wm("vector", f"av_r0_{tq}")
        P.i("vector", lambda E, t=t: E.tensor_copy(
            out=self.lu_sb[:, t, 0:7, :], in_=self.ps_c[:, 0:462]), self.s_ve)
        P.mark(f"lu_ev_{tq}_0", self.s_ve)
        for h in range(14, 16):
            av(h, (h - 14) * 66, extra_wait=f"lu_ev_{tq}_0")
        P.mark(f"av_r2_{tq}", self.s_te)
        P.wm("vector", f"av_r1_{tq}")
        P.i("vector", lambda E, t=t: E.tensor_copy(
            out=self.lu_sb[:, t, 7:14, :], in_=self.ps_c[:, 512:974]), self.s_ve)
        P.wm("vector", f"av_r2_{tq}")
        P.i("vector", lambda E, t=t: E.tensor_copy(
            out=self.lu_sb[:, t, 14:16, :], in_=self.ps_c[:, 0:132]), self.s_ve)
        P.mark(f"lu_ev_{tq}_all", self.s_ve)
        P.mark(f"av_done_{tq}", self.s_te)
        if self.cfg.dbg:
            P.selfwait("vector", self.s_ve)
            P.w("scalar", self.s_ve, P.cur(self.s_ve))
            P.i("scalar", lambda E, tq=tq, t=t: E.dma_start(
                out=self.dbg["lu"][tq * 128:(tq + 1) * 128, :],
                in_=self.lu_sb[:, t, :, :]), self.s_vd, 16)

    def latent_group(self, g):
        P, c = self.P, self.cfg
        self.dbgq(*("vector", "scalar", "tensor"))
        CH = c.chunk
        lsc = 1.0 / np.sqrt(LHD)
        P.wm("tensor", "rv")
        P.wm("tensor", "rk")
        P.wm("tensor", f"rq_{g}")
        P.wm("tensor", f"exp_{4 * g + 3}_3")
        for pr in range(2):
            for hp in range(2):
                hq = 2 * pr + hp
                rb = hp * 64
                if pr == 0:
                    P.i("tensor", lambda E, hq=hq, rb=rb: E.matmul(
                        self.ps_s[rb:rb + 64, 0:512],
                        self.rk_fm[rb:rb + 64, hq // 2, :],
                        self.rq_sb[rb:rb + 64, hq // 2, :],
                        start=True, stop=True), self.s_te)
                else:
                    P.i("tensor", lambda E, hq=hq, rb=rb: E.matmul(
                        self.ps_b[rb:rb + 64, 0:512],
                        self.rk_fm[rb:rb + 64, hq // 2, :],
                        self.rq_sb[rb:rb + 64, hq // 2, :],
                        start=True, stop=True), self.s_te)
            P.mark(f"lsc_{g}_{pr}", self.s_te)
            P.w("scalar", self.s_te, P.cur(self.s_te))
            if pr == 0:
                P.i("scalar", lambda E: E.activation(
                    out=self.elat_sb[:, 0, :], in_=self.ps_s[:, 0:512],
                    func=AF.Exp, scale=lsc), self.s_sc)
            else:
                P.i("scalar", lambda E: E.activation(
                    out=self.elat_sb[:, 1, :], in_=self.ps_b[:, 0:512],
                    func=AF.Exp, scale=lsc), self.s_sc)
            P.mark(f"lexp_{g}_{pr}", self.s_sc)
        P.w("vector", self.s_sc, P.cur(self.s_sc))
        ncg = c.ch_per_grp
        P.i("vector", lambda E, g=g, ncg=ncg: E.tensor_tensor(
            out=AP(self.elat_sb, 0, [[2 * 512, 128], [512, 2], [1, 512]]),
            in0=AP(self.elat_sb, 0, [[2 * 512, 128], [512, 2], [1, 512]]),
            in1=AP(self.latmask_sb, g * ncg, [[16, 128], [0, 2], [1, ncg], [0, CH]]),
            op=ALU.mult), self.s_ve)
        P.mark(f"lmask_{g}", self.s_ve)
        if self.cfg.dbg:
            P.selfwait("vector", self.s_ve)
            P.w("scalar", self.s_ve, P.cur(self.s_ve))
            P.i("scalar", lambda E, g=g: E.dma_start(
                out=self.dbg["elat"][0:NLAT, 0 * 512 + g * 512:(g + 1) * 512],
                in_=self.elat_sb[0:64, 0, :]), self.s_vd, 16)
            P.w("scalar", self.s_ve, P.cur(self.s_ve))
            P.i("scalar", lambda E, g=g: E.dma_start(
                out=self.dbg["elat"][0:NLAT, LH // 2 * c.s_core + g * 512:
                                     LH // 2 * c.s_core + (g + 1) * 512],
                in_=self.elat_sb[0:64, 1, :]), self.s_vd, 16)
            P.w("scalar", self.s_ve, P.cur(self.s_ve))
            P.i("scalar", lambda E, g=g: E.dma_start(
                out=self.dbg["elat"][0:NLAT, 1 * c.s_core + g * 512:
                                     1 * c.s_core + (g + 1) * 512],
                in_=self.elat_sb[64:128, 0, :]), self.s_vd, 16)
            P.w("scalar", self.s_ve, P.cur(self.s_ve))
            P.i("scalar", lambda E, g=g: E.dma_start(
                out=self.dbg["elat"][0:NLAT, 3 * c.s_core + g * 512:
                                     3 * c.s_core + (g + 1) * 512],
                in_=self.elat_sb[64:128, 1, :]), self.s_vd, 16)

        P.wm("tensor", f"lmask_{g}")
        P.wm("tensor", f"av_done_{4 * g + 3}")
        P.wm("tensor", f"lu_ev_{4 * g + 3}_all")
        for st in range(4):
            if st >= 1:
                P.wm("tensor", f"latn_ev_{g}_{st - 1}")
            for hq in range(LH):
                pr, hp = hq // 2, hq % 2
                rb = hp * 64
                col = (hq % 2) * 512 + (hq // 2) * 130
                P.i("tensor", lambda E, st=st, hq=hq, pr=pr, rb=rb, col=col: E.matmul(
                    self.ps_c[:, col:col + 65],
                    self.elat_sb[rb:rb + 64, pr, st * 128:(st + 1) * 128],
                    self.rv_aug[rb:rb + 64, hq, :],
                    start=True, stop=True), self.s_te)
            P.mark(f"latav_{g}_{st}", self.s_te)
            P.wm("vector", f"latav_{g}_{st}")
            P.i("vector", lambda E: E.tensor_copy(
                out=self.dtmp[:, :],
                in_=AP(self.ps_c, 64, [[1024, 128], [512, 2], [130, 2], [1, 1]])),
                self.s_ve)
            P.selfwait("vector", self.s_ve)
            P.i("vector", lambda E: E.tensor_scalar_max(
                out=self.dtmp[:, :], in0=self.dtmp[:, :], scalar1=1e-30), self.s_ve)
            P.selfwait("vector", self.s_ve)
            P.i("vector", lambda E, st=st: E.reciprocal(
                out=self.dinv_sb[:, st, :], in_=self.dtmp[:, :]), self.s_ve)
            P.selfwait("vector", self.s_ve)
            P.i("vector", lambda E, st=st: E.tensor_tensor(
                out=self.latn_sb[:, st, :, :],
                in0=AP(self.ps_c, 0, [[1024, 128], [512, 2], [130, 2], [1, LHD]]),
                in1=AP(self.dinv_sb, st * LH, [[4 * LH, 128], [1, LH], [0, LHD]]),
                op=ALU.mult), self.s_ve)
            P.mark(f"latn_ev_{g}_{st}", self.s_ve)
        jof = {0: 0, 1: 2, 2: 1, 3: 3}
        for cc in range(2):
            P.wm("tensor", f"latn_ev_{g}_3")
            if g >= 1:
                P.wm("tensor", f"mixT_ev_{4 * g - 1}")
            else:
                P.wm("tensor", "w8t_ev_7")
            if cc == 1:
                P.wm("tensor", f"latfm_{g}_0")
            for hp in range(2):
                hq = 2 * cc + hp
                j = jof[hq]
                for st in range(4):
                    P.i("tensor", lambda E, st=st, j=j, hp=hp: E.transpose(
                        self.ps_tr[hp * 64:hp * 64 + 64, st * 128:(st + 1) * 128],
                        self.latn_sb[:, st, j, :], self.ident[:, :]), self.s_te)
            P.mark(f"latT_{g}_{cc}", self.s_te)
            P.w("vector", self.s_te, P.cur(self.s_te))
            P.i("vector", lambda E, cc=cc: E.tensor_copy(
                out=self.latfm_sb[:, cc, :], in_=self.ps_tr[:, 0:512]), self.s_ve)
            P.mark(f"latfm_{g}_{cc}", self.s_ve)
        if self.cfg.dbg:
            P.selfwait("vector", self.s_ve)
            P.w("scalar", self.s_ve, P.cur(self.s_ve))
            P.i("scalar", lambda E, g=g: E.dma_start(
                out=AP(self.dbg["latn"].tensor, g * 512 * LD,
                       [[LD, 128], [128 * LD, 4], [1, LD]]),
                in_=AP(self.latn_sb, 0, [[4 * LH * LHD, 128], [LH * LHD, 4],
                                         [1, LH * LHD]])), self.s_vd, 16)

        P.wm("tensor", "wdma2")
        for st in range(4):
            P.wm("tensor", f"latfm_{g}_1")
            if st >= 1:
                P.wm("tensor", f"rem_ev_{g}_{st - 1}")
                P.wm("tensor", f"rem_ev_v_{g}_{st - 1}")
            P.i("tensor", lambda E: E.matmul(
                self.ps_b[:, 0:512], self.ones1[0:1, :], self.routb_row[0:1, 0:512],
                start=True, stop=False), self.s_te)
            P.i("tensor", lambda E: E.matmul(
                self.ps_b[:, 512:1024], self.ones1[0:1, :],
                self.routb_row[0:1, 512:1024], start=True, stop=False), self.s_te)
            P.i("tensor", lambda E: E.matmul(
                self.ps_s[:, 0:H], self.ones1[0:1, :], self.routb_row[0:1, 1024:1040],
                start=True, stop=False), self.s_te)
            for cc in range(2):
                last = (cc == 1)
                P.i("tensor", lambda E, st=st, cc=cc, last=last: E.matmul(
                    self.ps_b[:, 0:512],
                    self.latfm_sb[:, cc, st * 128:(st + 1) * 128],
                    self.rout_wb[:, cc, 0:512], start=False, stop=last), self.s_te)
                P.i("tensor", lambda E, st=st, cc=cc, last=last: E.matmul(
                    self.ps_b[:, 512:1024],
                    self.latfm_sb[:, cc, st * 128:(st + 1) * 128],
                    self.rout_wb[:, cc, 512:1024], start=False, stop=last), self.s_te)
                P.i("tensor", lambda E, st=st, cc=cc, last=last: E.matmul(
                    self.ps_s[:, 0:H],
                    self.latfm_sb[:, cc, st * 128:(st + 1) * 128],
                    self.rout_wb[:, cc, 1024:1040], start=False, stop=last), self.s_te)
            P.mark(f"rout_mm_{g}_{st}", self.s_te)
            P.w("scalar", self.s_te, P.cur(self.s_te))
            P.i("scalar", lambda E, st=st: E.activation(
                out=self.remote_sb[:, st, 0:512], in_=self.ps_b[:, 0:512],
                func=AF.Copy), self.s_sc)
            P.i("scalar", lambda E, st=st: E.activation(
                out=self.remote_sb[:, st, 512:1024], in_=self.ps_b[:, 512:1024],
                func=AF.Copy), self.s_sc)
            P.w("vector", self.s_te, P.cur(self.s_te))
            P.i("vector", lambda E, st=st: E.tensor_copy(
                out=self.rgr_sb[:, st, :], in_=self.ps_s[:, 0:H]), self.s_ve)
            P.mark(f"rem_ev_s_{g}_{st}", self.s_sc)
            P.mark(f"rem_ev_v_{g}_{st}", self.s_ve)
            P.mark(f"rem_ev_{g}_{st}", self.s_sc)
        P.mark(f"latent_{g}", self.s_sc)
        if self.cfg.dbg:
            P.selfwait("scalar", self.s_sc)
            P.i("scalar", lambda E, g=g: E.dma_start(
                out=AP(self.dbg["remote"].tensor, g * 512 * D,
                       [[D, 128], [128 * D, 4], [1, D]]),
                in_=AP(self.remote_sb, 0, [[4 * D, 128], [D, 4], [1, D]])),
                self.s_sd, 16)
            P.selfwait("vector", self.s_ve)
            P.w("scalar", self.s_ve, P.cur(self.s_ve))
            P.i("scalar", lambda E, g=g: E.dma_start(
                out=AP(self.dbg["rgr"].tensor, g * 512 * H,
                       [[H, 128], [128 * H, 4], [1, H]]),
                in_=AP(self.rgr_sb, 0, [[4 * H, 128], [H, 4], [1, H]])), self.s_vd, 16)

    def gate_outw_res(self, tq):
        P, c = self.P, self.cfg
        self.dbgq(*("vector", "scalar", "tensor"))
        g, t = tq // 4, tq % 4
        NT = c.ntiles
        P.wm("vector", f"rem_ev_s_{g}_{t}")
        P.wm("vector", f"rem_ev_v_{g}_{t}")
        P.wm("vector", f"lu_ev_{tq}_all")
        lu_base = t * H * 66
        luap = lambda off, inner: AP(self.lu_sb, lu_base + off,
                                     [[4 * H * 66, 128], [66, H], [1, inner]])
        P.i("vector", lambda E: E.tensor_scalar_max(
            out=self.gsc[:, 0, :], in0=luap(64, 1), scalar1=1e-30), self.s_ve)
        P.selfwait("vector", self.s_ve)
        P.i("vector", lambda E: E.reciprocal(out=self.gsc[:, 0, :],
                                             in_=self.gsc[:, 0, :]), self.s_ve)
        P.selfwait("vector", self.s_ve)
        P.i("vector", lambda E: E.tensor_tensor(
            out=self.gsc[:, 1, :], in0=luap(65, 1), in1=self.gsc[:, 0, :],
            op=ALU.mult), self.s_ve)
        P.selfwait("vector", self.s_ve)
        P.i("vector", lambda E, t=t: E.tensor_tensor(
            out=self.gsc[:, 1, :], in0=self.gsc[:, 1, :], in1=self.rgr_sb[:, t, :],
            op=ALU.add), self.s_ve)
        P.selfwait("vector", self.s_ve)
        P.w("scalar", self.s_ve, P.cur(self.s_ve))
        P.i("scalar", lambda E: E.activation(
            out=self.gsc[:, 2, :], in_=self.gsc[:, 1, :], func=AF.Exp,
            scale=-1.0), self.s_sc)
        P.w("vector", self.s_sc, P.cur(self.s_sc))
        P.i("vector", lambda E: E.tensor_scalar_add(
            out=self.gsc[:, 3, :], in0=self.gsc[:, 2, :], scalar1=1.0), self.s_ve)
        P.selfwait("vector", self.s_ve)
        P.i("vector", lambda E: E.reciprocal(out=self.gsc[:, 3, :],
                                             in_=self.gsc[:, 3, :]), self.s_ve)
        P.selfwait("vector", self.s_ve)
        P.i("vector", lambda E: E.tensor_tensor(
            out=self.gsc[:, 4, :], in0=self.gsc[:, 3, :], in1=self.gsc[:, 0, :],
            op=ALU.mult), self.s_ve)
        P.i("vector", lambda E: E.tensor_scalar(
            out=self.gsc[:, 5, :], in0=self.gsc[:, 3, :], scalar1=-1.0, scalar2=1.0,
            op0=ALU.mult, op1=ALU.add), self.s_ve)
        P.selfwait("vector", self.s_ve)
        msl = 0
        if tq >= 1:
            P.wm("vector", f"mixT_{tq - 1}")
        P.i("vector", lambda E, msl=msl: E.tensor_tensor(
            out=AP(self.mixed_sb, 0, [[D, 128], [HD, H], [1, HD]]),
            in0=luap(0, HD),
            in1=AP(self.gsc, 4 * H, [[6 * H, 128], [1, H], [0, HD]]),
            op=ALU.mult), self.s_ve)
        if tq >= 1:
            P.wm("vector", f"outw_{tq - 1}")
        P.i("vector", lambda E, t=t: E.tensor_tensor(
            out=AP(self.mixt_sb, 0, [[D, 128], [HD, H], [1, HD]]),
            in0=AP(self.remote_sb, t * D, [[4 * D, 128], [HD, H], [1, HD]]),
            in1=AP(self.gsc, 5 * H, [[6 * H, 128], [1, H], [0, HD]]),
            op=ALU.mult), self.s_ve)
        P.selfwait("vector", self.s_ve)
        P.i("vector", lambda E: E.tensor_tensor(
            out=self.mixed_sb[:, 0, :], in0=self.mixed_sb[:, 0, :],
            in1=self.mixt_sb[:, :], op=ALU.add), self.s_ve)
        P.mark(f"mixed_{tq}", self.s_ve)
        if self.cfg.dbg:
            P.selfwait("vector", self.s_ve)
            P.w("scalar", self.s_ve, P.cur(self.s_ve))
            P.i("scalar", lambda E, tq=tq: E.dma_start(
                out=self.dbg["mixed"][tq * 128:(tq + 1) * 128, :],
                in_=self.mixed_sb[:, 0, :]), self.s_vd, 16)

        P.wm("tensor", f"mixed_{tq}")
        if tq >= 1:
            P.wm("tensor", f"mixT_ev_{tq - 1}")
        for j in range(8):
            P.i("tensor", lambda E, j=j: E.transpose(
                self.ps_tr[:, j * 128:(j + 1) * 128],
                self.mixed_sb[:, 0, j * 128:(j + 1) * 128], self.ident[:, :]),
                self.s_te)
        P.mark(f"mixT_{tq}", self.s_te)
        P.w("vector", self.s_te, P.cur(self.s_te))
        P.i("vector", lambda E: E.tensor_copy(
            out=self.mixt_sb[:, :], in_=self.ps_tr[:, :]), self.s_ve)
        P.mark(f"mixT_ev_{tq}", self.s_ve)

        P.wm("tensor", f"mixT_ev_{tq}")
        P.wm("tensor", "wdma2")
        if t == 0:
            P.wm("tensor", f"rem_ev_{g}_3")
        else:
            P.wm("tensor", f"res1_{tq - 1}")
        for half in range(2):
            P.i("tensor", lambda E, half=half: E.matmul(
                self.ps_b[:, half * 512:(half + 1) * 512], self.ones1[0:1, :],
                self.outb_row[0:1, half * 512:(half + 1) * 512],
                start=True, stop=False), self.s_te)
            for dc in range(8):
                P.i("tensor", lambda E, dc=dc, half=half: E.matmul(
                    self.ps_b[:, half * 512:(half + 1) * 512],
                    self.mixt_sb[:, dc * 128:(dc + 1) * 128],
                    self.out_wb[:, dc, half * 512:(half + 1) * 512],
                    start=False, stop=(dc == 7)), self.s_te)
        P.mark(f"outw_{tq}", self.s_te)

        if tq >= 1:
            P.wm("sync", f"res1_{tq - 1}")
        P.i("sync", lambda E, tq=tq: E.dma_start(
            out=self.xr_sb[:, 0, :], in_=self.d_x[tq * 128:(tq + 1) * 128, :]),
            self.s_xr, 16)
        P.w("vector", self.s_xr, P.cur(self.s_xr))
        P.wm("vector", f"outw_{tq}")
        if tq >= 1:
            P.wm("vector", f"hdma_{tq - 1}")
        P.selfwait("vector", self.s_ve)
        P.i("vector", lambda E, tq=tq: E.tensor_tensor(
            out=self.h_sb[:, 0, :], in0=self.ps_b[:, :],
            in1=self.xr_sb[:, 0, :], op=ALU.add), self.s_ve)
        P.mark(f"res1_{tq}", self.s_ve)
        P.selfwait("vector", self.s_ve)
        P.w("scalar", self.s_ve, P.cur(self.s_ve))
        P.i("scalar", lambda E, tq=tq: E.dma_start(
            out=self.d_h_scratch[tq * 128:(tq + 1) * 128, :],
            in_=self.h_sb[:, 0, :]), self.s_hw, 16)
        P.mark(f"hdma_{tq}", self.s_hw)
        P.i("vector", lambda E: E.bn_stats(
            out=self.bn_buf[:, 0:6], in_=self.h_sb[:, 0, 0:512]), self.s_ve)
        P.i("vector", lambda E: E.bn_stats(
            out=self.bn_buf[:, 6:12], in_=self.h_sb[:, 0, 512:1024]), self.s_ve)
        P.selfwait("vector", self.s_ve)
        P.i("vector", lambda E: E.bn_aggr(out=self.mv_tmp[:, :],
                                          in_=self.bn_buf[:, :]), self.s_ve)
        P.selfwait("vector", self.s_ve)
        P.w("scalar", self.s_ve, P.cur(self.s_ve))
        P.i("scalar", lambda E, tq=tq: E.activation(
            out=self.mr2[:, tq, 0:1], in_=self.mv_tmp[:, 1:2], func=AF.Ln,
            bias=EPS), self.s_sc)
        P.selfwait("scalar", self.s_sc)
        P.i("scalar", lambda E, tq=tq: E.activation(
            out=self.mr2[:, tq, 0:1], in_=self.mr2[:, tq, 0:1], func=AF.Exp,
            scale=-0.5), self.s_sc)
        P.w("vector", self.s_sc, P.cur(self.s_sc))
        P.i("vector", lambda E, tq=tq: E.tensor_scalar(
            out=self.mr2[:, tq, 1:2], in0=self.mv_tmp[:, 0:1],
            scalar1=self.mr2[:, tq, 0:1], scalar2=-1.0,
            op0=ALU.mult, op1=ALU.mult), self.s_ve)
        P.selfwait("vector", self.s_ve)
        P.i("vector", lambda E, tq=tq: E.tensor_scalar(
            out=self.ps_c[:, :], in0=self.h_sb[:, 0, :],
            scalar1=self.mr2[:, tq, 0:1], scalar2=self.mr2[:, tq, 1:2],
            op0=ALU.mult, op1=ALU.add), self.s_ve)
        P.selfwait("vector", self.s_ve)
        P.i("vector", lambda E: E.reduce_max(
            out=self.rowmax_sb[:, 0:1], in_=self.ps_c[:, :], axis=AX.X,
            apply_absolute_value=True), self.s_ve)
        P.selfwait("vector", self.s_ve)
        P.i("vector", lambda E: E.tensor_tensor(
            out=self.amax_acc[:, :], in0=self.amax_acc[:, :],
            in1=self.rowmax_sb[:, 0:1], op=ALU.max), self.s_ve)
        P.mark(f"amax_{tq}", self.s_ve)
        if self.cfg.dbg:
            P.selfwait("vector", self.s_ve)
            P.w("scalar", self.s_ve, P.cur(self.s_ve))
            P.i("scalar", lambda E, tq=tq: E.dma_start(
                out=self.dbg["h"][tq * 128:(tq + 1) * 128, :],
                in_=self.h_sb[:, 0, :]), self.s_vd, 16)

    def barrier(self):
        P = self.P
        NT = self.cfg.ntiles
        P.wm("vector", f"amax_{NT - 1}")
        P.selfwait("vector", self.s_ve)
        P.w("scalar", self.s_ve, P.cur(self.s_ve))
        P.i("scalar", lambda E: E.dma_start(out=self.amax_row[0:1, :],
                                            in_=self.amax_acc[:, :]), self.s_stg, 16)
        P.w("vector", self.s_stg, P.cur(self.s_stg))
        P.i("vector", lambda E: E.reduce_max(
            out=self.scal[0:1, 0:1], in_=self.amax_row[0:1, :], axis=AX.X), self.s_ve)
        P.selfwait("vector", self.s_ve)
        P.i("vector", lambda E: E.memset(self.amax_row[0:1, :], 0.0), self.s_ve)
        P.selfwait("vector", self.s_ve)
        P.i("vector", lambda E: E.tensor_copy(out=self.amax_row[0:1, 0:1],
                                              in_=self.scal[0:1, 0:1]), self.s_ve)
        P.selfwait("vector", self.s_ve)
        P.w("scalar", self.s_ve, P.cur(self.s_ve))
        P.i("scalar", lambda E: E.dma_start(out=self.d_cca_in,
                                            in_=self.amax_row[0:1, :]), self.s_stg, 16)
        P.w("gpsimd", self.s_stg, P.cur(self.s_stg))
        P.i("gpsimd", lambda E: E.collective_compute(
            "AllReduce", ALU.max, ins=[self.d_cca_in], outs=[self.d_cca_out],
            replica_groups=RG), self.s_gp)
        P.selfwait("gpsimd", self.s_gp)
        P.i("gpsimd", lambda E: E.dma_start(out=self.scal[0:1, 0:1],
                                            in_=self.d_cca_out[0:1, 0:1]), self.s_gd2, 16)
        P.mark("amax_g", self.s_gd2)
        P.wm("vector", "amax_g")
        P.i("vector", lambda E: E.tensor_scalar_max(
            out=self.scal[0:1, 0:1], in0=self.scal[0:1, 0:1], scalar1=1e-12), self.s_ve)
        P.selfwait("vector", self.s_ve)
        P.i("vector", lambda E: E.reciprocal(out=self.scal[0:1, 5:6],
                                             in_=self.scal[0:1, 0:1]), self.s_ve)
        P.selfwait("vector", self.s_ve)
        P.i("vector", lambda E: E.tensor_scalar(
            out=self.scal[0:1, 5:6], in0=self.scal[0:1, 5:6], scalar1=FP8MAX,
            scalar2=FP8MAX, op0=ALU.mult, op1=ALU.min), self.s_ve)
        P.selfwait("vector", self.s_ve)
        P.i("vector", lambda E: E.tensor_scalar_mul(
            out=self.scal[0:1, 6:7], in0=self.scal[0:1, 5:6], scalar1=0.5), self.s_ve)
        P.i("vector", lambda E: E.tensor_tensor(
            out=self.scal[0:1, 7:8], in0=self.scal[0:1, 5:6], in1=self.scal[0:1, 3:4],
            op=ALU.mult), self.s_ve)
        P.selfwait("vector", self.s_ve)
        P.i("vector", lambda E: E.reciprocal(out=self.scal[0:1, 7:8],
                                             in_=self.scal[0:1, 7:8]), self.s_ve)
        P.selfwait("vector", self.s_ve)
        P.i("vector", lambda E: E.tensor_scalar_mul(
            out=self.scal[0:1, 7:8], in0=self.scal[0:1, 7:8], scalar1=4.0), self.s_ve)
        P.selfwait("vector", self.s_ve)
        self.bcast_col(lambda: self.scal[0:1, 6:7], lambda: self.sxh_col[:, :], "sxh2")
        self.bcast_col(lambda: self.scal[0:1, 7:8], lambda: self.sout_col[:, :], "sout2")
        P.i("vector", lambda E: E.tensor_tensor(
            out=self.scal[0:1, 1:2], in0=self.scal[0:1, 5:6], in1=self.scal[0:1, 3:4],
            op=ALU.mult), self.s_ve)
        P.selfwait("vector", self.s_ve)
        P.i("vector", lambda E: E.tensor_scalar_mul(
            out=self.scal[0:1, 1:2], in0=self.scal[0:1, 1:2], scalar1=0.25), self.s_ve)
        P.selfwait("vector", self.s_ve)
        P.i("vector", lambda E: E.tensor_scalar_mul(
            out=self.fbiaspre_row[0:1, :], in0=self.fbias_row[0:1, :],
            scalar1=self.scal[0:1, 1:2]), self.s_ve)
        P.mark("scales", self.s_ve)
        if self.cfg.dbg:
            P.selfwait("vector", self.s_ve)
            P.w("scalar", self.s_ve, P.cur(self.s_ve))
            P.i("scalar", lambda E: E.dma_start(out=self.dbg["scal"],
                                                in_=self.scal[0:1, :]), self.s_vd, 16)

    def p2_early_pre(self, u):
        P, c = self.P, self.cfg
        NT = c.ntiles
        s5, s2 = u % 5, u % 2
        if u >= 5:
            P.wm("sync", f"p2_res_{u - 5}")
        else:
            P.wm("sync", f"av_done_{NT - 1}")
            P.wm("sync", f"lu_ev_{NT - 1}_all")
        P.wm("sync", f"hdma_{u}")
        P.i("sync", lambda E, u=u, s5=s5: E.dma_start(
            out=self.h2_sb[:, s5, 0:512],
            in_=self.d_h_scratch[u * 128:(u + 1) * 128, 0:512]),
            self.s_h2[u % 2], 16)
        if u >= 5:
            P.wm("gpsimd", f"p2_res_{u - 5}")
        else:
            P.wm("gpsimd", f"av_done_{NT - 1}")
            P.wm("gpsimd", f"lu_ev_{NT - 1}_all")
        P.wm("gpsimd", f"hdma_{u}")
        P.i("gpsimd", lambda E, u=u, s5=s5: E.dma_start(
            out=self.h2_sb[:, s5, 512:1024],
            in_=self.d_h_scratch[u * 128:(u + 1) * 128, 512:1024]),
            self.s_gd2, 16)
        P.wm("vector", "scales")
        P.wm("vector", "sxh2")
        if u >= 5:
            P.wm("vector", f"p2_x8_{u - 5}")
        P.selfwait("vector", self.s_ve)
        P.i("vector", lambda E, u=u, s5=s5: E.tensor_scalar_mul(
            out=self.sc_tile[:, s5, 0:1], in0=self.mr2[:, u, 0:1],
            scalar1=self.sxh_col[:, 0:1]), self.s_ve)
        P.i("vector", lambda E, u=u, s5=s5: E.tensor_scalar_mul(
            out=self.sc_tile[:, s5, 1:2], in0=self.mr2[:, u, 1:2],
            scalar1=self.sxh_col[:, 0:1]), self.s_ve)
        P.mark(f"p2_sc_{u}", self.s_ve)
        P.w("scalar", self.s_ve, P.cur(self.s_ve))
        P.w("scalar", self.s_h2[u % 2], P.cur(self.s_h2[u % 2]))
        if u >= 2:
            P.wm("scalar", f"p2_up_{u - 2}")
        else:
            P.wm("scalar", f"mixed_{NT - 1}")
        P.w("scalar", self.s_gd2, P.cur(self.s_gd2))
        P.i("scalar", lambda E, s5=s5, s2=s2: E.activation(
            out=self.x8_sb[:, s2, :], in_=self.h2_sb[:, s5, :], func=AF.Identity,
            scale=self.sc_tile[:, s5, 0:1], bias=self.sc_tile[:, s5, 1:2]), self.s_sc)
        P.mark(f"p2_x8_{u}", self.s_sc)
        P.w("vector", self.s_sc, P.cur(self.s_sc))
        if u >= 2:
            P.wm("vector", f"p2_tr_{u - 2}")
        P.i("vector", lambda E, s2=s2: E.tensor_copy(
            out=self.x8b_sb[:, s2, :], in_=self.x8_sb[:, s2, :]), self.s_ve)
        P.mark(f"p2_up_{u}", self.s_ve)

    def p2_early_pe(self, u):
        P, c = self.P, self.cfg
        NT = c.ntiles
        s2, s4 = u % 2, u % 4
        P.wm("tensor", f"p2_up_{u}")
        if u >= 1:
            P.wm("tensor", f"p2_fm_{u - 1}")
        else:
            P.wm("tensor", f"mixT_ev_{NT - 1}")
        for j in range(8):
            P.i("tensor", lambda E, j=j, s2=s2: E.transpose(
                self.ps_tr[:, j * 128:(j + 1) * 128],
                self.x8b_sb[:, s2, j * 128:(j + 1) * 128], self.ident[:, :]),
                self.s_te)
        P.mark(f"p2_tr_{u}", self.s_te)
        P.w("vector", self.s_te, P.cur(self.s_te))
        if u == 0:
            P.wm("vector", f"av_done_{NT - 1}")
            P.wm("vector", f"b_ln_{NT}")
        if u >= 4:
            P.wm("vector", f"p2_mm_{u - 4}")
        P.i("vector", lambda E, s4=s4: E.tensor_copy(
            out=self.x8fm_sb[:, s4, :, :], in_=self.ps_tr[:, :]), self.s_ve)
        P.mark(f"p2_fm_{u}", self.s_ve)

    def p2_late(self, t):
        P, c = self.P, self.cfg
        NT = c.ntiles
        s4, s5, s2 = t % 4, t % 5, t % 2
        psf = self.ps_b if t % 2 == 0 else self.ps_c
        P.wm("tensor", f"p2_fm_{t}")
        if t == 0:
            P.wm("tensor", "w8t")
            P.wm("tensor", "sout2")
            P.wm("tensor", "scales")
            P.wm("tensor", f"res1_{NT - 1}")
        elif t == 1:
            P.wm("tensor", f"amax_{NT - 1}")
        if t >= 2:
            P.wm("tensor", f"p2_ffn_{t - 2}")
        for half in range(2):
            P.i("tensor", lambda E, half=half, psf=psf: E.matmul(
                psf[:, half * 512:(half + 1) * 512], self.ones1[0:1, :],
                self.fbiaspre_row[0:1, half * 512:(half + 1) * 512],
                start=True, stop=False), self.s_te)
            for dc in range(8):
                P.i("tensor", lambda E, dc=dc, half=half, psf=psf, s4=s4: E.matmul(
                    psf[:, half * 512:(half + 1) * 512],
                    self.x8fm_sb[:, s4, dc, :],
                    self.w8t[:, dc, half * 512:(half + 1) * 512],
                    start=False, stop=(dc == 7)), self.s_te)
        P.mark(f"p2_mm_{t}", self.s_te)
        P.w("scalar", self.s_te, P.cur(self.s_te))
        if t < 2:
            P.wm("scalar", "sout2")
            P.wm("scalar", f"mixed_{NT - 1}")
        else:
            P.wm("scalar", f"p2_res_{t - 2}")
        P.i("scalar", lambda E, s2=s2, psf=psf: E.activation(
            out=self.ffn_sb[:, s2, :], in_=psf[:, :], func=AF.Copy,
            scale=self.sout_col[:, 0:1]), self.s_sc)
        P.mark(f"p2_ffn_{t}", self.s_sc)
        P.w("vector", self.s_sc, P.cur(self.s_sc))
        if t >= 2:
            P.wm("vector", f"odma_{t - 2}")
            P.wm("vector", f"odmav_{t - 2}")
        P.selfwait("vector", self.s_ve)
        P.i("vector", lambda E, s5=s5, s2=s2: E.tensor_tensor(
            out=self.outt_sb[:, s2, :], in0=self.h2_sb[:, s5, :],
            in1=self.ffn_sb[:, s2, :], op=ALU.add), self.s_ve)
        P.mark(f"p2_res_{t}", self.s_ve)
        P.selfwait("vector", self.s_ve)
        P.w("scalar", self.s_ve, P.cur(self.s_ve))
        P.i("scalar", lambda E, t=t, s2=s2: E.dma_start(
            out=self.d_out[t * 128:(t + 1) * 128, 0:512],
            in_=self.outt_sb[:, s2, 0:512]), self.s_out[s2], 16)
        P.mark(f"odma_{t}", self.s_out[s2])
        P.w("gpsimd", self.s_ve, P.cur(self.s_ve))
        P.i("gpsimd", lambda E, t=t, s2=s2: E.dma_start(
            out=self.d_out[t * 128:(t + 1) * 128, 512:1024],
            in_=self.outt_sb[:, s2, 512:1024]), self.s_vd, 16)
        P.mark(f"odmav_{t}", self.s_vd)

    def build(self):
        c = self.cfg
        P = self.P
        NT = c.ntiles
        self.declare()
        self.prologue_early()
        self.p1a()
        self.wamax_finish()
        self.ag_start()
        self.prologue_late()
        P.i("vector", lambda E: E.memset(self.amax_acc[:, :], 0.0), self.s_ve)
        self.chain_tile(0, NT, pstr_wait="bias_cols")
        return self.main_loop()

    def kh_block(self):
        P, c = self.P, self.cfg
        P.wm("tensor", f"b_fm_0")
        P.wm("tensor", "v_4")
        for j in range(8):
            if j >= 2:
                P.wm("tensor", f"kh_{j - 2}")
            for dc in range(8):
                P.i("tensor", lambda E, j=j, dc=dc: E.matmul(
                    self.ps_a[:, (j % 2) * 512:(j % 2) * 512 + 128],
                    self.qkv_wb[:, dc, (8 + j) * 128:(9 + j) * 128],
                    self.lnfmh_sb[:, dc, :], start=(dc == 0), stop=(dc == 7)), self.s_te)
            P.w("scalar", self.s_te, P.cur(self.s_te))
            P.i("scalar", lambda E, j=j: E.activation(
                out=self.kh_sb[:, j, :], in_=self.ps_a[:, (j % 2) * 512:(j % 2) * 512 + 128],
                func=AF.Identity, bias=self.qkb_col[:, 8 + j:9 + j]), self.s_sc)
            P.mark(f"kh_{j}", self.s_sc)
        P.mark("kh", self.s_sc)
        if self.cfg.dbg:
            P.selfwait("scalar", self.s_sc)
            for j in range(8):
                P.i("scalar", lambda E, j=j: E.dma_start(
                    out=self.dbg["k"][j * 128:(j + 1) * 128, c.s_core:c.s_core + 128],
                    in_=self.kh_sb[:, j, :]), self.s_sd, 16)

    def main_loop(self):
        c = self.cfg
        P = self.P
        NT = c.ntiles
        for g in range(c.ngroups):
            if g == 0:
                self.chain_tile(1, 0)
                self.chain_tile(2, 1)
                self.ag_finish()
                self.chain_tile(3, 2, pstr_wait="pooled_fm",
                                pss_wait="rv_ps_done")
                self.chain_tile(4, 3)
                self.kh_block()
                for cc in range(8):
                    self.w8t_chunk(cc, pstr_wait="b_fm_4")
            else:
                for t in range(4):
                    self.chain_tile(4 * g + t + 1, 4 * g + t)
            gates = [] if g == 0 else [4 * (g - 1) + t for t in range(4)]
            self.qk_group(g, gates)
            self.rq_group(g)
            for t in range(4):
                self.local_qtile(4 * g + t)
            self.latent_group(g)
            P.selfwait("vector", self.s_ve)
            P.i("vector", lambda E: E.tensor_copy(
                out=self.kprev_sb[:, :, :],
                in_=AP(self.k_sb, 384, [[8 * 512, 128], [512, 8], [1, 128]])),
                self.s_ve)
            P.mark(f"kprev_{g}", self.s_ve)
        for t in range(4):
            self.gate_outw_res(4 * (c.ngroups - 1) + t)
        P.mark("hw_all", self.s_hw)
        self.barrier()
        self.p2_early_pre(0)
        self.p2_early_pre(1)
        for u in range(4):
            self.p2_early_pe(u)
            if u + 2 < 5:
                self.p2_early_pre(u + 2)
        for t in range(NT):
            self.p2_late(t)
            if t + 5 < NT:
                self.p2_early_pre(t + 5)
            if t + 4 < NT:
                self.p2_early_pe(t + 4)
        for s in [self.s_vd, self.s_sy, self.s_gd, self.s_gd2, self.s_gd3, self.s_sd,
                  self.s_ve, self.s_sc, self.s_te, self.s_gp, self.s_xa0,
                  self.s_xa1, self.s_xl0, self.s_xl1, self.s_xr, self.s_h20,
                  self.s_h21, self.s_hw, self.s_out0, self.s_out1, self.s_stg]:
            P.selfwait("gpsimd", s)
        with self.nc.Block() as block:
            P.emit(block)
        self.ctx.pop_all()
        return self.nc


def build(cfg: Cfg):
    return MK(cfg).build()


S_CORE = 4096
S_SEQ = 16384
B = 2


def _core_inputs(xb, wd, core):
    b, c = core // 4, core % 4
    xs = np.ascontiguousarray(xb[b, c * S_CORE:(c + 1) * S_CORE])
    if c == 0:
        xh = np.zeros((128, D), np.float32)
        hv = 0.0
    else:
        xh = np.ascontiguousarray(xb[b, c * S_CORE - 128:c * S_CORE])
        hv = 1.0
    lm = (np.arange(NLAT)[:, None] < (16 * c + np.arange(16))[None, :])
    m = {"x": xs, "xh": xh, "latmask": lm.astype(np.float32),
         "halo_valid": np.array([[hv]], np.float32)}
    m.update(wd)
    return m


def kernel(**inputs):
    f32 = np.float32
    xb = np.ascontiguousarray(np.asarray(inputs["L0_attn_x"], f32))
    wd = {
        "qkv_w": np.ascontiguousarray(np.asarray(inputs["L0_attn_qkv_w"], f32)),
        "qkv_b": np.ascontiguousarray(np.asarray(inputs["L0_attn_qkv_b"], f32)),
        "rq_w": np.ascontiguousarray(np.asarray(inputs["L0_attn_rq_w"], f32)),
        "rq_b": np.ascontiguousarray(np.asarray(inputs["L0_attn_rq_b"], f32)),
        "rkv_w": np.ascontiguousarray(np.asarray(inputs["L0_attn_rkv_w"], f32)),
        "rkv_b": np.ascontiguousarray(np.asarray(inputs["L0_attn_rkv_b"], f32)),
        "rout_w": np.ascontiguousarray(np.asarray(inputs["L0_attn_rout_w"], f32)),
        "rout_b": np.ascontiguousarray(np.asarray(inputs["L0_attn_rout_b"], f32)),
        "out_w": np.ascontiguousarray(np.asarray(inputs["L0_attn_out_w"], f32)),
        "out_b": np.ascontiguousarray(np.asarray(inputs["L0_attn_out_b"], f32)),
        "gl_w": np.ascontiguousarray(np.asarray(inputs["L0_attn_gl_w"], f32)),
        "gr_w": np.ascontiguousarray(np.asarray(inputs["L0_attn_gr_w"], f32)),
        "ffn_w": np.ascontiguousarray(np.asarray(inputs["L0_ffn_weight"], f32)),
        "ffn_b": np.ascontiguousarray(np.asarray(inputs["L0_ffn_bias"], f32)),
    }
    nc = build(Cfg(s_core=S_CORE, dbg=False))
    in_maps = [_core_inputs(xb, wd, core) for core in range(8)]
    from concourse.bass_utils import run_bass_kernel_spmd
    res = run_bass_kernel_spmd(nc, in_maps, core_ids=list(range(8)))
    out = np.empty((B, S_SEQ, D), f32)
    for core in range(8):
        b, c = core // 4, core % 4
        out[b, c * S_CORE:(c + 1) * S_CORE] = res.results[core]["out"]
    return out


# version 15 (final)
# speedup vs baseline: 1.1356x; 1.0426x over previous
from contextlib import ExitStack

import numpy as np
import concourse.bass as bass
import concourse.mybir as mybir

F32 = mybir.dt.float32
BF16 = mybir.dt.bfloat16
FP8 = mybir.dt.float8e4
AF = mybir.ActivationFunctionType
ALU = mybir.AluOpType
AX = mybir.AxisListType
AP = bass.AP

D = 1024
H = 16
HD = 64
LH = 4
LHD = 64
LD = 256
NLAT = 64
EPS = 1e-5
FP8MAX = 448.0
RG = [[0, 1, 2, 3, 4, 5, 6, 7]]


class Cfg:
    def __init__(self, s_core=4096, dbg=False):
        self.s_core = s_core
        self.s_seq = 4 * s_core
        self.chunk = self.s_seq // NLAT
        self.ntiles = s_core // 128
        self.ngroups = self.ntiles // 4
        self.ch_per_grp = 512 // self.chunk
        self.dbg = dbg
        assert self.chunk * NLAT == self.s_seq
        assert self.ntiles % 4 == 0


PARANOID = False


class Prog:
    ENGINES = ("sync", "gpsimd", "tensor", "vector", "scalar")

    def __init__(self):
        self.ops = []
        self.cnt = {}
        self.marks = {}
        self.all_sems = {}

    def _bump(self, sem, k):
        self.cnt[sem.name] = self.cnt.get(sem.name, 0) + k
        return self.cnt[sem.name]

    def i(self, eng, fn, sem=None, k=1):
        if PARANOID:
            snap = dict(self.cnt)
            self.ops.append((eng, "psnap", snap))
        self.ops.append((eng, "inst", (fn, sem, k)))
        if sem is not None:
            self.all_sems[sem.name] = sem
            return self._bump(sem, k)

    def w(self, eng, sem, val):
        if val > 0:
            self.ops.append((eng, "wait", (sem, val)))

    def mark(self, name, sem):
        self.marks[name] = (sem, self.cnt.get(sem.name, 0))

    def wm(self, eng, name):
        sem, val = self.marks[name]
        self.w(eng, sem, val)

    def cur(self, sem):
        return self.cnt.get(sem.name, 0)

    def selfwait(self, eng, sem):
        self.w(eng, sem, self.cur(sem))

    def emit(self, block):
        by_eng = {e: [] for e in self.ENGINES}
        for eng, kind, payload in self.ops:
            by_eng[eng].append((kind, payload))

        def run(E, ops):
            waited = {}
            for kind, payload in ops:
                if kind == "psnap":
                    for nm, val in payload.items():
                        if val > 0 and waited.get(nm, -1) < val:
                            E.wait_ge(self.all_sems[nm], val)
                            waited[nm] = val
                    continue
                if kind == "wait":
                    sem, val = payload
                    if waited.get(sem.name, -1) >= val:
                        continue
                    E.wait_ge(sem, val)
                    waited[sem.name] = val
                else:
                    fn, sem, k = payload
                    instr = fn(E)
                    if sem is not None:
                        instr.then_inc(sem, k)

        for eng in self.ENGINES:
            if by_eng[eng]:
                getattr(block, eng)(lambda E, ops=by_eng[eng]: run(E, ops))


class MK:
    def __init__(self, cfg: Cfg):
        self.cfg = cfg
        self.nc = bass.Bass("TRN2", target_bir_lowering=False,
                            dynamic_dma_scratch_size=16384)
        self.P = Prog()
        self.ctx = ExitStack()

    def bcast_col(self, cell_ap_fn, dst_col_fn, tag):
        P = self.P
        P.wm("tensor", "consts")
        P.w("tensor", self.s_ve, P.cur(self.s_ve))
        P.i("tensor", lambda E: E.matmul(
            self.ps_s[:, 30:31], self.ones1f[0:1, :], cell_ap_fn(),
            start=True, stop=True), self.s_te)
        P.w("vector", self.s_te, P.cur(self.s_te))
        P.i("vector", lambda E: E.tensor_copy(out=dst_col_fn(),
                                              in_=self.ps_s[:, 30:31]), self.s_ve)
        P.mark(tag, self.s_ve)

    def lp(self, f):
        with self.nc.allow_low_precision("bf16 gate-weight reduction"):
            return f()

    def sb(self, name, shape, dtype):
        t = self.ctx.enter_context(self.nc.sbuf_tensor(name, shape, dtype))
        setattr(self, name, t)
        return t

    def ps(self, name, shape, dtype):
        t = self.ctx.enter_context(self.nc.psum_tensor(name, shape, dtype))
        setattr(self, name, t)
        return t

    def sem(self, name):
        s = self.ctx.enter_context(self.nc.semaphore(name))
        setattr(self, name, s)
        return s

    def dram(self, name, shape, dtype=F32, kind="ExternalInput", **kw):
        ap = self.nc.dram_tensor(name, shape, dtype, kind=kind, **kw).ap()
        setattr(self, "d_" + name, ap)
        return ap

    def declare(self):
        c = self.cfg
        S, NT = c.s_core, c.ntiles
        self.dram("x", [S, D])
        self.dram("xh", [128, D])
        self.dram("qkv_w", [D, 3 * D]); self.dram("qkv_b", [3 * D])
        self.dram("rq_w", [D, LD]); self.dram("rq_b", [LD])
        self.dram("rkv_w", [D, 2 * LD]); self.dram("rkv_b", [2 * LD])
        self.dram("rout_w", [LD, D]); self.dram("rout_b", [D])
        self.dram("out_w", [D, D]); self.dram("out_b", [D])
        self.dram("gl_w", [HD, 1]); self.dram("gr_w", [HD, 1])
        self.dram("ffn_w", [D, D]); self.dram("ffn_b", [D])
        self.dram("latmask", [NLAT, 16])
        self.dram("halo_valid", [1, 1])
        self.dram("out", [S, D], kind="ExternalOutput")
        self.dram("h_scratch", [S, D], kind="Internal")
        self.d_ccp_in = self.nc.dram_tensor("ccp_in", [16, D], BF16, kind="Internal").ap()
        self.d_ccp_out = self.nc.dram_tensor("ccp_out", [128, D], BF16, kind="Internal", addr_space="Shared").ap()
        self.dram("cca_in", [1, 128], kind="Internal")
        self.dram("cca_out", [1, 128], kind="Internal", addr_space="Shared")
        self.dbg = {}
        if c.dbg:
            for nm, shape, dt_ in [
                ("ln1", [S + 128, D], BF16), ("q", [D, S], BF16),
                ("k", [D, S + 128], BF16), ("v", [S + 128, H * 66], BF16),
                ("lu", [S, H * 66], BF16), ("remote", [S, D], BF16),
                ("mixed", [S, D], BF16), ("h", [S, D], F32),
                ("scal", [1, 8], F32), ("rq", [LD, S], BF16),
                ("rk", [LD, NLAT], BF16), ("rv", [NLAT, LH * 65], BF16),
                ("latn", [S, LD], BF16), ("pooled", [128, D], BF16),
                ("x8", [S, D], BF16), ("rgr", [S, H], F32),
                ("ffn", [S, D], BF16), ("elat", [NLAT, LH * S], BF16),
                ("probe", [128, 2048], F32),
            ]:
                self.dbg[nm] = self.nc.dram_tensor(
                    "dbg_" + nm, shape, dt_, kind="ExternalOutput").ap()

        NTt = NT
        self.sb("qkv_wb", [128, 8, 3 * D], BF16)
        self.sb("rq_wb", [128, 8, LD], BF16)
        self.sb("rkv_wb", [128, 8, 2 * LD], BF16)
        self.sb("rout_wb", [128, 2, 1040], BF16)
        self.sb("out_wb", [128, 8, D], BF16)
        self.sb("wvgl", [128, 8, H], BF16)
        self.sb("w8t", [128, 8, D], BF16)
        self.sb("qkb_col", [128, 16], F32)
        self.sb("rqb_col", [128, 2], F32)
        self.sb("rkb_col", [128, 2], F32)
        self.sb("vb_row", [1, D], BF16)
        self.sb("rvb_row", [1, LD], BF16)
        self.sb("rqb_row", [1, LD], BF16)
        self.sb("rkb_row", [1, LD], BF16)
        self.sb("routb_row", [1, 1040], BF16)
        self.sb("outb_row", [1, D], BF16)
        self.sb("bgl_row", [1, H], BF16)
        self.sb("fbias_row", [1, D], BF16)
        self.sb("glrep", [128, HD], BF16)
        self.sb("pid_sb", [1, 2], mybir.dt.uint32)
        self.sb("grrep", [128, HD], BF16)
        self.sb("glrow", [1, HD], BF16)
        self.sb("grrow", [1, HD], BF16)
        self.sb("maskAB", [128, 512], BF16)
        self.sb("maskAB0", [128, 512], BF16)
        self.sb("a_all", [128, NTt, 16], BF16)
        self.sb("latmask_sb", [128, 16], BF16)
        self.sb("ident", [128, 128], BF16)
        self.sb("ones1", [1, 128], BF16)
        self.sb("ones1f", [1, 128], F32)
        self.sb("cpool", [128, 8], F32)
        cvals = [0.0, 1.0, EPS, -0.5, 0.125, -1.0]
        self.cvals = cvals
        for j, v in enumerate(cvals):
            self.nc.const_aps.register(F32, v, self.cpool[:, j:j + 1])
        self.sb("hval_col", [128, 1], F32)
        self.sb("mr1", [128, NTt + 1, 2], F32)
        self.sb("mr2", [128, NTt, 2], F32)
        self.sb("amax_acc", [128, 1], F32)
        self.sb("wamax_col", [128, 1], F32)
        self.sb("scal", [1, 8], F32)
        self.sb("sxh_col", [128, 1], F32)
        self.sb("sout_col", [128, 1], F32)
        self.sb("sc_tile", [128, 5, 2], F32)
        self.sb("mv_tmp", [128, 2], F32)
        self.sb("bn_buf", [128, 12], F32)
        self.sb("rowmax_sb", [128, 1], F32)
        regP = self.nc.alloc_sbuf_tensor("regP", [128, 2048], mybir.dt.uint8)
        baseP = self.nc.lookup_mloc(regP).addr
        self.pooled_all = self.nc.alloc_sbuf_tensor_at(
            "pooled_all", [64, D], BF16, offset=baseP)
        self.pool_sb = self.nc.alloc_sbuf_tensor_at(
            "pool_sb", [16, D], BF16, offset=baseP)
        self.sb("pooled_fm", [128, 8, NLAT], BF16)
        self.sb("rk_fm", [128, 2, NLAT], BF16)
        self.sb("rv_aug", [128, LH, 65], BF16)
        regA = self.nc.alloc_sbuf_tensor("regA", [128, 16640], mybir.dt.uint8)
        baseA = self.nc.lookup_mloc(regA).addr

        def at(name, shape, dtype, off):
            t = self.nc.alloc_sbuf_tensor_at(name, shape, dtype, offset=off)
            setattr(self, name, t)
            return t

        at("xa_sb", [128, 2, D], F32, baseA)
        at("ln1a_sb", [128, 3, D], BF16, baseA + 8192)
        at("lu_sb", [128, 4, H, 66], BF16, baseA)
        at("remote_sb", [128, 4, D], BF16, baseA + 8448)
        at("outt_sb", [128, 2, D], F32, baseA)
        at("ffn_sb", [128, 2, D], BF16, baseA + 8192)
        at("x8_sb", [128, 2, D], FP8, baseA + 12288)
        regB = self.nc.alloc_sbuf_tensor("regB", [128, 24576], mybir.dt.uint8)
        baseB = self.nc.lookup_mloc(regB).addr
        at("qkvb_row", [1, 3 * D], BF16, baseB)
        at("tmp_row", [1, D], BF16, baseB + 6144)
        at("wstage", [128, D], F32, baseB + 8192)
        at("wtmpb", [128, D], BF16, baseB + 12288)
        at("wtmp8", [128, D], FP8, baseB + 14336)
        self.sb("amax_row", [1, 128], F32)
        at("q_sb", [128, 8, 512], BF16, baseB)
        at("k_sb", [128, 8, 512], BF16, baseB + 8192)
        at("e_sb", [128, 1, 4096], BF16, baseB + 16384)
        at("h2_sb", [128, 5, D], F32, baseB)
        regC = self.nc.alloc_sbuf_tensor("regC", [128, 4096], mybir.dt.uint8)
        baseC = self.nc.lookup_mloc(regC).addr
        at("lnfmh_sb", [128, 8, 128], BF16, baseC)
        at("kh_sb", [128, 8, 128], BF16, baseC + 2048)
        at("mixed_sb", [128, 1, D], BF16, baseC)
        at("mixt_sb", [128, D], BF16, baseC + 2048)
        at("x8b_sb", [128, 2, D], BF16, baseC)
        regD = self.nc.alloc_sbuf_tensor("regD", [128, 16384], mybir.dt.uint8)
        baseD = self.nc.lookup_mloc(regD).addr
        at("xl_sb", [128, 2, D], F32, baseD)
        at("xr_sb", [128, 1, D], F32, baseD + 8192)
        at("h_sb", [128, 1, D], F32, baseD + 12288)
        at("x8fm_sb", [128, 4, 8, 128], BF16, baseD)
        self.sb("ln1b_sb", [128, 2, D], BF16)
        self.sb("lnfm_sb", [128, 8, 512], BF16)
        self.sb("kprev_sb", [128, 8, 128], BF16)
        self.sb("rq_sb", [128, 2, 512], BF16)
        self.sb("v_ring", [128, 5, H, 66], BF16)
        self.sb("elat_sb", [128, 2, 512], BF16)
        self.sb("latn_sb", [128, 4, LH, LHD], BF16)
        at("latfm_sb", [128, 2, 512], BF16, baseP)
        at("fbiaspre_row", [1, D], BF16, baseA + 14336)
        self.sb("dinv_sb", [128, 4, LH], F32)
        self.sb("dtmp", [128, 4], F32)
        self.sb("gsc", [128, 6, H], F32)
        self.sb("rgr_sb", [128, 4, H], F32)

        self.ps("ps_tr", [128, 1024], BF16)
        self.ps("ps_a", [128, 1024], F32)
        self.ps("ps_b", [128, 1024], F32)
        self.ps("ps_c", [128, 1024], F32)
        self.ps("ps_s", [128, 512], F32)

        for s in ["s_sy", "s_gd", "s_gd2", "s_gd3", "s_gd4", "s_gp", "s_te", "s_sc", "s_ve",
                  "s_vd", "s_sd", "s_xa0", "s_xa1", "s_xl0", "s_xl1", "s_xr",
                  "s_h20", "s_h21", "s_hw", "s_out0", "s_out1", "s_stg"]:
            self.sem(s)
        self.s_xa = [self.s_xa0, self.s_xa1]
        self.s_xl = [self.s_xl0, self.s_xl1]
        for s in ["s_h22", "s_h23", "s_h24"]:
            self.sem(s)
        self.s_h2 = [self.s_h20, self.s_h21, self.s_h22, self.s_h23, self.s_h24]
        self.s_out = [self.s_out0, self.s_out1]

    def prologue_early(self):
        P, c = self.P, self.cfg
        nc = self.nc
        CH = c.chunk
        NT = c.ntiles

        from concourse.hw_specs import get_activation_tables
        tabs = list(get_activation_tables(nc.m.arch).keys())
        set_id = tabs.index("natural_log_exp_and_others")
        P.i("scalar", lambda E: E.add_instruction(mybir.InstLoadActFuncSet(
            name=nc.get_next_instruction_name(), act_func_set_id=set_id,
            ins=[], outs=[])))

        for cc in range(8):
            P.i("gpsimd", lambda E, cc=cc: E.dma_start(
                out=self.qkv_wb[:, cc, :], in_=self.d_qkv_w[cc * 128:(cc + 1) * 128, :]), self.s_gd, 16)
        for cc in range(8):
            P.i("gpsimd", lambda E, cc=cc: E.dma_start(
                out=self.rq_wb[:, cc, :], in_=self.d_rq_w[cc * 128:(cc + 1) * 128, :]), self.s_gd, 16)
            P.i("gpsimd", lambda E, cc=cc: E.dma_start(
                out=self.rkv_wb[:, cc, :], in_=self.d_rkv_w[cc * 128:(cc + 1) * 128, :]), self.s_gd, 16)
        rows = [
            (self.qkvb_row[0:1, :], AP(self.d_qkv_b.tensor, 0, [[3 * D, 1], [1, 3 * D]])),
            (self.rqb_row[0:1, :], AP(self.d_rq_b.tensor, 0, [[LD, 1], [1, LD]])),
            (self.rkb_row[0:1, :], AP(self.d_rkv_b.tensor, 0, [[LD, 1], [1, LD]])),
            (self.rvb_row[0:1, :], AP(self.d_rkv_b.tensor, LD, [[LD, 1], [1, LD]])),
            (self.routb_row[0:1, 0:D], AP(self.d_rout_b.tensor, 0, [[D, 1], [1, D]])),
            (self.outb_row[0:1, :], AP(self.d_out_b.tensor, 0, [[D, 1], [1, D]])),
            (self.fbias_row[0:1, :], AP(self.d_ffn_b.tensor, 0, [[D, 1], [1, D]])),
            (self.glrow[0:1, :], AP(self.d_gl_w.tensor, 0, [[HD, 1], [1, HD]])),
            (self.grrow[0:1, :], AP(self.d_gr_w.tensor, 0, [[HD, 1], [1, HD]])),
            (self.glrep[:, :], AP(self.d_gl_w.tensor, 0, [[0, 128], [1, HD]])),
            (self.grrep[:, :], AP(self.d_gr_w.tensor, 0, [[0, 128], [1, HD]])),
            (self.latmask_sb[0:64, :], self.d_latmask),
            (self.latmask_sb[64:128, :], self.d_latmask),
            (self.hval_col[:, :], AP(self.d_halo_valid.tensor, 0, [[0, 128], [1, 1]])),
        ]
        for out, in_ in rows:
            P.i("gpsimd", lambda E, o=out, i=in_: E.dma_start(
                out=o, in_=i, allow_slow_non_contiguous=True), self.s_gd, 16)
        P.mark("wdma", self.s_gd)

        P.i("gpsimd", lambda E: E.memset(self.ident[:, :], 1.0), self.s_gp)
        P.i("gpsimd", lambda E: E.memset(self.ones1[0:1, :], 1.0), self.s_gp)
        P.i("gpsimd", lambda E: E.memset(self.ones1f[0:1, :], 1.0), self.s_gp)
        for j, v in enumerate(self.cvals):
            P.i("gpsimd", lambda E, j=j, v=v: E.memset(self.cpool[:, j:j + 1], v),
                self.s_gp)
        P.i("gpsimd", lambda E: E.memset(self.maskAB[:, :], 1.0), self.s_gp)
        P.i("gpsimd", lambda E: E.memset(self.a_all[:, :, :], 1.0 / CH), self.s_gp)
        P.selfwait("gpsimd", self.s_gp)
        P.i("gpsimd", lambda E: E.affine_select(
            out=self.ident[:, :], in_=self.ident[:, :], compare_op=ALU.is_ge,
            fill=0.0, base=0, pattern=[[-1, 128]], channel_multiplier=1), self.s_gp)
        P.selfwait("gpsimd", self.s_gp)
        P.i("gpsimd", lambda E: E.affine_select(
            out=self.ident[:, :], in_=self.ident[:, :], compare_op=ALU.is_ge,
            fill=0.0, base=0, pattern=[[1, 128]], channel_multiplier=-1), self.s_gp)
        P.i("gpsimd", lambda E: E.affine_select(
            out=AP(self.maskAB, 0, [[512, 128], [256, 2], [1, 128]]),
            in_=AP(self.maskAB, 0, [[512, 128], [256, 2], [1, 128]]),
            compare_op=ALU.is_ge, fill=0.0, base=0,
            pattern=[[0, 2], [-1, 128]], channel_multiplier=1), self.s_gp)
        P.selfwait("gpsimd", self.s_gp)
        P.i("gpsimd", lambda E: E.affine_select(
            out=AP(self.maskAB, 128, [[512, 128], [256, 2], [1, 128]]),
            in_=AP(self.maskAB, 128, [[512, 128], [256, 2], [1, 128]]),
            compare_op=ALU.is_gt, fill=0.0, base=0,
            pattern=[[0, 2], [1, 128]], channel_multiplier=-1), self.s_gp)
        P.i("gpsimd", lambda E: E.affine_select(
            out=self.a_all[:, :, :], in_=self.a_all[:, :, :], compare_op=ALU.is_ge,
            fill=0.0, base=0, pattern=[[128, NT], [-CH, 16]], channel_multiplier=1),
            self.s_gp)
        P.selfwait("gpsimd", self.s_gp)
        P.i("gpsimd", lambda E: E.affine_select(
            out=self.a_all[:, :, :], in_=self.a_all[:, :, :], compare_op=ALU.is_ge,
            fill=0.0, base=(CH - 1), pattern=[[-128, NT], [CH, 16]],
            channel_multiplier=-1), self.s_gp)
        P.mark("consts", self.s_gp)
        P.wm("scalar", "consts")

    def wamax_start(self):
        P = self.P
        P.i("vector", lambda E: E.memset(self.wamax_col[:, :], 0.0), self.s_ve)
        for cc in range(2):
            P.i("gpsimd", lambda E, cc=cc: E.dma_start(
                out=self.wam_buf(cc), in_=self.d_ffn_w[cc * 128:(cc + 1) * 128, :]),
                self.s_gd2, 16)
            P.mark(f"wam_d_{cc}", self.s_gd2)

    def wam_buf(self, cc):
        return self.wstage[:, :] if cc % 2 == 0 else self.xl_sb[:, 0, :]

    def wamax_reduce(self, cc):
        P = self.P
        P.w("vector", self.s_gd2, P.cur(self.s_gd2))
        P.selfwait("vector", self.s_ve)
        P.i("vector", lambda E, cc=cc: E.reduce_max(
            out=self.rowmax_sb[:, 0:1], in_=self.wam_buf(cc), axis=AX.X,
            apply_absolute_value=True), self.s_ve)
        P.selfwait("vector", self.s_ve)
        P.i("vector", lambda E: E.tensor_max(
            out=self.wamax_col[:, :], in0=self.wamax_col[:, :],
            in1=self.rowmax_sb[:, 0:1]), self.s_ve)
        P.mark(f"wam_rd_{cc}", self.s_ve)
        if cc + 2 < 8:
            P.wm("gpsimd", f"wam_rd_{cc}")
            P.i("gpsimd", lambda E, cc=cc: E.dma_start(
                out=self.wam_buf(cc + 2),
                in_=self.d_ffn_w[(cc + 2) * 128:(cc + 3) * 128, :]), self.s_gd2, 16)
            P.mark(f"wam_d_{cc + 2}", self.s_gd2)

    def wamax_finish(self):
        P = self.P
        P.selfwait("vector", self.s_ve)
        P.w("scalar", self.s_ve, P.cur(self.s_ve))
        P.i("scalar", lambda E: E.dma_start(out=self.amax_row[0:1, :],
                                            in_=self.wamax_col[:, :]), self.s_stg, 16)
        P.w("vector", self.s_stg, P.cur(self.s_stg))
        P.i("vector", lambda E: E.reduce_max(
            out=self.scal[0:1, 2:3], in_=self.amax_row[0:1, :], axis=AX.X), self.s_ve)
        P.selfwait("vector", self.s_ve)
        P.i("vector", lambda E: E.tensor_scalar_max(
            out=self.scal[0:1, 2:3], in0=self.scal[0:1, 2:3], scalar1=1e-12), self.s_ve)
        P.selfwait("vector", self.s_ve)
        P.i("vector", lambda E: E.reciprocal(out=self.scal[0:1, 3:4],
                                             in_=self.scal[0:1, 2:3]), self.s_ve)
        P.selfwait("vector", self.s_ve)
        P.i("vector", lambda E: E.tensor_scalar(
            out=self.scal[0:1, 3:4], in0=self.scal[0:1, 3:4], scalar1=FP8MAX,
            scalar2=FP8MAX, op0=ALU.mult, op1=ALU.min), self.s_ve)
        P.selfwait("vector", self.s_ve)
        P.i("vector", lambda E: E.tensor_scalar_mul(
            out=self.scal[0:1, 4:5], in0=self.scal[0:1, 3:4], scalar1=0.5), self.s_ve)
        P.selfwait("vector", self.s_ve)
        self.bcast_col(lambda: self.scal[0:1, 4:5], lambda: self.sxh_col[:, :], "swh")
        for cc in range(8):
            P.i("gpsimd", lambda E, cc=cc: E.dma_start(
                out=self.out_wb[:, cc, :], in_=self.d_out_w[cc * 128:(cc + 1) * 128, :]), self.s_gd3, 16)
        for cc in range(2):
            P.i("gpsimd", lambda E, cc=cc: E.dma_start(
                out=self.rout_wb[:, cc, 0:D], in_=self.d_rout_w[cc * 128:(cc + 1) * 128, :]), self.s_gd3, 16)
        P.mark("wdma2", self.s_gd3)

    def prologue_late(self):
        P, c = self.P, self.cfg
        NT = c.ntiles

        P.wm("vector", "consts")
        P.wm("vector", "wdma")
        P.i("vector", lambda E: E.tensor_copy(out=self.maskAB0[:, :], in_=self.maskAB[:, :]), self.s_ve)
        P.selfwait("vector", self.s_ve)
        P.i("vector", lambda E: E.tensor_scalar_mul(
            out=AP(self.maskAB0, 0, [[512, 128], [256, 2], [1, 128]]),
            in0=AP(self.maskAB0, 0, [[512, 128], [256, 2], [1, 128]]),
            scalar1=self.hval_col[:, 0:1]), self.s_ve)

        for cc in range(8):
            P.selfwait("vector", self.s_ve)
            P.i("vector", lambda E, cc=cc: E.tensor_tensor(
                out=AP(self.wtmpb, 0, [[D, 128], [HD, H], [1, HD]]),
                in0=AP(self.qkv_wb, cc * 3 * D + 2 * D,
                       [[8 * 3 * D, 128], [HD, H], [1, HD]]),
                in1=AP(self.glrep, 0, [[HD, 128], [0, H], [1, HD]]),
                op=ALU.mult), self.s_ve)
            P.selfwait("vector", self.s_ve)
            P.i("vector", lambda E, cc=cc: self.lp(lambda: E.reduce_sum(
                out=self.wvgl[:, cc, :],
                in_=AP(self.wtmpb, 0, [[D, 128], [HD, H], [1, HD]]),
                axis=AX.X)), self.s_ve)
        P.wm("vector", "wdma2")
        for cc in range(2):
            P.selfwait("vector", self.s_ve)
            P.i("vector", lambda E, cc=cc: E.tensor_tensor(
                out=AP(self.wtmpb, 0, [[D, 128], [HD, H], [1, HD]]),
                in0=AP(self.rout_wb, cc * 1040, [[2 * 1040, 128], [HD, H], [1, HD]]),
                in1=AP(self.grrep, 0, [[HD, 128], [0, H], [1, HD]]),
                op=ALU.mult), self.s_ve)
            P.selfwait("vector", self.s_ve)
            P.i("vector", lambda E, cc=cc: self.lp(lambda: E.reduce_sum(
                out=self.rout_wb[:, cc, D:D + H],
                in_=AP(self.wtmpb, 0, [[D, 128], [HD, H], [1, HD]]),
                axis=AX.X)), self.s_ve)
        P.i("vector", lambda E: E.tensor_tensor(
            out=AP(self.tmp_row, 0, [[D, 1], [HD, H], [1, HD]]),
            in0=AP(self.qkvb_row, 2 * D, [[3 * D, 1], [HD, H], [1, HD]]),
            in1=AP(self.glrow, 0, [[HD, 1], [0, H], [1, HD]]), op=ALU.mult), self.s_ve)
        P.selfwait("vector", self.s_ve)
        P.i("vector", lambda E: self.lp(lambda: E.reduce_sum(
            out=self.bgl_row[0:1, :], in_=AP(self.tmp_row, 0, [[D, 1], [HD, H], [1, HD]]),
            axis=AX.X)), self.s_ve)
        P.selfwait("vector", self.s_ve)
        P.i("vector", lambda E: E.tensor_tensor(
            out=AP(self.tmp_row, 0, [[D, 1], [HD, H], [1, HD]]),
            in0=AP(self.routb_row, 0, [[1040, 1], [HD, H], [1, HD]]),
            in1=AP(self.grrow, 0, [[HD, 1], [0, H], [1, HD]]), op=ALU.mult), self.s_ve)
        P.selfwait("vector", self.s_ve)
        P.i("vector", lambda E: self.lp(lambda: E.reduce_sum(
            out=self.routb_row[0:1, D:D + H],
            in_=AP(self.tmp_row, 0, [[D, 1], [HD, H], [1, HD]]), axis=AX.X)), self.s_ve)
        P.i("vector", lambda E: E.tensor_copy(
            out=self.vb_row[0:1, :], in_=self.qkvb_row[0:1, 2 * D:3 * D]), self.s_ve)
        for sl in range(5):
            P.i("vector", lambda E, sl=sl: E.memset(
                AP(self.v_ring, sl * H * 66 + 64, [[5 * H * 66, 128], [66, H], [1, 1]]),
                1.0), self.s_ve)
        P.i("vector", lambda E: E.memset(
            AP(self.rv_aug, 64, [[LH * 65, 128], [65, LH], [1, 1]]), 1.0), self.s_ve)
        P.mark("prologue_v", self.s_ve)

        P.wm("tensor", "wdma")
        P.wm("tensor", "consts")
        for j in range(16):
            P.i("tensor", lambda E, j=j: E.transpose(
                self.ps_tr[:, 2 * j:2 * j + 1], self.qkvb_row[0:1, j * 128:(j + 1) * 128],
                self.ones1[0:1, 0:1]), self.s_te)
        for j in range(2):
            P.i("tensor", lambda E, j=j: E.transpose(
                self.ps_tr[:, 32 + 2 * j:33 + 2 * j],
                self.rqb_row[0:1, j * 128:(j + 1) * 128],
                self.ones1[0:1, 0:1]), self.s_te)
            P.i("tensor", lambda E, j=j: E.transpose(
                self.ps_tr[:, 36 + 2 * j:37 + 2 * j],
                self.rkb_row[0:1, j * 128:(j + 1) * 128],
                self.ones1[0:1, 0:1]), self.s_te)
        P.w("vector", self.s_te, P.cur(self.s_te))
        P.i("vector", lambda E: E.tensor_copy(
            out=self.qkb_col[:, :], in_=AP(self.ps_tr, 0, [[1024, 128], [2, 16]])),
            self.s_ve)
        P.i("vector", lambda E: E.tensor_copy(
            out=self.rqb_col[:, :], in_=AP(self.ps_tr, 32, [[1024, 128], [2, 2]])),
            self.s_ve)
        P.i("vector", lambda E: E.tensor_copy(
            out=self.rkb_col[:, :], in_=AP(self.ps_tr, 36, [[1024, 128], [2, 2]])),
            self.s_ve)
        P.mark("bias_cols", self.s_ve)

    def w8t_chunk(self, cc, pstr_wait="bias_cols"):
        P = self.P
        if cc == 0:
            P.wm("sync", "wam_rd_6")
        else:
            P.wm("sync", f"w8q_{cc - 1}")
        P.i("sync", lambda E, cc=cc: E.dma_start(
            out=self.wstage[:, :], in_=self.d_ffn_w[cc * 128:(cc + 1) * 128, :]), self.s_sy, 16)
        P.mark(f"w8d_{cc}", self.s_sy)
        P.wm("scalar", f"w8d_{cc}")
        P.wm("scalar", "swh")
        if cc > 0:
            P.wm("scalar", f"w8c_{cc - 1}")
        P.i("scalar", lambda E: E.activation(
            out=self.wtmp8[:, :], in_=self.wstage[:, :], func=AF.Copy,
            scale=self.sxh_col[:, 0:1]), self.s_sc)
        P.mark(f"w8q_{cc}", self.s_sc)
        P.wm("vector", f"w8q_{cc}")
        if cc > 0:
            P.wm("vector", f"w8tr_{cc - 1}")
        P.i("vector", lambda E: E.tensor_copy(out=self.wtmpb[:, :],
                                              in_=self.wtmp8[:, :]), self.s_ve)
        P.mark(f"w8c_{cc}", self.s_ve)
        P.wm("tensor", f"w8c_{cc}")
        if cc > 0:
            P.wm("tensor", f"w8t_ev_{cc - 1}")
        else:
            P.wm("tensor", pstr_wait)
        for j in range(8):
            P.i("tensor", lambda E, j=j: E.transpose(
                self.ps_tr[:, j * 128:(j + 1) * 128],
                self.wtmpb[:, j * 128:(j + 1) * 128], self.ident[:, :]), self.s_te)
        P.mark(f"w8tr_{cc}", self.s_te)
        P.wm("vector", f"w8tr_{cc}")
        P.i("vector", lambda E, cc=cc: E.tensor_copy(
            out=AP(self.w8t, cc * 128, [[8 * D, 128], [D, 8], [1, 128]]),
            in_=self.ps_tr[:, :]), self.s_ve)
        P.mark(f"w8t_ev_{cc}", self.s_ve)
        if cc == 7:
            P.mark("w8t", self.s_ve)

    def p1a(self):
        P, c = self.P, self.cfg
        NT = c.ntiles

        def tile(i, t):
            self.dbgq("scalar", "vector", "sync")
            sl = i % 2
            lsl = i % 3
            eng = "sync" if sl == 0 else "scalar"
            src = self.d_xh if t == NT else self.d_x[t * 128:(t + 1) * 128, :]
            if i >= 2:
                P.wm(eng, f"a_bn_{i - 2}")
                P.wm(eng, f"a_ln_{i - 2}")
            P.i(eng, lambda E: E.dma_start(out=self.xa_sb[:, sl, :], in_=src),
                self.s_xa[sl], 16)
            P.w("vector", self.s_xa[sl], P.cur(self.s_xa[sl]))
            P.selfwait("vector", self.s_ve)
            P.i("vector", lambda E: E.bn_stats(out=self.bn_buf[:, 0:6],
                                               in_=self.xa_sb[:, sl, 0:512]), self.s_ve)
            P.i("vector", lambda E: E.bn_stats(out=self.bn_buf[:, 6:12],
                                               in_=self.xa_sb[:, sl, 512:1024]), self.s_ve)
            P.selfwait("vector", self.s_ve)
            P.i("vector", lambda E: E.bn_aggr(out=self.mv_tmp[:, :],
                                              in_=self.bn_buf[:, :]), self.s_ve)
            P.mark(f"a_bn_{i}", self.s_ve)
            P.w("scalar", self.s_ve, P.cur(self.s_ve))
            P.i("scalar", lambda E, t=t: E.activation(
                out=self.mr1[:, t, 0:1], in_=self.mv_tmp[:, 1:2], func=AF.Ln,
                bias=EPS), self.s_sc)
            P.selfwait("scalar", self.s_sc)
            P.i("scalar", lambda E, t=t: E.activation(
                out=self.mr1[:, t, 0:1], in_=self.mr1[:, t, 0:1], func=AF.Exp,
                scale=-0.5), self.s_sc)
            P.w("vector", self.s_sc, P.cur(self.s_sc))
            P.i("vector", lambda E, t=t: E.tensor_scalar(
                out=self.mr1[:, t, 1:2], in0=self.mv_tmp[:, 0:1],
                scalar1=self.mr1[:, t, 0:1], scalar2=-1.0,
                op0=ALU.mult, op1=ALU.mult), self.s_ve)
            P.mark(f"a_mr_{i}", self.s_ve)
            P.w("scalar", self.s_ve, P.cur(self.s_ve))
            if i >= 3:
                P.wm("scalar", f"a_pool_{i - 3}")
            P.i("scalar", lambda E, t=t, sl=sl, lsl=lsl: E.activation(
                out=self.ln1a_sb[:, lsl, :], in_=self.xa_sb[:, sl, :], func=AF.Identity,
                scale=self.mr1[:, t, 0:1], bias=self.mr1[:, t, 1:2]), self.s_sc)
            P.mark(f"a_ln_{i}", self.s_sc)
            if t != NT:
                P.w("tensor", self.s_sc, P.cur(self.s_sc))
                P.wm("tensor", "consts")
                first = (t == 0)
                P.i("tensor", lambda E, t=t, lsl=lsl, first=first: E.matmul(
                    self.ps_a[0:16, 0:512], self.a_all[:, t, :],
                    self.ln1a_sb[:, lsl, 0:512], start=first,
                    stop=(t == NT - 1)), self.s_te)
                P.i("tensor", lambda E, t=t, lsl=lsl, first=first: E.matmul(
                    self.ps_a[0:16, 512:1024], self.a_all[:, t, :],
                    self.ln1a_sb[:, lsl, 512:1024], start=first,
                    stop=(t == NT - 1)), self.s_te)
            P.mark(f"a_pool_{i}", self.s_te)
            if self.cfg.dbg:
                P.selfwait("scalar", self.s_sc)
                row = NT * 128 if t == NT else t * 128
                P.i("scalar", lambda E, lsl=lsl, row=row: E.dma_start(
                    out=self.dbg["ln1"][row:row + 128, :],
                    in_=self.ln1a_sb[:, lsl, :]), self.s_sd, 16)

        tile(0, NT)
        self.wamax_start()
        for t in range(NT):
            tile(t + 1, t)
            if t >= 2 and t % 2 == 0 and (t - 2) // 2 < 8:
                self.wamax_reduce((t - 2) // 2)
        P.mark("pool_mm", self.s_te)

    def ag_start(self):
        P = self.P
        P.wm("vector", "pool_mm")
        P.i("vector", lambda E: E.tensor_copy(out=self.pool_sb[:, :],
                                              in_=self.ps_a[0:16, :]), self.s_ve)
        P.mark("pool_ev", self.s_ve)
        P.selfwait("vector", self.s_ve)
        P.w("scalar", self.s_ve, P.cur(self.s_ve))
        P.i("scalar", lambda E: E.dma_start(out=self.d_ccp_in, in_=self.pool_sb[:, :]),
            self.s_stg, 16)
        P.w("gpsimd", self.s_stg, P.cur(self.s_stg))
        P.i("gpsimd", lambda E: E.collective_compute(
            "AllGather", ALU.bypass, ins=[self.d_ccp_in], outs=[self.d_ccp_out],
            replica_groups=RG), self.s_gp)
        P.selfwait("gpsimd", self.s_gp)
        pid = self.nc.partition_id_tensor
        P.i("gpsimd", lambda E: E.dma_start(out=self.pid_sb[0:1, 0:1],
                                            in_=AP(pid, 0, [[1, 1], [1, 1]])), self.s_gd2, 16)
        P.w("gpsimd", self.s_gd2, P.cur(self.s_gd2))

        def dyn_read(E):
            with E.register("rpid") as r:
                E.reg_load(r, self.pid_sb[0:1, 0:1])
                E.reg_alu(r, r, 4, ALU.divide)
                E.reg_alu(r, r, 64 * D, ALU.mult)
                return E.dma_start(out=self.pooled_all[0:64, :],
                                   in_=AP(self.d_ccp_out.tensor, r, [[D, 64], [1, D]]))
        P.i("gpsimd", dyn_read, self.s_gd2, 16)
        P.mark("pooled_my", self.s_gd2)
        if self.cfg.dbg:
            P.w("gpsimd", self.s_gd2, P.cur(self.s_gd2))
            P.i("gpsimd", lambda E: E.dma_start(out=self.dbg["pooled"],
                                                in_=self.d_ccp_out), self.s_gd2, 16)

    def ag_finish(self):
        P = self.P
        P.wm("tensor", "pooled_my")
        P.wm("tensor", "b_fm_2")
        for j in range(8):
            P.i("tensor", lambda E, j=j: E.transpose(
                self.ps_tr[:, j * NLAT:(j + 1) * NLAT],
                self.pooled_all[0:64, j * 128:(j + 1) * 128],
                self.ident[0:64, 0:64]), self.s_te)
        P.w("vector", self.s_te, P.cur(self.s_te))
        P.i("vector", lambda E: E.tensor_copy(
            out=self.pooled_fm[:, :, :],
            in_=AP(self.ps_tr, 0, [[1024, 128], [NLAT, 8], [1, NLAT]])), self.s_ve)
        P.mark("pooled_fm", self.s_ve)

        P.wm("tensor", "pooled_fm")
        P.wm("tensor", "v_2")
        for rc in range(2):
            for dc in range(8):
                P.i("tensor", lambda E, rc=rc, dc=dc: E.matmul(
                    self.ps_s[:, rc * NLAT:(rc + 1) * NLAT],
                    self.rkv_wb[:, dc, rc * 128:(rc + 1) * 128],
                    self.pooled_fm[:, dc, :], start=(dc == 0), stop=(dc == 7)),
                    self.s_te)
        P.w("scalar", self.s_te, P.cur(self.s_te))
        for rc in range(2):
            P.i("scalar", lambda E, rc=rc: E.activation(
                out=self.rk_fm[:, rc, :], in_=self.ps_s[:, rc * NLAT:(rc + 1) * NLAT],
                func=AF.Identity, bias=self.rkb_col[:, rc:rc + 1]), self.s_sc)
        P.mark("rk", self.s_sc)

        P.wm("tensor", "rk")
        P.i("tensor", lambda E: E.matmul(
            self.ps_s[0:64, 0:LD], self.ones1[0:1, 0:64],
            self.rvb_row[0:1, :], start=True, stop=False), self.s_te)
        for dc in range(8):
            P.i("tensor", lambda E, dc=dc: E.matmul(
                self.ps_s[0:64, 0:LD], self.pooled_fm[:, dc, :],
                self.rkv_wb[:, dc, LD:2 * LD], start=False, stop=(dc == 7)), self.s_te)
        P.w("vector", self.s_te, P.cur(self.s_te))
        P.i("vector", lambda E: E.tensor_copy(
            out=AP(self.rv_aug, 0, [[LH * 65, 64], [65, LH], [1, LHD]]),
            in_=self.ps_s[0:64, 0:LD]), self.s_ve)
        P.mark("rv_ps_done", self.s_ve)
        P.selfwait("vector", self.s_ve)
        P.w("scalar", self.s_ve, P.cur(self.s_ve))
        P.i("scalar", lambda E: E.dma_start(
            out=self.rv_aug[64:128, :, :], in_=self.rv_aug[0:64, :, :]), self.s_stg, 16)
        P.mark("rv", self.s_stg)
        if self.cfg.dbg:
            P.wm("vector", "rv")
            P.w("scalar", self.s_ve, P.cur(self.s_ve))
            P.i("scalar", lambda E: E.dma_start(
                out=AP(self.dbg["rk"].tensor, 0,
                       [[NLAT, 128], [128 * NLAT, 2], [1, NLAT]]),
                in_=AP(self.rk_fm, 0, [[2 * NLAT, 128], [NLAT, 2], [1, NLAT]])),
                self.s_vd, 16)
            P.w("scalar", self.s_ve, P.cur(self.s_ve))
            P.i("scalar", lambda E: E.dma_start(
                out=self.dbg["rv"], in_=self.rv_aug[0:64, :, :]), self.s_vd, 16)

    def dbgq(self, *engs):
        if not self.cfg.dbg:
            return
        for e in engs:
            self.P.w(e, self.s_vd, self.P.cur(self.s_vd))
            self.P.w(e, self.s_sd, self.P.cur(self.s_sd))

    def chain_tile(self, i, t, pstr_wait=None, pss_wait=None):
        P, c = self.P, self.cfg
        self.dbgq(*("vector", "scalar", "tensor"))
        NT = c.ntiles
        halo = (t == NT)
        xsl = i % 2
        lsl = i % 2
        src = self.d_xh if halo else self.d_x[t * 128:(t + 1) * 128, :]
        if i >= 2:
            P.wm("sync", f"b_ln_{i - 2}")
        if i == 0:
            P.wm("sync", "wam_rd_7")
        P.i("sync", lambda E, xsl=xsl, src=src: E.dma_start(
            out=self.xl_sb[:, xsl, :], in_=src), self.s_xl[xsl], 16)
        P.w("scalar", self.s_xl[xsl], P.cur(self.s_xl[xsl]))
        if i >= 2:
            P.wm("scalar", f"b_tr_{i - 2}")
        P.i("scalar", lambda E, t=t, xsl=xsl, lsl=lsl: E.activation(
            out=self.ln1b_sb[:, lsl, :], in_=self.xl_sb[:, xsl, :], func=AF.Identity,
            scale=self.mr1[:, t, 0:1], bias=self.mr1[:, t, 1:2]), self.s_sc)
        P.mark(f"b_ln_{i}", self.s_sc)
        P.w("tensor", self.s_sc, P.cur(self.s_sc))
        if i >= 1:
            P.wm("tensor", f"b_fm_{i - 1}")
        if pstr_wait:
            P.wm("tensor", pstr_wait)
        if not halo and t % 4 == 0 and t // 4 >= 1:
            P.wm("tensor", f"latfm_{t // 4 - 1}_1")
        for j in range(8):
            P.i("tensor", lambda E, j=j, lsl=lsl: E.transpose(
                self.ps_tr[:, j * 128:(j + 1) * 128],
                self.ln1b_sb[:, lsl, j * 128:(j + 1) * 128], self.ident[:, :]),
                self.s_te)
        P.mark(f"b_tr_{i}", self.s_te)
        P.w("vector", self.s_te, P.cur(self.s_te))
        if halo:
            P.i("vector", lambda E: E.tensor_copy(
                out=self.lnfmh_sb[:, :, :],
                in_=AP(self.ps_tr, 0, [[1024, 128], [128, 8], [1, 128]])), self.s_ve)
        else:
            tin = t % 4
            P.i("vector", lambda E, tin=tin: E.tensor_copy(
                out=AP(self.lnfm_sb, tin * 128, [[8 * 512, 128], [512, 8], [1, 128]]),
                in_=AP(self.ps_tr, 0, [[1024, 128], [128, 8], [1, 128]])), self.s_ve)
        P.mark(f"b_fm_{i}", self.s_ve)

        P.wm("tensor", f"b_fm_{i}")
        if i >= 1:
            P.wm("tensor", f"v_{i - 1}")
        vslot = 0 if halo else (t + 1) % 5
        if not halo and t >= 4:
            P.wm("tensor", f"av_done_{t - 4}")
        if pss_wait:
            P.wm("tensor", pss_wait)
        if halo:
            P.wm("tensor", "pool_ev")
        elif t // 4 >= 1 and t % 4 == 0:
            P.wm("tensor", f"rem_ev_v_{t // 4 - 1}_3")
            P.wm("tensor", f"exp_{t - 1}_3")
        lnsrc = (lambda dc: self.lnfmh_sb[:, dc, :]) if halo else \
                (lambda dc, tin=t % 4: self.lnfm_sb[:, dc, tin * 128:(tin + 1) * 128])
        for half in range(2):
            P.i("tensor", lambda E, half=half: E.matmul(
                self.ps_a[:, half * 512:(half + 1) * 512], self.ones1[0:1, :],
                self.vb_row[0:1, half * 512:(half + 1) * 512],
                start=True, stop=False), self.s_te)
            for dc in range(8):
                P.i("tensor", lambda E, dc=dc, half=half: E.matmul(
                    self.ps_a[:, half * 512:(half + 1) * 512], lnsrc(dc),
                    self.qkv_wb[:, dc, 2 * D + half * 512:2 * D + (half + 1) * 512],
                    start=False, stop=(dc == 7)), self.s_te)
        P.i("tensor", lambda E: E.matmul(
            self.ps_s[:, 0:H], self.ones1[0:1, :], self.bgl_row[0:1, :],
            start=True, stop=False), self.s_te)
        for dc in range(8):
            P.i("tensor", lambda E, dc=dc: E.matmul(
                self.ps_s[:, 0:H], lnsrc(dc), self.wvgl[:, dc, :],
                start=False, stop=(dc == 7)), self.s_te)
        P.mark(f"v_mm_{i}", self.s_te)
        P.w("vector", self.s_te, P.cur(self.s_te))
        P.i("vector", lambda E, vslot=vslot: E.tensor_copy(
            out=AP(self.v_ring, vslot * H * 66, [[5 * H * 66, 128], [66, H], [1, HD]]),
            in_=AP(self.ps_a, 0, [[1024, 128], [HD, H], [1, HD]])), self.s_ve)
        P.i("vector", lambda E, vslot=vslot: E.tensor_copy(
            out=AP(self.v_ring, vslot * H * 66 + 65, [[5 * H * 66, 128], [66, H], [1, 1]]),
            in_=self.ps_s[:, 0:H]), self.s_ve)
        P.mark(f"v_{i}", self.s_ve)
        if self.cfg.dbg:
            P.selfwait("vector", self.s_ve)
            row = NT * 128 if halo else t * 128
            P.w("scalar", self.s_ve, P.cur(self.s_ve))
            P.i("scalar", lambda E, vslot=vslot, row=row: E.dma_start(
                out=self.dbg["v"][row:row + 128, :],
                in_=self.v_ring[:, vslot, :, :]), self.s_vd, 16)

    def qk_group(self, g, gates=()):
        P, c = self.P, self.cfg
        self.dbgq(*("scalar",))
        for j in range(16):
            half = j % 2
            if j >= 2:
                P.wm("tensor", f"qk_ev_{g}_{j - 2}")
            else:
                P.wm("tensor", f"v_{4 * g + 4}")
            if g == 0 and j == 0:
                P.wm("tensor", "kh")
                P.wm("tensor", "prologue_v")
            for dc in range(8):
                P.i("tensor", lambda E, j=j, dc=dc, half=half: E.matmul(
                    self.ps_a[:, half * 512:(half + 1) * 512],
                    self.qkv_wb[:, dc, j * 128:(j + 1) * 128],
                    self.lnfm_sb[:, dc, :], start=(dc == 0), stop=(dc == 7)), self.s_te)
            P.mark(f"qk_mm_{g}_{j}", self.s_te)
            if self.cfg.dbg and g == 0 and j == 0:
                P.w("gpsimd", self.s_te, P.cur(self.s_te))
                P.i("gpsimd", lambda E: E.dma_start(
                    out=self.dbg["probe"][0:128, 0:2048],
                    in_=self.qkv_wb[:, 0, 0:2048]), self.s_gd2, 16)
                P.w("scalar", self.s_gd2, P.cur(self.s_gd2))
            P.w("scalar", self.s_te, P.cur(self.s_te))
            if j < 8:
                dst = lambda E, j=j: self.q_sb[:, j, :]
            else:
                dst = lambda E, j=j: self.k_sb[:, j - 8, :]
            if j == 8 and g > 0:
                P.wm("scalar", f"kprev_{g - 1}")
            P.i("scalar", lambda E, j=j, half=half, dst=dst: E.activation(
                out=dst(E), in_=self.ps_a[:, half * 512:(half + 1) * 512],
                func=AF.Identity, bias=self.qkb_col[:, j:j + 1]), self.s_sc)
            P.mark(f"qk_ev_{g}_{j}", self.s_sc)
            if j % 4 == 3 and gates:
                self.gate_outw_res(gates[j // 4])
        P.mark(f"qk_{g}", self.s_sc)
        if self.cfg.dbg:
            P.selfwait("scalar", self.s_sc)
            for j in range(8):
                P.i("scalar", lambda E, j=j, g=g: E.dma_start(
                    out=self.dbg["q"][j * 128:(j + 1) * 128, g * 512:(g + 1) * 512],
                    in_=self.q_sb[:, j, :]), self.s_sd, 16)
                P.i("scalar", lambda E, j=j, g=g: E.dma_start(
                    out=self.dbg["k"][j * 128:(j + 1) * 128, g * 512:(g + 1) * 512],
                    in_=self.k_sb[:, j, :]), self.s_sd, 16)

    def rq_group(self, g):
        P = self.P
        self.dbgq(*("scalar",))
        for rc in range(2):
            P.wm("tensor", f"qk_ev_{g}_{14 + rc}")
            for dc in range(8):
                P.i("tensor", lambda E, rc=rc, dc=dc: E.matmul(
                    self.ps_a[:, rc * 512:(rc + 1) * 512],
                    self.rq_wb[:, dc, rc * 128:(rc + 1) * 128],
                    self.lnfm_sb[:, dc, :], start=(dc == 0), stop=(dc == 7)), self.s_te)
            P.mark(f"rq_mm_{g}_{rc}", self.s_te)
            P.w("scalar", self.s_te, P.cur(self.s_te))
            P.i("scalar", lambda E, rc=rc: E.activation(
                out=self.rq_sb[:, rc, :], in_=self.ps_a[:, rc * 512:(rc + 1) * 512],
                func=AF.Identity, bias=self.rqb_col[:, rc:rc + 1]), self.s_sc)
            P.mark(f"rq_ev_{g}_{rc}", self.s_sc)
        P.mark(f"rq_{g}", self.s_sc)
        if self.cfg.dbg:
            P.selfwait("scalar", self.s_sc)
            for rc in range(2):
                P.i("scalar", lambda E, rc=rc, g=g: E.dma_start(
                    out=self.dbg["rq"][rc * 128:(rc + 1) * 128, g * 512:(g + 1) * 512],
                    in_=self.rq_sb[:, rc, :]), self.s_sd, 16)

    def local_qtile(self, tq):
        P, c = self.P, self.cfg
        self.dbgq(*("vector", "scalar", "tensor"))
        g = tq // 4
        t = tq % 4
        esl = 0
        for w in range(4):
            pb = self.ps_b if w % 2 == 0 else self.ps_a
            if w >= 2:
                P.wm("tensor", f"exp_{tq}_{w - 2}")
            elif w == 0:
                if tq >= 4 and t == 0:
                    P.wm("tensor", f"res1_{tq - 1}")
                elif tq >= 1:
                    P.wm("tensor", f"exp_{tq - 1}_2")
            else:
                if t == 0:
                    P.wm("tensor", f"rq_ev_{g}_1")
                else:
                    P.wm("tensor", f"exp_{tq - 1}_3")
            if w == 0 and t == 0 and g > 0:
                P.wm("tensor", f"kprev_{g - 1}")
            P.wm("tensor", f"qk_{g}")
            for hh in range(2):
                for ab in range(2):
                    for hp in range(2):
                        h = 4 * w + 2 * hh + hp
                        kchunk = h // 2
                        rb = (h % 2) * 64
                        if ab == 0:
                            if tq == 0:
                                klhs = lambda E, kchunk=kchunk, rb=rb: \
                                    self.kh_sb[rb:rb + 64, kchunk, :]
                            elif t == 0:
                                klhs = lambda E, kchunk=kchunk, rb=rb: \
                                    self.kprev_sb[rb:rb + 64, kchunk, :]
                            else:
                                klhs = lambda E, kchunk=kchunk, rb=rb, t=t: \
                                    self.k_sb[rb:rb + 64, kchunk, (t - 1) * 128:t * 128]
                        else:
                            klhs = lambda E, kchunk=kchunk, rb=rb, t=t: \
                                self.k_sb[rb:rb + 64, kchunk, t * 128:(t + 1) * 128]
                        col = (h % 2) * 512 + hh * 256 + ab * 128
                        P.i("tensor", lambda E, klhs=klhs, h=h, t=t, col=col, pb=pb: E.matmul(
                            pb[:, col:col + 128], klhs(E),
                            self.q_sb[(h % 2) * 64:(h % 2) * 64 + 64, h // 2,
                                      t * 128:(t + 1) * 128],
                            start=True, stop=True), self.s_te)
            P.mark(f"sc_{tq}_{w}", self.s_te)
            P.w("scalar", self.s_te, P.cur(self.s_te))
            if tq >= 1 and w == 0:
                P.wm("scalar", f"av_done_{tq - 1}")
            P.i("scalar", lambda E, esl=esl, w=w, pb=pb: E.activation(
                out=self.e_sb[:, 0, w * 1024:(w + 1) * 1024],
                in_=pb[:, :], func=AF.Exp, scale=0.125), self.s_sc)
            P.mark(f"exp_{tq}_{w}", self.s_sc)
            if w % 2 == 1:
                P.w("vector", self.s_sc, P.cur(self.s_sc))
                mask = self.maskAB0 if tq == 0 else self.maskAB
                P.i("vector", lambda E, w=w, mask=mask: E.tensor_tensor(
                    out=AP(self.e_sb, (w - 1) * 1024, [[4096, 128], [1, 2048]]),
                    in0=AP(self.e_sb, (w - 1) * 1024, [[4096, 128], [1, 2048]]),
                    in1=AP(mask, 0, [[512, 128], [0, 4], [1, 512]]),
                    op=ALU.mult), self.s_ve)
                P.mark(f"mask_{tq}_{w // 2}", self.s_ve)

        vA = tq % 5
        vB = (tq + 1) % 5
        P.wm("tensor", f"v_{tq + 1}")
        if t == 0 and tq >= 4:
            P.wm("tensor", f"amax_{tq - 1}")
        if t == 0 and g >= 1:
            P.wm("tensor", f"latn_ev_{g - 1}_3")

        def av(h, col, extra_wait=None):
            P.wm("tensor", f"mask_{tq}_{(h // 4) // 2}")
            if extra_wait:
                P.wm("tensor", extra_wait)
            if tq >= 1:
                P.wm("tensor", f"lu_ev_{tq - 1}_all")
            w = h // 4
            ecol = lambda ab: w * 1024 + (h % 2) * 512 + \
                ((h % 4) // 2) * 256 + ab * 128
            P.i("tensor", lambda E, h=h, col=col: E.matmul(
                self.ps_c[:, col:col + 66],
                AP(self.e_sb, ecol(0), [[4096, 128], [1, 128]]),
                AP(self.v_ring, vA * H * 66 + h * 66, [[5 * H * 66, 128], [1, 66]]),
                start=True, stop=False), self.s_te)
            P.i("tensor", lambda E, h=h, col=col: E.matmul(
                self.ps_c[:, col:col + 66],
                AP(self.e_sb, ecol(1), [[4096, 128], [1, 128]]),
                AP(self.v_ring, vB * H * 66 + h * 66, [[5 * H * 66, 128], [1, 66]]),
                start=False, stop=True), self.s_te)

        for h in range(7):
            av(h, h * 66)
        P.mark(f"av_r0_{tq}", self.s_te)
        for h in range(7, 14):
            av(h, 512 + (h - 7) * 66)
        P.mark(f"av_r1_{tq}", self.s_te)
        P.wm("vector", f"av_r0_{tq}")
        P.i("vector", lambda E, t=t: E.tensor_copy(
            out=self.lu_sb[:, t, 0:7, :], in_=self.ps_c[:, 0:462]), self.s_ve)
        P.mark(f"lu_ev_{tq}_0", self.s_ve)
        for h in range(14, 16):
            av(h, (h - 14) * 66, extra_wait=f"lu_ev_{tq}_0")
        P.mark(f"av_r2_{tq}", self.s_te)
        P.wm("vector", f"av_r1_{tq}")
        P.i("vector", lambda E, t=t: E.tensor_copy(
            out=self.lu_sb[:, t, 7:14, :], in_=self.ps_c[:, 512:974]), self.s_ve)
        P.wm("vector", f"av_r2_{tq}")
        P.i("vector", lambda E, t=t: E.tensor_copy(
            out=self.lu_sb[:, t, 14:16, :], in_=self.ps_c[:, 0:132]), self.s_ve)
        P.mark(f"lu_ev_{tq}_all", self.s_ve)
        P.mark(f"av_done_{tq}", self.s_te)
        if self.cfg.dbg:
            P.selfwait("vector", self.s_ve)
            P.w("scalar", self.s_ve, P.cur(self.s_ve))
            P.i("scalar", lambda E, tq=tq, t=t: E.dma_start(
                out=self.dbg["lu"][tq * 128:(tq + 1) * 128, :],
                in_=self.lu_sb[:, t, :, :]), self.s_vd, 16)

    def latent_group(self, g):
        P, c = self.P, self.cfg
        self.dbgq(*("vector", "scalar", "tensor"))
        CH = c.chunk
        lsc = 1.0 / np.sqrt(LHD)
        P.wm("tensor", "rv")
        P.wm("tensor", "rk")
        P.wm("tensor", f"rq_{g}")
        P.wm("tensor", f"exp_{4 * g + 3}_3")
        for pr in range(2):
            for hp in range(2):
                hq = 2 * pr + hp
                rb = hp * 64
                if pr == 0:
                    P.i("tensor", lambda E, hq=hq, rb=rb: E.matmul(
                        self.ps_s[rb:rb + 64, 0:512],
                        self.rk_fm[rb:rb + 64, hq // 2, :],
                        self.rq_sb[rb:rb + 64, hq // 2, :],
                        start=True, stop=True), self.s_te)
                else:
                    P.i("tensor", lambda E, hq=hq, rb=rb: E.matmul(
                        self.ps_b[rb:rb + 64, 0:512],
                        self.rk_fm[rb:rb + 64, hq // 2, :],
                        self.rq_sb[rb:rb + 64, hq // 2, :],
                        start=True, stop=True), self.s_te)
            P.mark(f"lsc_{g}_{pr}", self.s_te)
            P.w("scalar", self.s_te, P.cur(self.s_te))
            if pr == 0:
                P.i("scalar", lambda E: E.activation(
                    out=self.elat_sb[:, 0, :], in_=self.ps_s[:, 0:512],
                    func=AF.Exp, scale=lsc), self.s_sc)
            else:
                P.i("scalar", lambda E: E.activation(
                    out=self.elat_sb[:, 1, :], in_=self.ps_b[:, 0:512],
                    func=AF.Exp, scale=lsc), self.s_sc)
            P.mark(f"lexp_{g}_{pr}", self.s_sc)
        P.w("vector", self.s_sc, P.cur(self.s_sc))
        ncg = c.ch_per_grp
        P.i("vector", lambda E, g=g, ncg=ncg: E.tensor_tensor(
            out=AP(self.elat_sb, 0, [[2 * 512, 128], [512, 2], [1, 512]]),
            in0=AP(self.elat_sb, 0, [[2 * 512, 128], [512, 2], [1, 512]]),
            in1=AP(self.latmask_sb, g * ncg, [[16, 128], [0, 2], [1, ncg], [0, CH]]),
            op=ALU.mult), self.s_ve)
        P.mark(f"lmask_{g}", self.s_ve)
        if self.cfg.dbg:
            P.selfwait("vector", self.s_ve)
            P.w("scalar", self.s_ve, P.cur(self.s_ve))
            P.i("scalar", lambda E, g=g: E.dma_start(
                out=self.dbg["elat"][0:NLAT, 0 * 512 + g * 512:(g + 1) * 512],
                in_=self.elat_sb[0:64, 0, :]), self.s_vd, 16)
            P.w("scalar", self.s_ve, P.cur(self.s_ve))
            P.i("scalar", lambda E, g=g: E.dma_start(
                out=self.dbg["elat"][0:NLAT, LH // 2 * c.s_core + g * 512:
                                     LH // 2 * c.s_core + (g + 1) * 512],
                in_=self.elat_sb[0:64, 1, :]), self.s_vd, 16)
            P.w("scalar", self.s_ve, P.cur(self.s_ve))
            P.i("scalar", lambda E, g=g: E.dma_start(
                out=self.dbg["elat"][0:NLAT, 1 * c.s_core + g * 512:
                                     1 * c.s_core + (g + 1) * 512],
                in_=self.elat_sb[64:128, 0, :]), self.s_vd, 16)
            P.w("scalar", self.s_ve, P.cur(self.s_ve))
            P.i("scalar", lambda E, g=g: E.dma_start(
                out=self.dbg["elat"][0:NLAT, 3 * c.s_core + g * 512:
                                     3 * c.s_core + (g + 1) * 512],
                in_=self.elat_sb[64:128, 1, :]), self.s_vd, 16)

        P.wm("tensor", f"lmask_{g}")
        P.wm("tensor", f"av_done_{4 * g + 3}")
        P.wm("tensor", f"lu_ev_{4 * g + 3}_all")
        for st in range(4):
            if st >= 1:
                P.wm("tensor", f"latn_ev_{g}_{st - 1}")
            for hq in range(LH):
                pr, hp = hq // 2, hq % 2
                rb = hp * 64
                col = (hq % 2) * 512 + (hq // 2) * 130
                P.i("tensor", lambda E, st=st, hq=hq, pr=pr, rb=rb, col=col: E.matmul(
                    self.ps_c[:, col:col + 65],
                    self.elat_sb[rb:rb + 64, pr, st * 128:(st + 1) * 128],
                    self.rv_aug[rb:rb + 64, hq, :],
                    start=True, stop=True), self.s_te)
            P.mark(f"latav_{g}_{st}", self.s_te)
            P.wm("vector", f"latav_{g}_{st}")
            P.i("vector", lambda E: E.tensor_copy(
                out=self.dtmp[:, :],
                in_=AP(self.ps_c, 64, [[1024, 128], [512, 2], [130, 2], [1, 1]])),
                self.s_ve)
            P.selfwait("vector", self.s_ve)
            P.i("vector", lambda E: E.tensor_scalar_max(
                out=self.dtmp[:, :], in0=self.dtmp[:, :], scalar1=1e-30), self.s_ve)
            P.selfwait("vector", self.s_ve)
            P.i("vector", lambda E, st=st: E.reciprocal(
                out=self.dinv_sb[:, st, :], in_=self.dtmp[:, :]), self.s_ve)
            P.selfwait("vector", self.s_ve)
            P.i("vector", lambda E, st=st: E.tensor_tensor(
                out=self.latn_sb[:, st, :, :],
                in0=AP(self.ps_c, 0, [[1024, 128], [512, 2], [130, 2], [1, LHD]]),
                in1=AP(self.dinv_sb, st * LH, [[4 * LH, 128], [1, LH], [0, LHD]]),
                op=ALU.mult), self.s_ve)
            P.mark(f"latn_ev_{g}_{st}", self.s_ve)
        jof = {0: 0, 1: 2, 2: 1, 3: 3}
        for cc in range(2):
            P.wm("tensor", f"latn_ev_{g}_3")
            if g >= 1:
                P.wm("tensor", f"mixT_ev_{4 * g - 1}")
            else:
                P.wm("tensor", "w8t_ev_7")
            if cc == 1:
                P.wm("tensor", f"latfm_{g}_0")
            for hp in range(2):
                hq = 2 * cc + hp
                j = jof[hq]
                for st in range(4):
                    P.i("tensor", lambda E, st=st, j=j, hp=hp: E.transpose(
                        self.ps_tr[hp * 64:hp * 64 + 64, st * 128:(st + 1) * 128],
                        self.latn_sb[:, st, j, :], self.ident[:, :]), self.s_te)
            P.mark(f"latT_{g}_{cc}", self.s_te)
            P.w("vector", self.s_te, P.cur(self.s_te))
            P.i("vector", lambda E, cc=cc: E.tensor_copy(
                out=self.latfm_sb[:, cc, :], in_=self.ps_tr[:, 0:512]), self.s_ve)
            P.mark(f"latfm_{g}_{cc}", self.s_ve)
        if self.cfg.dbg:
            P.selfwait("vector", self.s_ve)
            P.w("scalar", self.s_ve, P.cur(self.s_ve))
            P.i("scalar", lambda E, g=g: E.dma_start(
                out=AP(self.dbg["latn"].tensor, g * 512 * LD,
                       [[LD, 128], [128 * LD, 4], [1, LD]]),
                in_=AP(self.latn_sb, 0, [[4 * LH * LHD, 128], [LH * LHD, 4],
                                         [1, LH * LHD]])), self.s_vd, 16)

        P.wm("tensor", "wdma2")
        for st in range(4):
            P.wm("tensor", f"latfm_{g}_1")
            if st >= 1:
                P.wm("tensor", f"rem_ev_{g}_{st - 1}")
                P.wm("tensor", f"rem_ev_v_{g}_{st - 1}")
            P.i("tensor", lambda E: E.matmul(
                self.ps_b[:, 0:512], self.ones1[0:1, :], self.routb_row[0:1, 0:512],
                start=True, stop=False), self.s_te)
            P.i("tensor", lambda E: E.matmul(
                self.ps_b[:, 512:1024], self.ones1[0:1, :],
                self.routb_row[0:1, 512:1024], start=True, stop=False), self.s_te)
            P.i("tensor", lambda E: E.matmul(
                self.ps_s[:, 0:H], self.ones1[0:1, :], self.routb_row[0:1, 1024:1040],
                start=True, stop=False), self.s_te)
            for cc in range(2):
                last = (cc == 1)
                P.i("tensor", lambda E, st=st, cc=cc, last=last: E.matmul(
                    self.ps_b[:, 0:512],
                    self.latfm_sb[:, cc, st * 128:(st + 1) * 128],
                    self.rout_wb[:, cc, 0:512], start=False, stop=last), self.s_te)
                P.i("tensor", lambda E, st=st, cc=cc, last=last: E.matmul(
                    self.ps_b[:, 512:1024],
                    self.latfm_sb[:, cc, st * 128:(st + 1) * 128],
                    self.rout_wb[:, cc, 512:1024], start=False, stop=last), self.s_te)
                P.i("tensor", lambda E, st=st, cc=cc, last=last: E.matmul(
                    self.ps_s[:, 0:H],
                    self.latfm_sb[:, cc, st * 128:(st + 1) * 128],
                    self.rout_wb[:, cc, 1024:1040], start=False, stop=last), self.s_te)
            P.mark(f"rout_mm_{g}_{st}", self.s_te)
            P.w("scalar", self.s_te, P.cur(self.s_te))
            P.i("scalar", lambda E, st=st: E.activation(
                out=self.remote_sb[:, st, 0:512], in_=self.ps_b[:, 0:512],
                func=AF.Copy), self.s_sc)
            P.i("scalar", lambda E, st=st: E.activation(
                out=self.remote_sb[:, st, 512:1024], in_=self.ps_b[:, 512:1024],
                func=AF.Copy), self.s_sc)
            P.w("vector", self.s_te, P.cur(self.s_te))
            P.i("vector", lambda E, st=st: E.tensor_copy(
                out=self.rgr_sb[:, st, :], in_=self.ps_s[:, 0:H]), self.s_ve)
            P.mark(f"rem_ev_s_{g}_{st}", self.s_sc)
            P.mark(f"rem_ev_v_{g}_{st}", self.s_ve)
            P.mark(f"rem_ev_{g}_{st}", self.s_sc)
        P.mark(f"latent_{g}", self.s_sc)
        if self.cfg.dbg:
            P.selfwait("scalar", self.s_sc)
            P.i("scalar", lambda E, g=g: E.dma_start(
                out=AP(self.dbg["remote"].tensor, g * 512 * D,
                       [[D, 128], [128 * D, 4], [1, D]]),
                in_=AP(self.remote_sb, 0, [[4 * D, 128], [D, 4], [1, D]])),
                self.s_sd, 16)
            P.selfwait("vector", self.s_ve)
            P.w("scalar", self.s_ve, P.cur(self.s_ve))
            P.i("scalar", lambda E, g=g: E.dma_start(
                out=AP(self.dbg["rgr"].tensor, g * 512 * H,
                       [[H, 128], [128 * H, 4], [1, H]]),
                in_=AP(self.rgr_sb, 0, [[4 * H, 128], [H, 4], [1, H]])), self.s_vd, 16)

    def gate_outw_res(self, tq):
        P, c = self.P, self.cfg
        self.dbgq(*("vector", "scalar", "tensor"))
        g, t = tq // 4, tq % 4
        NT = c.ntiles
        P.wm("vector", f"rem_ev_s_{g}_{t}")
        P.wm("vector", f"rem_ev_v_{g}_{t}")
        P.wm("vector", f"lu_ev_{tq}_all")
        lu_base = t * H * 66
        luap = lambda off, inner: AP(self.lu_sb, lu_base + off,
                                     [[4 * H * 66, 128], [66, H], [1, inner]])
        P.i("vector", lambda E: E.tensor_scalar_max(
            out=self.gsc[:, 0, :], in0=luap(64, 1), scalar1=1e-30), self.s_ve)
        P.selfwait("vector", self.s_ve)
        P.i("vector", lambda E: E.reciprocal(out=self.gsc[:, 0, :],
                                             in_=self.gsc[:, 0, :]), self.s_ve)
        P.selfwait("vector", self.s_ve)
        P.i("vector", lambda E: E.tensor_tensor(
            out=self.gsc[:, 1, :], in0=luap(65, 1), in1=self.gsc[:, 0, :],
            op=ALU.mult), self.s_ve)
        P.selfwait("vector", self.s_ve)
        P.i("vector", lambda E, t=t: E.tensor_tensor(
            out=self.gsc[:, 1, :], in0=self.gsc[:, 1, :], in1=self.rgr_sb[:, t, :],
            op=ALU.add), self.s_ve)
        P.selfwait("vector", self.s_ve)
        P.w("scalar", self.s_ve, P.cur(self.s_ve))
        P.i("scalar", lambda E: E.activation(
            out=self.gsc[:, 2, :], in_=self.gsc[:, 1, :], func=AF.Exp,
            scale=-1.0), self.s_sc)
        P.w("vector", self.s_sc, P.cur(self.s_sc))
        P.i("vector", lambda E: E.tensor_scalar_add(
            out=self.gsc[:, 3, :], in0=self.gsc[:, 2, :], scalar1=1.0), self.s_ve)
        P.selfwait("vector", self.s_ve)
        P.i("vector", lambda E: E.reciprocal(out=self.gsc[:, 3, :],
                                             in_=self.gsc[:, 3, :]), self.s_ve)
        P.selfwait("vector", self.s_ve)
        P.i("vector", lambda E: E.tensor_tensor(
            out=self.gsc[:, 4, :], in0=self.gsc[:, 3, :], in1=self.gsc[:, 0, :],
            op=ALU.mult), self.s_ve)
        P.i("vector", lambda E: E.tensor_scalar(
            out=self.gsc[:, 5, :], in0=self.gsc[:, 3, :], scalar1=-1.0, scalar2=1.0,
            op0=ALU.mult, op1=ALU.add), self.s_ve)
        P.selfwait("vector", self.s_ve)
        msl = 0
        if tq >= 1:
            P.wm("vector", f"mixT_{tq - 1}")
        P.i("vector", lambda E, msl=msl: E.tensor_tensor(
            out=AP(self.mixed_sb, 0, [[D, 128], [HD, H], [1, HD]]),
            in0=luap(0, HD),
            in1=AP(self.gsc, 4 * H, [[6 * H, 128], [1, H], [0, HD]]),
            op=ALU.mult), self.s_ve)
        if tq >= 1:
            P.wm("vector", f"outw_{tq - 1}")
        P.i("vector", lambda E, t=t: E.tensor_tensor(
            out=AP(self.mixt_sb, 0, [[D, 128], [HD, H], [1, HD]]),
            in0=AP(self.remote_sb, t * D, [[4 * D, 128], [HD, H], [1, HD]]),
            in1=AP(self.gsc, 5 * H, [[6 * H, 128], [1, H], [0, HD]]),
            op=ALU.mult), self.s_ve)
        P.selfwait("vector", self.s_ve)
        P.i("vector", lambda E: E.tensor_tensor(
            out=self.mixed_sb[:, 0, :], in0=self.mixed_sb[:, 0, :],
            in1=self.mixt_sb[:, :], op=ALU.add), self.s_ve)
        P.mark(f"mixed_{tq}", self.s_ve)
        if self.cfg.dbg:
            P.selfwait("vector", self.s_ve)
            P.w("scalar", self.s_ve, P.cur(self.s_ve))
            P.i("scalar", lambda E, tq=tq: E.dma_start(
                out=self.dbg["mixed"][tq * 128:(tq + 1) * 128, :],
                in_=self.mixed_sb[:, 0, :]), self.s_vd, 16)

        P.wm("tensor", f"mixed_{tq}")
        if tq >= 1:
            P.wm("tensor", f"mixT_ev_{tq - 1}")
        for j in range(8):
            P.i("tensor", lambda E, j=j: E.transpose(
                self.ps_tr[:, j * 128:(j + 1) * 128],
                self.mixed_sb[:, 0, j * 128:(j + 1) * 128], self.ident[:, :]),
                self.s_te)
        P.mark(f"mixT_{tq}", self.s_te)
        P.w("vector", self.s_te, P.cur(self.s_te))
        P.i("vector", lambda E: E.tensor_copy(
            out=self.mixt_sb[:, :], in_=self.ps_tr[:, :]), self.s_ve)
        P.mark(f"mixT_ev_{tq}", self.s_ve)

        P.wm("tensor", f"mixT_ev_{tq}")
        P.wm("tensor", "wdma2")
        if t == 0:
            P.wm("tensor", f"rem_ev_{g}_3")
        else:
            P.wm("tensor", f"res1_{tq - 1}")
        for half in range(2):
            P.i("tensor", lambda E, half=half: E.matmul(
                self.ps_b[:, half * 512:(half + 1) * 512], self.ones1[0:1, :],
                self.outb_row[0:1, half * 512:(half + 1) * 512],
                start=True, stop=False), self.s_te)
            for dc in range(8):
                P.i("tensor", lambda E, dc=dc, half=half: E.matmul(
                    self.ps_b[:, half * 512:(half + 1) * 512],
                    self.mixt_sb[:, dc * 128:(dc + 1) * 128],
                    self.out_wb[:, dc, half * 512:(half + 1) * 512],
                    start=False, stop=(dc == 7)), self.s_te)
        P.mark(f"outw_{tq}", self.s_te)

        if tq >= 1:
            P.wm("sync", f"res1_{tq - 1}")
        P.i("sync", lambda E, tq=tq: E.dma_start(
            out=self.xr_sb[:, 0, :], in_=self.d_x[tq * 128:(tq + 1) * 128, :]),
            self.s_xr, 16)
        P.w("vector", self.s_xr, P.cur(self.s_xr))
        P.wm("vector", f"outw_{tq}")
        if tq >= 1:
            P.wm("vector", f"hdma_{tq - 1}")
        P.selfwait("vector", self.s_ve)
        P.i("vector", lambda E, tq=tq: E.tensor_tensor(
            out=self.h_sb[:, 0, :], in0=self.ps_b[:, :],
            in1=self.xr_sb[:, 0, :], op=ALU.add), self.s_ve)
        P.mark(f"res1_{tq}", self.s_ve)
        P.selfwait("vector", self.s_ve)
        P.w("scalar", self.s_ve, P.cur(self.s_ve))
        P.i("scalar", lambda E, tq=tq: E.dma_start(
            out=self.d_h_scratch[tq * 128:(tq + 1) * 128, :],
            in_=self.h_sb[:, 0, :]), self.s_hw, 16)
        P.mark(f"hdma_{tq}", self.s_hw)
        P.i("vector", lambda E: E.bn_stats(
            out=self.bn_buf[:, 0:6], in_=self.h_sb[:, 0, 0:512]), self.s_ve)
        P.i("vector", lambda E: E.bn_stats(
            out=self.bn_buf[:, 6:12], in_=self.h_sb[:, 0, 512:1024]), self.s_ve)
        P.selfwait("vector", self.s_ve)
        P.i("vector", lambda E: E.bn_aggr(out=self.mv_tmp[:, :],
                                          in_=self.bn_buf[:, :]), self.s_ve)
        P.selfwait("vector", self.s_ve)
        P.w("scalar", self.s_ve, P.cur(self.s_ve))
        P.i("scalar", lambda E, tq=tq: E.activation(
            out=self.mr2[:, tq, 0:1], in_=self.mv_tmp[:, 1:2], func=AF.Ln,
            bias=EPS), self.s_sc)
        P.selfwait("scalar", self.s_sc)
        P.i("scalar", lambda E, tq=tq: E.activation(
            out=self.mr2[:, tq, 0:1], in_=self.mr2[:, tq, 0:1], func=AF.Exp,
            scale=-0.5), self.s_sc)
        P.w("vector", self.s_sc, P.cur(self.s_sc))
        P.i("vector", lambda E, tq=tq: E.tensor_scalar(
            out=self.mr2[:, tq, 1:2], in0=self.mv_tmp[:, 0:1],
            scalar1=self.mr2[:, tq, 0:1], scalar2=-1.0,
            op0=ALU.mult, op1=ALU.mult), self.s_ve)
        P.selfwait("vector", self.s_ve)
        P.i("vector", lambda E, tq=tq: E.tensor_scalar(
            out=self.ps_c[:, :], in0=self.h_sb[:, 0, :],
            scalar1=self.mr2[:, tq, 0:1], scalar2=self.mr2[:, tq, 1:2],
            op0=ALU.mult, op1=ALU.add), self.s_ve)
        P.selfwait("vector", self.s_ve)
        P.i("vector", lambda E: E.reduce_max(
            out=self.rowmax_sb[:, 0:1], in_=self.ps_c[:, :], axis=AX.X,
            apply_absolute_value=True), self.s_ve)
        P.selfwait("vector", self.s_ve)
        P.i("vector", lambda E: E.tensor_tensor(
            out=self.amax_acc[:, :], in0=self.amax_acc[:, :],
            in1=self.rowmax_sb[:, 0:1], op=ALU.max), self.s_ve)
        P.mark(f"amax_{tq}", self.s_ve)
        if self.cfg.dbg:
            P.selfwait("vector", self.s_ve)
            P.w("scalar", self.s_ve, P.cur(self.s_ve))
            P.i("scalar", lambda E, tq=tq: E.dma_start(
                out=self.dbg["h"][tq * 128:(tq + 1) * 128, :],
                in_=self.h_sb[:, 0, :]), self.s_vd, 16)

    def barrier(self):
        P = self.P
        NT = self.cfg.ntiles
        P.wm("vector", f"amax_{NT - 1}")
        P.selfwait("vector", self.s_ve)
        P.w("scalar", self.s_ve, P.cur(self.s_ve))
        P.i("scalar", lambda E: E.dma_start(out=self.amax_row[0:1, :],
                                            in_=self.amax_acc[:, :]), self.s_stg, 16)
        P.w("vector", self.s_stg, P.cur(self.s_stg))
        P.i("vector", lambda E: E.reduce_max(
            out=self.scal[0:1, 0:1], in_=self.amax_row[0:1, :], axis=AX.X), self.s_ve)
        P.selfwait("vector", self.s_ve)
        P.i("vector", lambda E: E.memset(self.amax_row[0:1, :], 0.0), self.s_ve)
        P.selfwait("vector", self.s_ve)
        P.i("vector", lambda E: E.tensor_copy(out=self.amax_row[0:1, 0:1],
                                              in_=self.scal[0:1, 0:1]), self.s_ve)
        P.selfwait("vector", self.s_ve)
        P.w("scalar", self.s_ve, P.cur(self.s_ve))
        P.i("scalar", lambda E: E.dma_start(out=self.d_cca_in,
                                            in_=self.amax_row[0:1, :]), self.s_stg, 16)
        P.w("gpsimd", self.s_stg, P.cur(self.s_stg))
        P.i("gpsimd", lambda E: E.collective_compute(
            "AllReduce", ALU.max, ins=[self.d_cca_in], outs=[self.d_cca_out],
            replica_groups=RG), self.s_gp)
        P.selfwait("gpsimd", self.s_gp)
        P.i("gpsimd", lambda E: E.dma_start(out=self.scal[0:1, 0:1],
                                            in_=self.d_cca_out[0:1, 0:1]), self.s_gd2, 16)
        P.mark("amax_g", self.s_gd2)
        P.wm("vector", "amax_g")
        P.i("vector", lambda E: E.tensor_scalar_max(
            out=self.scal[0:1, 0:1], in0=self.scal[0:1, 0:1], scalar1=1e-12), self.s_ve)
        P.selfwait("vector", self.s_ve)
        P.i("vector", lambda E: E.reciprocal(out=self.scal[0:1, 5:6],
                                             in_=self.scal[0:1, 0:1]), self.s_ve)
        P.selfwait("vector", self.s_ve)
        P.i("vector", lambda E: E.tensor_scalar(
            out=self.scal[0:1, 5:6], in0=self.scal[0:1, 5:6], scalar1=FP8MAX,
            scalar2=FP8MAX, op0=ALU.mult, op1=ALU.min), self.s_ve)
        P.selfwait("vector", self.s_ve)
        P.i("vector", lambda E: E.tensor_scalar_mul(
            out=self.scal[0:1, 6:7], in0=self.scal[0:1, 5:6], scalar1=0.5), self.s_ve)
        P.i("vector", lambda E: E.tensor_tensor(
            out=self.scal[0:1, 7:8], in0=self.scal[0:1, 5:6], in1=self.scal[0:1, 3:4],
            op=ALU.mult), self.s_ve)
        P.selfwait("vector", self.s_ve)
        P.i("vector", lambda E: E.reciprocal(out=self.scal[0:1, 7:8],
                                             in_=self.scal[0:1, 7:8]), self.s_ve)
        P.selfwait("vector", self.s_ve)
        P.i("vector", lambda E: E.tensor_scalar_mul(
            out=self.scal[0:1, 7:8], in0=self.scal[0:1, 7:8], scalar1=4.0), self.s_ve)
        P.selfwait("vector", self.s_ve)
        self.bcast_col(lambda: self.scal[0:1, 6:7], lambda: self.sxh_col[:, :], "sxh2")
        self.bcast_col(lambda: self.scal[0:1, 7:8], lambda: self.sout_col[:, :], "sout2")
        P.i("vector", lambda E: E.tensor_tensor(
            out=self.scal[0:1, 1:2], in0=self.scal[0:1, 5:6], in1=self.scal[0:1, 3:4],
            op=ALU.mult), self.s_ve)
        P.selfwait("vector", self.s_ve)
        P.i("vector", lambda E: E.tensor_scalar_mul(
            out=self.scal[0:1, 1:2], in0=self.scal[0:1, 1:2], scalar1=0.25), self.s_ve)
        P.selfwait("vector", self.s_ve)
        P.i("vector", lambda E: E.tensor_scalar_mul(
            out=self.fbiaspre_row[0:1, :], in0=self.fbias_row[0:1, :],
            scalar1=self.scal[0:1, 1:2]), self.s_ve)
        P.mark("scales", self.s_ve)
        if self.cfg.dbg:
            P.selfwait("vector", self.s_ve)
            P.w("scalar", self.s_ve, P.cur(self.s_ve))
            P.i("scalar", lambda E: E.dma_start(out=self.dbg["scal"],
                                                in_=self.scal[0:1, :]), self.s_vd, 16)

    def p2_early_pre(self, u):
        P, c = self.P, self.cfg
        NT = c.ntiles
        s5, s2 = u % 5, u % 2
        if u >= 5:
            P.wm("sync", f"p2_res_{u - 5}")
        else:
            P.wm("sync", f"av_done_{NT - 1}")
            P.wm("sync", f"lu_ev_{NT - 1}_all")
        P.wm("sync", "hw_all")
        P.i("sync", lambda E, u=u, s5=s5: E.dma_start(
            out=self.h2_sb[:, s5, :],
            in_=self.d_h_scratch[u * 128:(u + 1) * 128, :]),
            self.s_h2[u % 5], 16)
        P.wm("vector", "scales")
        P.wm("vector", "sxh2")
        if u >= 5:
            P.wm("vector", f"p2_x8_{u - 5}")
        P.selfwait("vector", self.s_ve)
        P.i("vector", lambda E, u=u, s5=s5: E.tensor_scalar_mul(
            out=self.sc_tile[:, s5, 0:1], in0=self.mr2[:, u, 0:1],
            scalar1=self.sxh_col[:, 0:1]), self.s_ve)
        P.i("vector", lambda E, u=u, s5=s5: E.tensor_scalar_mul(
            out=self.sc_tile[:, s5, 1:2], in0=self.mr2[:, u, 1:2],
            scalar1=self.sxh_col[:, 0:1]), self.s_ve)
        P.mark(f"p2_sc_{u}", self.s_ve)
        P.w("scalar", self.s_ve, P.cur(self.s_ve))
        P.w("scalar", self.s_h2[u % 5], P.cur(self.s_h2[u % 5]))
        if u >= 2:
            P.wm("scalar", f"p2_up_{u - 2}")
        else:
            P.wm("scalar", f"mixed_{NT - 1}")
        P.i("scalar", lambda E, s5=s5, s2=s2: E.activation(
            out=self.x8_sb[:, s2, :], in_=self.h2_sb[:, s5, :], func=AF.Identity,
            scale=self.sc_tile[:, s5, 0:1], bias=self.sc_tile[:, s5, 1:2]), self.s_sc)
        P.mark(f"p2_x8_{u}", self.s_sc)
        P.w("vector", self.s_sc, P.cur(self.s_sc))
        if u >= 2:
            P.wm("vector", f"p2_tr_{u - 2}")
        P.i("vector", lambda E, s2=s2: E.tensor_copy(
            out=self.x8b_sb[:, s2, :], in_=self.x8_sb[:, s2, :]), self.s_ve)
        P.mark(f"p2_up_{u}", self.s_ve)

    def p2_early_pe(self, u):
        P, c = self.P, self.cfg
        NT = c.ntiles
        s2, s4 = u % 2, u % 4
        P.wm("tensor", f"p2_up_{u}")
        if u >= 1:
            P.wm("tensor", f"p2_fm_{u - 1}")
        else:
            P.wm("tensor", f"mixT_ev_{NT - 1}")
        for j in range(8):
            P.i("tensor", lambda E, j=j, s2=s2: E.transpose(
                self.ps_tr[:, j * 128:(j + 1) * 128],
                self.x8b_sb[:, s2, j * 128:(j + 1) * 128], self.ident[:, :]),
                self.s_te)
        P.mark(f"p2_tr_{u}", self.s_te)
        P.w("vector", self.s_te, P.cur(self.s_te))
        if u == 0:
            P.wm("vector", f"av_done_{NT - 1}")
            P.wm("vector", f"b_ln_{NT}")
        if u >= 4:
            P.wm("vector", f"p2_mm_{u - 4}")
        P.i("vector", lambda E, s4=s4: E.tensor_copy(
            out=self.x8fm_sb[:, s4, :, :], in_=self.ps_tr[:, :]), self.s_ve)
        P.mark(f"p2_fm_{u}", self.s_ve)

    def p2_late(self, t):
        P, c = self.P, self.cfg
        NT = c.ntiles
        s4, s5, s2 = t % 4, t % 5, t % 2
        psf = self.ps_b if t % 2 == 0 else self.ps_c
        P.wm("tensor", f"p2_fm_{t}")
        if t == 0:
            P.wm("tensor", "w8t")
            P.wm("tensor", "sout2")
            P.wm("tensor", "scales")
            P.wm("tensor", f"res1_{NT - 1}")
        elif t == 1:
            P.wm("tensor", f"amax_{NT - 1}")
        if t >= 2:
            P.wm("tensor", f"p2_ffn_{t - 2}")
        for half in range(2):
            P.i("tensor", lambda E, half=half, psf=psf: E.matmul(
                psf[:, half * 512:(half + 1) * 512], self.ones1[0:1, :],
                self.fbiaspre_row[0:1, half * 512:(half + 1) * 512],
                start=True, stop=False), self.s_te)
            for dc in range(8):
                P.i("tensor", lambda E, dc=dc, half=half, psf=psf, s4=s4: E.matmul(
                    psf[:, half * 512:(half + 1) * 512],
                    self.x8fm_sb[:, s4, dc, :],
                    self.w8t[:, dc, half * 512:(half + 1) * 512],
                    start=False, stop=(dc == 7)), self.s_te)
        P.mark(f"p2_mm_{t}", self.s_te)
        P.w("scalar", self.s_te, P.cur(self.s_te))
        if t < 2:
            P.wm("scalar", "sout2")
            P.wm("scalar", f"mixed_{NT - 1}")
        else:
            P.wm("scalar", f"p2_res_{t - 2}")
        P.i("scalar", lambda E, s2=s2, psf=psf: E.activation(
            out=self.ffn_sb[:, s2, :], in_=psf[:, :], func=AF.Copy,
            scale=self.sout_col[:, 0:1]), self.s_sc)
        P.mark(f"p2_ffn_{t}", self.s_sc)
        P.w("vector", self.s_sc, P.cur(self.s_sc))
        if t >= 2:
            P.wm("vector", f"odma_{t - 2}")
            P.wm("vector", f"odmav_{t - 2}")
        P.selfwait("vector", self.s_ve)
        P.i("vector", lambda E, s5=s5, s2=s2: E.tensor_tensor(
            out=self.outt_sb[:, s2, :], in0=self.h2_sb[:, s5, :],
            in1=self.ffn_sb[:, s2, :], op=ALU.add), self.s_ve)
        P.mark(f"p2_res_{t}", self.s_ve)
        P.selfwait("vector", self.s_ve)
        P.w("scalar", self.s_ve, P.cur(self.s_ve))
        P.i("scalar", lambda E, t=t, s2=s2: E.dma_start(
            out=self.d_out[t * 128:(t + 1) * 128, 0:512],
            in_=self.outt_sb[:, s2, 0:512]), self.s_out[s2], 16)
        P.mark(f"odma_{t}", self.s_out[s2])
        P.w("gpsimd", self.s_ve, P.cur(self.s_ve))
        P.w("gpsimd", self.s_vd, P.cur(self.s_vd))
        P.i("gpsimd", lambda E, t=t, s2=s2: E.dma_start(
            out=self.d_out[t * 128:(t + 1) * 128, 512:1024],
            in_=self.outt_sb[:, s2, 512:1024]), self.s_vd, 16)
        P.mark(f"odmav_{t}", self.s_vd)

    def build(self):
        c = self.cfg
        P = self.P
        NT = c.ntiles
        self.declare()
        self.prologue_early()
        self.p1a()
        self.wamax_finish()
        self.ag_start()
        self.prologue_late()
        P.i("vector", lambda E: E.memset(self.amax_acc[:, :], 0.0), self.s_ve)
        self.chain_tile(0, NT, pstr_wait="bias_cols")
        return self.main_loop()

    def kh_block(self):
        P, c = self.P, self.cfg
        P.wm("tensor", f"b_fm_0")
        P.wm("tensor", "v_4")
        for j in range(8):
            if j >= 2:
                P.wm("tensor", f"kh_{j - 2}")
            for dc in range(8):
                P.i("tensor", lambda E, j=j, dc=dc: E.matmul(
                    self.ps_a[:, (j % 2) * 512:(j % 2) * 512 + 128],
                    self.qkv_wb[:, dc, (8 + j) * 128:(9 + j) * 128],
                    self.lnfmh_sb[:, dc, :], start=(dc == 0), stop=(dc == 7)), self.s_te)
            P.w("scalar", self.s_te, P.cur(self.s_te))
            P.i("scalar", lambda E, j=j: E.activation(
                out=self.kh_sb[:, j, :], in_=self.ps_a[:, (j % 2) * 512:(j % 2) * 512 + 128],
                func=AF.Identity, bias=self.qkb_col[:, 8 + j:9 + j]), self.s_sc)
            P.mark(f"kh_{j}", self.s_sc)
        P.mark("kh", self.s_sc)
        if self.cfg.dbg:
            P.selfwait("scalar", self.s_sc)
            for j in range(8):
                P.i("scalar", lambda E, j=j: E.dma_start(
                    out=self.dbg["k"][j * 128:(j + 1) * 128, c.s_core:c.s_core + 128],
                    in_=self.kh_sb[:, j, :]), self.s_sd, 16)

    def main_loop(self):
        c = self.cfg
        P = self.P
        NT = c.ntiles
        for g in range(c.ngroups):
            if g == 0:
                self.chain_tile(1, 0)
                self.chain_tile(2, 1)
                self.ag_finish()
                self.chain_tile(3, 2, pstr_wait="pooled_fm",
                                pss_wait="rv_ps_done")
                self.chain_tile(4, 3)
                self.kh_block()
                for cc in range(8):
                    self.w8t_chunk(cc, pstr_wait="b_fm_4")
            else:
                for t in range(4):
                    self.chain_tile(4 * g + t + 1, 4 * g + t)
            gates = [] if g == 0 else [4 * (g - 1) + t for t in range(4)]
            self.qk_group(g, gates)
            self.rq_group(g)
            for t in range(4):
                self.local_qtile(4 * g + t)
            self.latent_group(g)
            P.selfwait("vector", self.s_ve)
            P.i("vector", lambda E: E.tensor_copy(
                out=self.kprev_sb[:, :, :],
                in_=AP(self.k_sb, 384, [[8 * 512, 128], [512, 8], [1, 128]])),
                self.s_ve)
            P.mark(f"kprev_{g}", self.s_ve)
        for t in range(4):
            self.gate_outw_res(4 * (c.ngroups - 1) + t)
        P.mark("hw_all", self.s_hw)
        self.barrier()
        self.p2_early_pre(0)
        self.p2_early_pre(1)
        for u in range(4):
            self.p2_early_pe(u)
            if u + 2 < 5:
                self.p2_early_pre(u + 2)
        for t in range(NT):
            self.p2_late(t)
            if t + 5 < NT:
                self.p2_early_pre(t + 5)
            if t + 4 < NT:
                self.p2_early_pe(t + 4)
        for s in [self.s_vd, self.s_sy, self.s_gd, self.s_gd2, self.s_gd3,
                  self.s_gd4, self.s_sd,
                  self.s_ve, self.s_sc, self.s_te, self.s_gp, self.s_xa0,
                  self.s_xa1, self.s_xl0, self.s_xl1, self.s_xr, self.s_h20,
                  self.s_h21, self.s_hw, self.s_out0, self.s_out1, self.s_stg]:
            P.selfwait("gpsimd", s)
        with self.nc.Block() as block:
            P.emit(block)
        self.ctx.pop_all()
        return self.nc


def build(cfg: Cfg):
    return MK(cfg).build()


S_CORE = 4096
S_SEQ = 16384
B = 2


def _core_inputs(xb, wd, core):
    b, c = core // 4, core % 4
    xs = np.ascontiguousarray(xb[b, c * S_CORE:(c + 1) * S_CORE])
    if c == 0:
        xh = np.zeros((128, D), np.float32)
        hv = 0.0
    else:
        xh = np.ascontiguousarray(xb[b, c * S_CORE - 128:c * S_CORE])
        hv = 1.0
    lm = (np.arange(NLAT)[:, None] < (16 * c + np.arange(16))[None, :])
    m = {"x": xs, "xh": xh, "latmask": lm.astype(np.float32),
         "halo_valid": np.array([[hv]], np.float32)}
    m.update(wd)
    return m


def kernel(**inputs):
    f32 = np.float32
    xb = np.ascontiguousarray(np.asarray(inputs["L0_attn_x"], f32))
    wd = {
        "qkv_w": np.ascontiguousarray(np.asarray(inputs["L0_attn_qkv_w"], f32)),
        "qkv_b": np.ascontiguousarray(np.asarray(inputs["L0_attn_qkv_b"], f32)),
        "rq_w": np.ascontiguousarray(np.asarray(inputs["L0_attn_rq_w"], f32)),
        "rq_b": np.ascontiguousarray(np.asarray(inputs["L0_attn_rq_b"], f32)),
        "rkv_w": np.ascontiguousarray(np.asarray(inputs["L0_attn_rkv_w"], f32)),
        "rkv_b": np.ascontiguousarray(np.asarray(inputs["L0_attn_rkv_b"], f32)),
        "rout_w": np.ascontiguousarray(np.asarray(inputs["L0_attn_rout_w"], f32)),
        "rout_b": np.ascontiguousarray(np.asarray(inputs["L0_attn_rout_b"], f32)),
        "out_w": np.ascontiguousarray(np.asarray(inputs["L0_attn_out_w"], f32)),
        "out_b": np.ascontiguousarray(np.asarray(inputs["L0_attn_out_b"], f32)),
        "gl_w": np.ascontiguousarray(np.asarray(inputs["L0_attn_gl_w"], f32)),
        "gr_w": np.ascontiguousarray(np.asarray(inputs["L0_attn_gr_w"], f32)),
        "ffn_w": np.ascontiguousarray(np.asarray(inputs["L0_ffn_weight"], f32)),
        "ffn_b": np.ascontiguousarray(np.asarray(inputs["L0_ffn_bias"], f32)),
    }
    nc = build(Cfg(s_core=S_CORE, dbg=False))
    in_maps = [_core_inputs(xb, wd, core) for core in range(8)]
    from concourse.bass_utils import run_bass_kernel_spmd
    res = run_bass_kernel_spmd(nc, in_maps, core_ids=list(range(8)))
    out = np.empty((B, S_SEQ, D), f32)
    for core in range(8):
        b, c = core // 4, core % 4
        out[b, c * S_CORE:(c + 1) * S_CORE] = res.results[core]["out"]
    return out
